# version 06
# baseline (speedup 1.0000x reference)
import numpy as np
import ml_dtypes
import concourse.bass as bass
import concourse.mybir as mybir
from concourse.bass_utils import run_bass_kernel_spmd

F32 = mybir.dt.float32
BF16 = mybir.dt.bfloat16
AF = mybir.ActivationFunctionType
ALU = mybir.AluOpType
AX = mybir.AxisListType

D = 1024
T = 2048
NT = 16
NDC = 8
NCORE = 8

S_MEAN, S_NEGM, S_VAR, S_RSTD = 0, 16, 32, 48
S_CW = 64
S_SEG = 192
S_TMP = 208
S_AT = 256


class Prog:
    def __init__(self, nc):
        self.nc = nc
        self.qs = {k: [] for k in "tvsgy"}
        keys = ["t", "v", "s", "g", "y", "gd", "yd", "sd"]
        self.sems = {k: nc.alloc_semaphore("sem_" + k) for k in keys}
        self.cnt = {k: 0 for k in keys}
        self.waited = {e: {} for e in "tvsgy"}

    def _push(self, eng, fn, wait, inc):
        wl = []
        seen = self.waited[eng]
        for w in wait:
            if w is None:
                continue
            k, v = w
            if seen.get(k, 0) >= v:
                continue
            seen[k] = v
            wl.append((k, v))
        self.qs[eng].append((fn, wl, inc))

    def op(self, eng, fn, wait=(), tag=True):
        ev = None
        inc = None
        if tag:
            self.cnt[eng] += 1
            inc = (eng, 1)
            ev = (eng, self.cnt[eng])
        self._push(eng, fn, wait, inc)
        return ev

    def dma(self, eng, fn, wait=()):
        k = eng + "d"
        self.cnt[k] += 16
        self._push(eng, fn, wait, (k, 16))
        return (k, self.cnt[k])

    def finalize(self):
        nc = self.nc
        sems = self.sems
        for e in "tvsgy":
            for _, wl, _ in self.qs[e]:
                for k, v in wl:
                    assert v <= self.cnt[k], ("wait beyond plan", e, k, v, self.cnt[k])

        def runner(q):
            def f(eng):
                for fn, wl, inc in q:
                    for k, v in wl:
                        eng.wait_ge(sems[k], v)
                    inst = fn(eng)
                    if inc is not None:
                        inst.then_inc(sems[inc[0]], inc[1])
            return f

        with nc.Block() as blk:
            blk.tensor(runner(self.qs["t"]))
            blk.vector(runner(self.qs["v"]))
            blk.scalar(runner(self.qs["s"]))
            blk.gpsimd(runner(self.qs["g"]))
            blk.sync(runner(self.qs["y"]))


class Bank:
    __slots__ = ("ap", "wr", "rd")

    def __init__(self, ap):
        self.ap = ap
        self.wr = None
        self.rd = None


def bank_bf(bank, slot):
    return bank.ap[:, slot * 64:(slot + 1) * 64].bitcast(BF16)


class Ctx:
    pass


def alloc_ctx(nc, p):
    c = Ctx()
    c.nc = nc
    c.p = p
    sb = lambda n, s, d: nc.alloc_sbuf_tensor(n, s, d).ap()
    c.x = sb("x_res", [128, NT, D], F32)
    c.xt = sb("xt_fm", [128, 8 * T], BF16)
    c.pool = sb("pool", [128, 16384], BF16)
    c.pool2 = sb("pool2", [128, 24576], BF16)
    c.scrf = sb("scrf", [128, 3072], F32)
    c.st = sb("stats", [128, 512], F32)
    c.cstf = sb("cstf", [128, 384], F32)
    c.cstb = sb("cstb", [128, 768], BF16)
    c.browq = sb("browq", [1, 1024], BF16)
    c.browk = sb("browk", [1, 1024], BF16)
    c.browv = sb("browv", [1, 1024], BF16)
    c.browo = sb("browo", [1, 1024], BF16)
    c.wr8 = sb("wr8", [128, 80], BF16)
    c.rt8 = sb("rt8", [9, 1024], BF16)
    c.curv = sb("curv", [1, 4], F32)
    c.ccol = sb("ccol", [128, 4], F32)
    c.pa = nc.alloc_psum_tensor("pa", [128, 2048], F32).ap()
    c.pb = nc.alloc_psum_tensor("pb", [128, 2048], F32).ap()
    c.banks = [Bank(c.pa[:, i * 512:(i + 1) * 512]) for i in range(4)] + \
              [Bank(c.pb[:, i * 512:(i + 1) * 512]) for i in range(4)]
    c.stg = sb("stg", [128, 2048], BF16)
    c.stg_free = [None, None]
    c.stg_par = 0
    c.s2ev = {}
    c.dram = {}
    c.wbf = {}
    c.ev_prepass = {}
    c.last_x_ev = None
    c.xt_ev = None
    c.fence_ev = None
    return c


def emit_consts(c):
    p = c.p
    def mk_ident(ap):
        e1 = p.op("g", lambda g, a=ap: g.memset(a, 0.0))
        return p.op("g", lambda g, a=ap: g.affine_select(
            out=a, in_=a, compare_op=ALU.not_equal, fill=1.0, base=0,
            pattern=[[-1, 128]], channel_multiplier=1), wait=[e1])
    mk_ident(c.cstf[:, 0:128])
    e1 = p.op("g", lambda g: g.memset(c.cstf[:, 128:256], 0.0))
    p.op("g", lambda g: g.affine_select(
        out=c.cstf[:, 128:256], in_=c.cstf[:, 128:256], compare_op=ALU.is_ge,
        fill=-1e9, base=0, pattern=[[-1, 128]], channel_multiplier=1), wait=[e1])
    e2 = p.op("g", lambda g: g.memset(c.cstf[:, 256:384], 0.0))
    p.op("g", lambda g: g.affine_select(
        out=c.cstf[:, 256:384], in_=c.cstf[:, 256:384], compare_op=ALU.is_ge,
        fill=-1e9, base=0, pattern=[[1, 128]], channel_multiplier=-1), wait=[e2])
    mk_ident(c.cstb[:, 0:128])
    ev = p.op("g", lambda g: g.memset(c.cstb[:, 128:640], 1.0))
    c.ev_consts = ev
    c.identf = c.cstf[:, 0:128]
    c.identb = c.cstb[:, 0:128]
    c.maskQ = c.cstf[:, 128:256]
    c.maskT = c.cstf[:, 256:384]
    c.ones = c.cstb[:, 128:640]


def emit_cast(c, full_names):
    p = c.p
    for n in full_names:
        if n in c.wbf:
            continue
        src = c.dram[n]
        shp = list(src.shape)
        if len(shp) == 1:
            shp = [1, shp[0]]
            src = src.rearrange("(a d) -> a d", a=1)
        dst = c.nc.dram_tensor("bf_" + n, shp, BF16).ap()
        if len(shp) == 3:
            evs = []
            for e in range(shp[0]):
                evs.append(p.dma("g", lambda g, a=dst[e], b=src[e]: g.dma_start(a, b)))
                c.ev_prepass[(n, e)] = evs[-1]
            c.ev_prepass[n] = evs[-1]
        else:
            c.ev_prepass[n] = p.dma("g", lambda g, a=dst, b=src: g.dma_start(a, b))
        c.wbf[n] = dst


def emit_rearrange_cast(c, key, src, dst, ev=None):
    c.s2ev[key] = c.p.dma("g", lambda g, a=dst, b=src: g.dma_start(a, b),
                          wait=[ev])


def emit_pre_transpose(c, key, pieces):
    p = c.p
    ev = None
    for src, dst, wev in pieces:
        b = c.stg_par % 2
        R = src.shape[0]
        stga = c.stg[:, b * 1024: b * 1024 + R]
        e1 = p.dma("y", lambda s, a=stga, b_=src: s.dma_start_transpose(a, b_),
                   wait=[wev, c.stg_free[b]])
        e2 = p.dma("y", lambda s, a=dst, b_=stga: s.dma_start(a, b_), wait=[e1])
        c.stg_free[b] = e2
        c.stg_par += 1
        ev = e2
    c.s2ev[key] = ev


def emit_prepass_v2(c, subs):
    nc = c.nc
    dram = lambda n, s: nc.dram_tensor(n, s, BF16).ap()
    done = set(subs)
    if "f0a" in done or "f0b" in done:
        px = "L0_ffn_"
        emit_cast(c, [px + n for n in ("router_weight", "shared_expert_gate_w",
                                       "gate_up_proj", "down_proj",
                                       "shared_gate_w", "shared_up_w",
                                       "shared_down_w")])
        c.s2gu0 = dram("s2gu0", [8, 128, 8, 1024])
        c.s2dn0 = dram("s2dn0", [8, 128, 4, 1024])
        c.s2sg0 = dram("s2sg0", [2, 4, 128, 8, 512])
        c.s2sd0 = dram("s2sd0", [128, 16, 1024])
        gu_bf = c.wbf[px + "gate_up_proj"]
        dn_bf = c.wbf[px + "down_proj"]
        for e in range(8):
            emit_pre_transpose(c, ("gu0", e), [
                (gu_bf[e][:, dc * 128:(dc + 1) * 128], c.s2gu0[e][:, dc, :],
                 c.ev_prepass[(px + "gate_up_proj", e)]) for dc in range(8)])
            emit_pre_transpose(c, ("dn0", e), [
                (dn_bf[e][:, ic * 128:(ic + 1) * 128], c.s2dn0[e][:, ic, :],
                 c.ev_prepass[(px + "down_proj", e)]) for ic in range(4)])
        for part, nm in ((0, "shared_gate_w"), (1, "shared_up_w")):
            for grp in range(4):
                emit_pre_transpose(c, ("sg0", part, grp), [
                    (c.wbf[px + nm][grp * 512:(grp + 1) * 512,
                                    dc * 128:(dc + 1) * 128],
                     c.s2sg0[part, grp][:, dc, :], c.ev_prepass[px + nm])
                    for dc in range(8)])
        emit_pre_transpose(c, "sd0", [
            (c.wbf[px + "shared_down_w"][:, ic * 128:(ic + 1) * 128],
             c.s2sd0[:, ic, :], c.ev_prepass[px + "shared_down_w"])
            for ic in range(16)])
    if "a0" in done:
        px = "L0_attn_"
        emit_cast(c, [px + n for n in ("W_q", "W_k", "W_v", "W_o")])
        c.s2l0q = dram("s2l0q", [128, 8, 1024])
        c.s2l0k = dram("s2l0k", [128, 8, 256])
        c.s2l0v = dram("s2l0v", [128, 8, 256])
        c.s2l0o = dram("s2l0o", [128, 8, 1024])
        for key, nm, scr in (("l0k", "W_k", c.s2l0k), ("l0v", "W_v", c.s2l0v),
                             ("l0q", "W_q", c.s2l0q), ("l0o", "W_o", c.s2l0o)):
            emit_pre_transpose(c, key, [
                (c.wbf[px + nm][:, dc * 128:(dc + 1) * 128], scr[:, dc, :],
                 c.ev_prepass[px + nm]) for dc in range(8)])
    if "f1a" in done or "f1b" in done:
        px = "L1_ffn_"
        emit_cast(c, [px + "w_router"])
        c.s2gug = dram("s2gug", [8, 16, 128, 2, 8, 256])
        c.s2wd = dram("s2wd", [8, 4, 128, 8, 1024])
        for ei in range(8):
            if ei < 7:
                usrc = c.dram[px + "w_up"][ei]
                gsrc = c.dram[px + "w_gate"][ei]
                dsrc = c.dram[px + "w_down"][ei]
            else:
                usrc = c.dram[px + "ws_up"]
                gsrc = c.dram[px + "ws_gate"]
                dsrc = c.dram[px + "ws_down"]
            ev = None
            for hp in range(16):
                emit_rearrange_cast(c, ("gug", ei), usrc[:, hp * 256:(hp + 1) * 256]
                                    .rearrange("(cc p) h -> p cc h", p=128),
                                    c.s2gug[ei, hp][:, 0])
                emit_rearrange_cast(c, ("gug", ei), gsrc[:, hp * 256:(hp + 1) * 256]
                                    .rearrange("(cc p) h -> p cc h", p=128),
                                    c.s2gug[ei, hp][:, 1])
            for q in range(4):
                emit_rearrange_cast(c, ("wd", ei), dsrc[q * 1024:(q + 1) * 1024, :]
                                    .rearrange("(hh p) o -> p hh o", p=128),
                                    c.s2wd[ei, q])
    if "a1" in done:
        px = "L1_attn_"
        emit_cast(c, [px + n for n in ("wq_w", "wk_w", "wv_w", "wo_w", "wq_b",
                                       "wk_b", "wv_b", "wo_b")])
        c.s2l1q = dram("s2l1q", [8, 128, 8, 128])
        c.s2l1k = dram("s2l1k", [8, 128, 8, 128])
        c.s2l1v = dram("s2l1v", [8, 128, 8, 128])
        c.s2l1o = dram("s2l1o", [128, 8, 1024])
        for key, nm, scr in (("l1q", "wq_w", c.s2l1q), ("l1k", "wk_w", c.s2l1k),
                             ("l1v", "wv_w", c.s2l1v)):
            for cc in range(8):
                emit_pre_transpose(c, (key, cc), [
                    (c.wbf[px + nm][cc * 128:(cc + 1) * 128,
                                    dc * 128:(dc + 1) * 128],
                     scr[cc][:, dc, :], c.ev_prepass[px + nm]) for dc in range(8)])
        emit_pre_transpose(c, "l1o", [
            (c.wbf[px + "wo_w"][:, dc * 128:(dc + 1) * 128], c.s2l1o[:, dc, :],
             c.ev_prepass[px + "wo_w"]) for dc in range(8)])



def emit_tok_to_feat(c, j, src_bf, bank, wait=()):
    p = c.p
    ev = None
    for half in range(2):
        tw = [bank.rd] + list(wait)
        for i in range(4):
            dc = half * 4 + i
            tev = p.op("t", lambda t, d=bank_bf(bank, i),
                       s=src_bf[:, dc * 128:(dc + 1) * 128]:
                       t.transpose(d, s, c.identb), wait=tw, tag=(i == 3))
        bank.wr = tev
        for i in range(4):
            dc = half * 4 + i
            eev = p.op("s", lambda s, d=c.xt[:, dc * T + j * 128: dc * T + (j + 1) * 128],
                       sp=bank_bf(bank, i): s.copy(d, sp),
                       wait=[bank.wr], tag=(i == 3))
        bank.rd = eev
        ev = eev
    return ev


def emit_ln_all(c, per_tile=None):
    p = c.p
    st = c.st
    mean_all = st[:, S_MEAN:S_MEAN + 16]
    negm_all = st[:, S_NEGM:S_NEGM + 16]
    var_all = st[:, S_VAR:S_VAR + 16]
    rstd_all = st[:, S_RSTD:S_RSTD + 16]
    scr = c.scrf[:, 0:1024]
    red = None
    for j in range(NT):
        red = p.op("v", lambda v, jj=j: v.reduce_sum(
            mean_all[:, jj:jj + 1], c.x[:, jj, :], AX.X), wait=[c.last_x_ev])
    em = p.op("v", lambda v: v.tensor_scalar_mul(mean_all, mean_all, 1.0 / 1024.0),
              wait=[red])
    en = p.op("s", lambda s: s.mul(negm_all, mean_all, -1.0), wait=[em])
    sq = None
    for j in range(NT):
        sq = p.op("s", lambda s, jj=j: s.activation(
            scr, c.x[:, jj, :], AF.Square, bias=negm_all[:, jj:jj + 1], scale=1.0,
            accum_out=var_all[:, jj:jj + 1]), wait=[en])
    ev1 = p.op("v", lambda v: v.tensor_scalar(out=rstd_all, in0=var_all,
                                              scalar1=1.0 / 1024.0, scalar2=1e-5,
                                              op0=ALU.mult, op1=ALU.add), wait=[sq])
    ev2 = p.op("v", lambda v: v.reciprocal(rstd_all, rstd_all), wait=[ev1])
    ev3 = p.op("s", lambda s: s.activation(rstd_all, rstd_all, AF.Sqrt), wait=[ev2])
    tok_bf = c.pool[:, 0:1024]
    lnf = c.scrf[:, 1024:2048]
    lnev = None
    for j in range(NT):
        dst = lnf if per_tile is not None else tok_bf
        e = p.op("v", lambda v, jj=j, d=dst: v.tensor_scalar(
            out=d, in0=c.x[:, jj, :], scalar1=mean_all[:, jj:jj + 1],
            scalar2=rstd_all[:, jj:jj + 1], op0=ALU.subtract, op1=ALU.mult),
            wait=[ev3, lnev, c.fence_ev, em])
        if per_tile is not None:
            e = per_tile(j, lnf, tok_bf, e)
        lnev = emit_tok_to_feat(c, j, tok_bf, c.banks[7], wait=[e])
    c.xt_ev = lnev


def xt_view(c, dc, t0, t1):
    return c.xt[:, dc * T + t0: dc * T + t1]



def emit_router_l1(c, px):
    p = c.p
    lw = p.dma("y", lambda y: y.dma_start(
        c.wr8[:, 0:64].rearrange("p (cc e) -> p cc e", e=8),
        c.wbf[px + "w_router"].rearrange("(cc p) e -> p cc e", p=128)),
        wait=[c.ev_prepass[px + "w_router"], c.fence_ev])
    bank = c.banks[0]
    cw = c.st[:, S_CW:S_CW + 128].rearrange("p (j e) -> p j e", e=8)
    ev = None
    prev_read = None
    for j in range(NT):
        for dc in range(NDC):
            evm = p.op("t", lambda t, d=dc, jj=j: t.matmul(
                bank.ap[:, 0:8], xt_view(c, d, jj * 128, (jj + 1) * 128),
                c.wr8[:, d * 8:(d + 1) * 8], start=(d == 0), stop=(d == NDC - 1)),
                wait=[bank.rd, lw, c.xt_ev, prev_read], tag=(dc == NDC - 1))
        bank.wr = evm
        lg = bank.ap[:, 0:8]
        mcol = c.st[:, S_TMP:S_TMP + 1]
        sumc = c.st[:, S_TMP + 1:S_TMP + 2]
        m1 = c.st[:, S_TMP + 2:S_TMP + 3]
        escr = c.scrf[:, 0:8]
        e1 = p.op("v", lambda v: v.tensor_reduce(mcol, lg, AX.X, ALU.max),
                  wait=[bank.wr, prev_read])
        e2 = p.op("s", lambda s: s.mul(mcol, mcol, -1.0), wait=[e1])
        e3 = p.op("s", lambda s: s.activation(escr, lg, AF.Exp, bias=mcol, scale=1.0,
                                              accum_out=sumc), wait=[e2, prev_read])
        bank.rd = e3
        e4 = p.op("v", lambda v: v.tensor_reduce(m1, escr, AX.X, ALU.max), wait=[e3])
        e5 = p.op("v", lambda v: v.reciprocal(sumc, sumc), wait=[e3, e4])
        e6 = p.op("v", lambda v, jj=j: v.tensor_scalar(
            out=cw[:, jj, :], in0=escr, scalar1=m1, scalar2=sumc,
            op0=ALU.is_ge, op1=ALU.mult), wait=[e5, e4])
        prev_read = e6
        ev = e6
    return ev


def emit_router_l0(c, px):
    p = c.p
    lw1 = p.dma("y", lambda y: y.dma_start(c.rt8[0:8, :], c.wbf[px + "router_weight"]),
                wait=[c.ev_prepass[px + "router_weight"], c.fence_ev])
    lw2 = p.dma("y", lambda y: y.dma_start(c.rt8[8:9, :],
                                           c.wbf[px + "shared_expert_gate_w"]),
                wait=[c.ev_prepass[px + "shared_expert_gate_w"], c.fence_ev])
    bank = c.banks[0]
    wr9 = c.wr8.rearrange("p (cc e) -> p cc e", e=10)
    for dcg in range(2):
        for i in range(4):
            dc = dcg * 4 + i
            tev = p.op("t", lambda t, d=bank_bf(bank, i)[:, 0:9],
                       s=c.rt8[0:9, dc * 128:(dc + 1) * 128]:
                       t.transpose(d, s, c.identb[0:9, 0:9]),
                       wait=[bank.rd, lw1, lw2], tag=(i == 3))
        bank.wr = tev
        for i in range(4):
            dc = dcg * 4 + i
            eev = p.op("s", lambda s, d=wr9[:, dc, 0:9],
                       sp=bank_bf(bank, i)[:, 0:9]: s.copy(d, sp),
                       wait=[bank.wr], tag=(i == 3))
        bank.rd = eev
    lwr = bank.rd
    cw = c.st[:, S_CW:S_CW + 128].rearrange("p (j e) -> p j e", e=8)
    seg = c.st[:, S_SEG:S_SEG + 16]
    ev = None
    prev_read = None
    for j in range(NT):
        for dc in range(NDC):
            evm = p.op("t", lambda t, d=dc, jj=j: t.matmul(
                bank.ap[:, 0:9], xt_view(c, d, jj * 128, (jj + 1) * 128),
                wr9[:, d, 0:9], start=(d == 0), stop=(d == NDC - 1)),
                wait=[bank.rd, lwr, c.xt_ev], tag=(dc == NDC - 1))
        bank.wr = evm
        e8 = p.op("s", lambda s, jj=j: s.activation(
            seg[:, jj:jj + 1], bank.ap[:, 8:9], AF.Sigmoid, scale=1.0),
            wait=[bank.wr])
        e8b = p.op("s", lambda s, jj=j: s.mul(seg[:, jj:jj + 1], seg[:, jj:jj + 1], 0.5),
                   wait=[e8])
        lg = bank.ap[:, 0:8]
        mcol = c.st[:, S_TMP:S_TMP + 1]
        den = c.st[:, S_TMP + 1:S_TMP + 2]
        escr = c.scrf[:, 0:8]
        s8 = c.scrf[:, 8:16]
        keep = c.scrf[:, 16:24]
        e1 = p.op("v", lambda v: v.tensor_reduce(mcol, lg, AX.X, ALU.max),
                  wait=[bank.wr, prev_read])
        e2 = p.op("s", lambda s: s.mul(mcol, mcol, -1.0), wait=[e1])
        e3 = p.op("s", lambda s: s.activation(escr, lg, AF.Exp, bias=mcol, scale=1.0),
                  wait=[e2, prev_read])
        bank.rd = e3
        e4 = p.op("v", lambda v: v.max(s8, escr), wait=[e3])
        e5 = p.op("v", lambda v: v.tensor_tensor(den, s8[:, 0:1], s8[:, 1:2], ALU.add),
                  wait=[e4])
        e6 = p.op("v", lambda v: v.reciprocal(den, den), wait=[e5])
        e7 = p.op("v", lambda v: v.tensor_single_scalar(keep, escr, s8[:, 1:2],
                                                        ALU.is_ge), wait=[e4, e3])
        e9 = p.op("v", lambda v: v.tensor_tensor(keep, keep, escr, ALU.mult), wait=[e7])
        e10 = p.op("v", lambda v, jj=j: v.tensor_scalar(
            out=cw[:, jj, :], in0=keep, scalar1=den, scalar2=0.5,
            op0=ALU.mult, op1=ALU.mult), wait=[e9, e6])
        prev_read = e10
        ev = e10
    return ev



def emit_ffn1(c, px):
    p = c.p
    emit_ln_all(c)
    rt_ev = emit_router_l1(c, px)

    wu = c.pool2[:, 0:8192].rearrange("p (b ug cc h) -> p b ug cc h", b=2, ug=2,
                                      cc=8)
    wd = c.pool2[:, 8192:24576].rearrange("p (b hh o) -> p b hh o", b=2, hh=8)
    wu_ev = [None, None]
    wu_free = [c.fence_ev, c.fence_ev]
    wd_ev = [None, None]
    wd_free = [c.fence_ev, c.fence_ev]
    h_t = c.pool.rearrange("p (hh tt) -> p hh tt", hh=32)
    xev = {}
    prev_sil_read = [None, None]

    experts = list(range(7)) + ["sh"]
    for tg in range(4):
        t0 = tg * 512
        for e in experts:
            ei = 7 if e == "sh" else e
            evug = c.s2ev[("gug", ei)]
            evwd = c.s2ev[("wd", ei)]
            h_done_q = {}
            for hp in range(16):
                b = hp % 2
                wu_ev[b] = p.dma("s", lambda y, e_=ei, hp_=hp, b_=b: y.dma_start(
                    wu[:, b_], c.s2gug[e_, hp_]),
                    wait=[wu_free[b], evug])
                for hi in range(2):
                    hh = hp * 2 + hi
                    bu = c.banks[hh % 2]
                    bg = c.banks[2 + hh % 2]
                    sil = c.scrf[:, 1024 + 512 * (hh % 2): 1536 + 512 * (hh % 2)]
                    for dc in range(NDC):
                        evu = p.op("t", lambda t, d=dc, b_=b, hi_=hi, bk=bu,
                                   t0_=t0: t.matmul(
                            bk.ap, wu[:, b_, 0, d, hi_ * 128:(hi_ + 1) * 128],
                            xt_view(c, d, t0_, t0_ + 512),
                            start=(d == 0), stop=(d == 7)),
                            wait=[bu.rd, wu_ev[b], c.xt_ev], tag=(dc == 7))
                    bu.wr = evu
                    for dc in range(NDC):
                        evg = p.op("t", lambda t, d=dc, b_=b, hi_=hi, bk=bg,
                                   t0_=t0: t.matmul(
                            bk.ap, wu[:, b_, 1, d, hi_ * 128:(hi_ + 1) * 128],
                            xt_view(c, d, t0_, t0_ + 512),
                            start=(d == 0), stop=(d == 7)),
                            wait=[bg.rd, wu_ev[b], c.xt_ev], tag=(dc == 7))
                    bg.wr = evg
                    es = p.op("s", lambda s, bk=bg, sl=sil: s.activation(
                        sl, bk.ap, AF.Silu), wait=[bg.wr, prev_sil_read[hh % 2]])
                    bg.rd = es
                    eh = p.op("v", lambda v, hh_=hh, bk=bu, sl=sil: v.tensor_tensor(
                        h_t[:, hh_, :], bk.ap, sl, ALU.mult), wait=[es, bu.wr])
                    bu.rd = eh
                    prev_sil_read[hh % 2] = eh
                    if hi == 1:
                        wu_free[b] = eh
                    if hh % 8 == 7:
                        h_done_q[hh // 8] = eh
            for q in range(4):
                b = q % 2
                wd_ev[b] = p.dma("s", lambda y, e_=ei, q_=q, b_=b: y.dma_start(
                    wd[:, b_], c.s2wd[e_, q_]),
                    wait=[wd_free[b], evwd])
                ev_last = None
                for i2 in range(2):
                    for o in range(2):
                        for half in range(2):
                            i = half * 2 + i2
                            bk = c.banks[4 + i2 * 2 + o]
                            for hh8 in range(8):
                                hh = q * 8 + hh8
                                ev_last = p.op("t", lambda t, hh_=hh, hh8_=hh8, i_=i,
                                               o_=o, b_=b, bk_=bk: t.matmul(
                                    bk_.ap, h_t[:, hh_, i_ * 128:(i_ + 1) * 128],
                                    wd[:, b_, hh8_, o_ * 512:(o_ + 1) * 512],
                                    start=(hh8_ == 0), stop=(hh8_ == 7)),
                                    wait=([bk.rd, wd_ev[b], h_done_q[q]]
                                          if hh8 == 0 else []),
                                    tag=(hh8 == 7))
                            bk.wr = ev_last
                            j = tg * 4 + i
                            sc = (0.5 if e == "sh" else
                                  c.st[:, S_CW + j * 8 + e:S_CW + j * 8 + e + 1])
                            evx = p.op("v", lambda v, jj=j, o_=o, bk_=bk, sc_=sc:
                                       v.scalar_tensor_tensor(
                                           out=c.x[:, jj, o_ * 512:(o_ + 1) * 512],
                                           in0=bk_.ap, scalar=sc_,
                                           in1=c.x[:, jj, o_ * 512:(o_ + 1) * 512],
                                           op0=ALU.mult, op1=ALU.add),
                                       wait=[bk.wr, rt_ev, xev.get((j, o))])
                            bk.rd = evx
                            xev[(j, o)] = evx
                wd_free[b] = ev_last
    c.last_x_ev = ("v", c.p.cnt["v"])



def emit_ffn0(c, px, parts=("experts", "shared"), tgs=(0, 1, 2, 3)):
    p = c.p
    emit_ln_all(c)
    rt_ev = emit_router_l0(c, px)
    cw = c.st[:, S_CW:S_CW + 128].rearrange("p (j e) -> p j e", e=8)
    seg = c.st[:, S_SEG:S_SEG + 16]

    gu = c.pool2[:, 0:16384].rearrange("p (b cc o) -> p b cc o", b=2, cc=8)
    dn = c.pool2[:, 16384:24576].rearrange("p (b cc o) -> p b cc o", b=2, cc=4)
    gu_ev = [None, None]; dn_ev = [None, None]
    gu_free = [c.fence_ev, c.fence_ev]; dn_free = [c.fence_ev, c.fence_ev]
    h_t = c.pool.rearrange("p (hh tt) -> p hh tt", hh=32)
    xev = {}
    prev_h_read = None
    sd_free_ev = c.fence_ev

    for tg in tgs:
        t0 = tg * 512
        for e in (range(8) if "experts" in parts else []):
            b = e % 2
            gu_ev[b] = p.dma("s", lambda y, e_=e, b_=b: y.dma_start(
                gu[:, b_], c.s2gu0[e_]),
                wait=[gu_free[b], c.s2ev[("gu0", e)]])
            dn_ev[b] = p.dma("s", lambda y, e_=e, b_=b: y.dma_start(
                dn[:, b_], c.s2dn0[e_]),
                wait=[dn_free[b], c.s2ev[("dn0", e)]])
            sil_ev = None
            for oc in range(8):
                bk = c.banks[oc % 2]
                for dc in range(NDC):
                    evm = p.op("t", lambda t, d=dc, o_=oc, b_=b, bk_=bk, t0_=t0:
                               t.matmul(
                        bk_.ap, gu[:, b_, d, o_ * 128:(o_ + 1) * 128],
                        xt_view(c, d, t0_, t0_ + 512),
                        start=(d == 0), stop=(d == 7)),
                        wait=[bk.rd, gu_ev[b], c.xt_ev], tag=(dc == 7))
                bk.wr = evm
                if oc < 4:
                    es = p.op("s", lambda s, o_=oc, bk_=bk: s.activation(
                        h_t[:, 16 + o_, :], bk_.ap, AF.Silu),
                        wait=[bk.wr, prev_h_read])
                    bk.rd = es
                    sil_ev = es
                else:
                    eh = p.op("v", lambda v, o_=oc, bk_=bk: v.tensor_tensor(
                        h_t[:, o_ - 4, :], bk_.ap, h_t[:, 16 + o_ - 4, :], ALU.mult),
                        wait=[bk.wr, sil_ev])
                    bk.rd = eh
            h_done = eh
            gu_free[b] = eh
            ev_last = None
            for i in range(4):
                jj = tg * 4 + i
                for o in range(2):
                    bk = c.banks[4 + (i % 2) * 2 + o]
                    for ic in range(4):
                        ev_last = p.op("t", lambda t, i_=i, o_=o, ic_=ic, b_=b, bk_=bk:
                                       t.matmul(
                            bk_.ap, h_t[:, ic_, i_ * 128:(i_ + 1) * 128],
                            dn[:, b_, ic_, o_ * 512:(o_ + 1) * 512],
                            start=(ic_ == 0), stop=(ic_ == 3)),
                            wait=[bk.rd, dn_ev[b], h_done], tag=(ic == 3))
                    bk.wr = ev_last
                    evx = p.op("v", lambda v, jj_=jj, o_=o, bk_=bk, e_=e:
                               v.scalar_tensor_tensor(
                                   out=c.x[:, jj_, o_ * 512:(o_ + 1) * 512],
                                   in0=bk_.ap, scalar=c.st[:, S_CW + jj_ * 8 + e_:S_CW + jj_ * 8 + e_ + 1],
                                   in1=c.x[:, jj_, o_ * 512:(o_ + 1) * 512],
                                   op0=ALU.mult, op1=ALU.add),
                               wait=[bk.wr, rt_ev, xev.get((jj, o))])
                    bk.rd = evx
                    xev[(jj, o)] = evx
            dn_free[b] = ev_last
            prev_h_read = ev_last
        if "shared" not in parts:
            continue
        shb = c.pool2[:, 0:8192].rearrange("p (b cc o) -> p b cc o", b=2, cc=8)
        sh_ev = [gu_ev[0], gu_ev[1]]
        sh_free = [gu_free[0], gu_free[1]]
        sil_done = {}
        for part, nm in ((0, "shared_gate_w"), (1, "shared_up_w")):
            for grp in range(4):
                b = grp % 2
                sh_ev[b] = p.dma("s", lambda y, b_=b, pt=part, g_=grp: y.dma_start(
                    shb[:, b_], c.s2sg0[pt, g_]),
                    wait=[sh_free[b], c.s2ev[("sg0", part, grp)]])
                for oc in range(4):
                    hc = grp * 4 + oc
                    bk = c.banks[oc % 2]
                    for dc in range(NDC):
                        evm = p.op("t", lambda t, d=dc, o_=oc, b_=b, bk_=bk, t0_=t0:
                                   t.matmul(
                            bk_.ap, shb[:, b_, d, o_ * 128:(o_ + 1) * 128],
                            xt_view(c, d, t0_, t0_ + 512),
                            start=(d == 0), stop=(d == 7)),
                            wait=[bk.rd, sh_ev[b], c.xt_ev], tag=(dc == 7))
                    bk.wr = evm
                    if part == 0:
                        es = p.op("s", lambda s, h_=hc, bk_=bk: s.activation(
                            h_t[:, 16 + h_, :], bk_.ap, AF.Silu),
                            wait=[bk.wr, prev_h_read])
                        bk.rd = es
                        sil_done[hc] = es
                    else:
                        eh = p.op("v", lambda v, h_=hc, bk_=bk: v.tensor_tensor(
                            h_t[:, h_, :], bk_.ap, h_t[:, 16 + h_, :], ALU.mult),
                            wait=[bk.wr, sil_done[hc], prev_h_read])
                        bk.rd = eh
                sh_free[b] = bk.rd
        h_done = bk.rd
        sdb = c.pool2[:, 8192:24576].rearrange("p (cc o) -> p cc o", cc=16)
        sd_ld = p.dma("s", lambda y: y.dma_start(sdb, c.s2sd0),
                      wait=[sd_free_ev, gu_free[0], gu_free[1], dn_free[0],
                            dn_free[1], c.s2ev["sd0"]])
        ev_last = None
        for i in range(4):
            jj = tg * 4 + i
            for o in range(2):
                bk = c.banks[4 + (i % 2) * 2 + o]
                for ic in range(16):
                    ev_last = p.op("t", lambda t, ic_=ic, i_=i, o_=o, bk_=bk:
                                   t.matmul(
                        bk_.ap, h_t[:, ic_, i_ * 128:(i_ + 1) * 128],
                        sdb[:, ic_, o_ * 512:(o_ + 1) * 512],
                        start=(ic_ == 0), stop=(ic_ == 15)),
                        wait=([bk.rd, sd_ld, h_done] if ic == 0 else []),
                        tag=(ic == 15))
                bk.wr = ev_last
                evx = p.op("v", lambda v, jj_=jj, o_=o, bk_=bk: v.scalar_tensor_tensor(
                    out=c.x[:, jj_, o_ * 512:(o_ + 1) * 512],
                    in0=bk_.ap, scalar=seg[:, jj_:jj_ + 1],
                    in1=c.x[:, jj_, o_ * 512:(o_ + 1) * 512],
                    op0=ALU.mult, op1=ALU.add),
                    wait=[bk.wr, rt_ev, xev.get((jj, o))])
                bk.rd = evx
                xev[(jj, o)] = evx
        sd_free_ev = ev_last
        prev_h_read = ev_last
        gu_free = [ev_last, ev_last]
        dn_free = [ev_last, ev_last]
    c.last_x_ev = ("v", c.p.cnt["v"])




def attn_core(c, layer, heads_iter, ka_of, vtok_of, rinv_sl, aowrite):
    raise NotImplementedError


def emit_attn0(c, px):
    p = c.p
    ropebuf = c.scrf[:, 2048:3072]
    tmp1 = c.scrf[:, 0:512]
    tmp2 = c.scrf[:, 512:1024]
    rope_evs = {}

    def rope_tile(j, lnf, tok_bf, ev):
        ld = p.dma("y", lambda y, jj=j: y.dma_start(
            ropebuf, c.dram["rope_cs"][jj * 128:(jj + 1) * 128, :]),
            wait=[rope_evs.get("last")])
        xe = lnf[:, 0:1024:2]
        xo = lnf[:, 1:1024:2]
        cosv = ropebuf[:, 0:512]
        sinv = ropebuf[:, 512:1024]
        e1 = p.op("v", lambda v: v.tensor_tensor(tmp1, xe, cosv, ALU.mult), wait=[ev, ld])
        e2 = p.op("v", lambda v: v.tensor_tensor(tmp2, xo, sinv, ALU.mult), wait=[ev, ld])
        e3 = p.op("v", lambda v: v.tensor_tensor(tok_bf[:, 0:512], tmp1, tmp2,
                                                 ALU.subtract), wait=[e1, e2])
        e4 = p.op("v", lambda v: v.tensor_tensor(tmp1, xe, sinv, ALU.mult), wait=[e3])
        e5 = p.op("v", lambda v: v.tensor_tensor(tmp2, xo, cosv, ALU.mult), wait=[e3])
        e6 = p.op("v", lambda v: v.tensor_tensor(tok_bf[:, 512:1024], tmp1, tmp2,
                                                 ALU.add), wait=[e4, e5])
        rope_evs["last"] = e6
        return e6

    emit_ln_all(c, per_tile=rope_tile)

    ka = [c.pool2[0:65, g * 2048:(g + 1) * 2048] for g in range(4)]
    v_tok = c.pool2[:, 8192:12288].rearrange("p (j o) -> p j o", o=256)
    wkT = c.pool2[:, 12288:14336].rearrange("p (cc o) -> p cc o", cc=8)
    wvT = c.pool2[:, 14336:16384].rearrange("p (cc o) -> p cc o", cc=8)
    wqT = c.pool2[:, 16384:24576].rearrange("p (cc o) -> p cc o", cc=8)
    wob = wqT
    q_t = c.pool.rearrange("p (cc tt) -> p cc tt", cc=8)

    wq = c.wbf[px + "W_q"]; wk = c.wbf[px + "W_k"]
    wv = c.wbf[px + "W_v"]; wo = c.wbf[px + "W_o"]
    evw = {k: c.ev_prepass[px + k] for k in ("W_q", "W_k", "W_v", "W_o")}

    e1 = p.dma("s", lambda y: y.dma_start(
        wkT.rearrange("p cc o -> p (cc o)"), c.s2l0k.rearrange("p a b -> p (a b)")),
        wait=[c.s2ev["l0k"], c.fence_ev])
    e2 = p.dma("s", lambda y: y.dma_start(
        wvT.rearrange("p cc o -> p (cc o)"), c.s2l0v.rearrange("p a b -> p (a b)")),
        wait=[c.s2ev["l0v"], c.fence_ev])
    kv_ld = e2
    wq_ld = p.dma("s", lambda y: y.dma_start(
        wqT.rearrange("p cc o -> p (cc o)"), c.s2l0q.rearrange("p a b -> p (a b)")),
        wait=[c.s2ev["l0q"], c.fence_ev])

    for tg in range(4):
        t0 = tg * 512
        for kc in range(2):
            bk = c.banks[kc]
            for dc in range(NDC):
                evm = p.op("t", lambda t, d=dc, k_=kc, t0_=t0, bk_=bk: t.matmul(
                    bk_.ap, wkT[:, d, k_ * 128:(k_ + 1) * 128],
                    xt_view(c, d, t0_, t0_ + 512), start=(d == 0), stop=(d == 7)),
                    wait=[bk.rd, kv_ld, c.xt_ev], tag=(dc == 7))
            bk.wr = evm
            e1 = p.op("s", lambda s, k_=kc, t0_=t0, bk_=bk: s.copy(
                ka[2 * k_][0:64, t0_:t0_ + 512], bk_.ap[0:64, :]), wait=[bk.wr])
            e2 = p.op("s", lambda s, k_=kc, t0_=t0, bk_=bk: s.copy(
                ka[2 * k_ + 1][0:64, t0_:t0_ + 512], bk_.ap[64:128, :]), wait=[bk.wr])
            bk.rd = e2
    k_done = e2
    mevs = [p.op("g", lambda gg, g_=g: gg.memset(ka[g_][64:65, :], -1.0),
                 wait=[c.fence_ev]) for g in range(4)]
    ka_neg1 = mevs[-1]

    vev = None
    for j in range(NT):
        bk = c.banks[2 + j % 2]
        for dc in range(NDC):
            evm = p.op("t", lambda t, d=dc, jj=j, bk_=bk: t.matmul(
                bk_.ap[:, 0:256], xt_view(c, d, jj * 128, (jj + 1) * 128),
                wvT[:, d, :], start=(d == 0), stop=(d == 7)),
                wait=[bk.rd, kv_ld, c.xt_ev], tag=(dc == 7))
        bk.wr = evm
        vev = p.op("s", lambda s, jj=j, bk_=bk: s.copy(v_tok[:, jj, :],
                                                       bk_.ap[:, 0:256]),
                   wait=[bk.wr])
        bk.rd = vev
    v_done = vev

    qev = None
    for oc in range(NDC):
        for tg in range(4):
            t0 = tg * 512
            bk = c.banks[4 + tg % 2]
            for dc in range(NDC):
                evm = p.op("t", lambda t, d=dc, o_=oc, t0_=t0, bk_=bk: t.matmul(
                    bk_.ap, wqT[:, d, o_ * 128:(o_ + 1) * 128],
                    xt_view(c, d, t0_, t0_ + 512), start=(d == 0), stop=(d == 7)),
                    wait=[bk.rd, wq_ld, c.xt_ev], tag=(dc == 7))
            bk.wr = evm
            ev = p.op("s", lambda s, o_=oc, t0_=t0, bk_=bk: s.copy(
                q_t[:, o_, t0_:t0_ + 512], bk_.ap), wait=[bk.wr])
            bk.rd = ev
            qev = ev

    Qa = c.xt[0:65, 0:512]
    escr = c.xt[:, 512:2560]
    ptb = c.xt[:, 2560:6656].rearrange("p (k q) -> p k q", k=8)
    aost = c.xt[:, 6656:10752].rearrange("p (i d) -> p i d", i=4)
    aoT = c.xt[:, 10752:11776].rearrange("p (cc d) -> p cc d", cc=8)
    avacc = c.scrf[:, 1024:1280].rearrange("p (i d) -> p i d", i=4)

    mcol = c.st[:, S_AT:S_AT + 1]
    mbf = c.st[:, S_AT + 1:S_AT + 2].bitcast(BF16)[:, 0:1]
    biasc = c.st[:, S_AT + 2:S_AT + 3]
    rcol = c.st[:, S_AT + 3:S_AT + 4]
    rinv = c.st[:, S_AT + 4:S_AT + 5]
    rsl = c.st[:, S_AT + 8:S_AT + 12]

    xev = {}
    qa_free = c.fence_ev
    stat_free = None
    for jg in range(4):
        aosl_ev = {}
        for h in range(16):
            g = h // 4
            qa_evs = []
            for i in range(4):
                j = jg * 4 + i
                qa_evs.append(p.op("g", lambda gg, i_=i, jj=j, h_=h:
                                   gg.tensor_copy(
                    Qa[0:64, i_ * 128:(i_ + 1) * 128],
                    q_t[64 * (h_ % 2):64 * (h_ % 2) + 64, h_ // 2,
                        jj * 128:(jj + 1) * 128]),
                    wait=[qev, qa_free]))
            qa_done = qa_evs[-1]
            row_evs = []
            for i in range(4):
                j = jg * 4 + i
                L = (j + 1) * 128
                nb = (L + 511) // 512
                off = 0 if i % 2 == 0 else 4
                reg = c.pa if off == 0 else c.pb
                wl = [qa_done, k_done, ka_neg1] + \
                     [c.banks[off + q].rd for q in range(nb)]
                for kb in range(nb):
                    n = min(512, L - kb * 512)
                    evm = p.op("t", lambda t, kb_=kb, n_=n, i_=i, g_=g, r_=reg:
                               t.matmul(
                        r_[:, kb_ * 512:kb_ * 512 + n_],
                        Qa[0:64, i_ * 128:(i_ + 1) * 128],
                        ka[g_][0:64, kb_ * 512:kb_ * 512 + n_],
                        start=True, stop=True),
                        wait=wl, tag=(kb == nb - 1))
                for kb in range(nb):
                    c.banks[off + kb].wr = evm
                em = p.op("v", lambda v, jj=j, r_=reg: v.tensor_tensor(
                    r_[:, jj * 128:(jj + 1) * 128],
                    r_[:, jj * 128:(jj + 1) * 128], c.maskQ, ALU.add),
                    wait=[evm, c.ev_consts])
                e1 = p.op("v", lambda v, L_=L, r_=reg: v.tensor_reduce(
                    mcol, r_[:, 0:L_], AX.X, ALU.max), wait=[em, stat_free])
                e2 = p.op("v", lambda v: v.tensor_copy(mbf, mcol), wait=[e1])
                e3 = p.op("s", lambda s: s.mul(biasc, mbf, -0.125), wait=[e2])
                e4 = p.op("s", lambda s, L_=L, r_=reg: s.activation(
                    escr[:, 0:L_], r_[:, 0:L_], AF.Exp, scale=0.125, bias=biasc,
                    accum_out=rcol), wait=[e3])
                for kb in range(nb):
                    c.banks[off + kb].rd = e4
                e5 = p.op("v", lambda v: v.reciprocal(rinv, rcol), wait=[e4])
                e5b = p.op("v", lambda v, i_=i: v.tensor_copy(
                    rsl[:, i_:i_ + 1], rinv), wait=[e5])
                b6 = c.banks[6]
                et = p.op("t", lambda t, d=bank_bf(b6, 0)[0:1, :]: t.transpose(
                    d, mbf, c.identb), wait=[e2, b6.rd, e4])
                b6.wr = et
                erow = p.op("s", lambda s, i_=i, d=bank_bf(b6, 0)[0:1, :]: s.copy(
                    Qa[64:65, i_ * 128:(i_ + 1) * 128], d), wait=[et, qa_done])
                b6.rd = erow
                row_evs.append(erow)
                stat_free = erow
            rows_done = row_evs[-1]
            rinv_done = ("v", p.cnt["v"])
            nkt = jg * 4 + 4
            avb = [c.banks[q] for q in range(4)]
            for half in range((nkt + 7) // 8):
                kts = list(range(half * 8, min(nkt, half * 8 + 8)))
                pt_done = {}
                for kt in kts:
                    b45 = c.banks[4 + kt % 2]
                    evm = p.op("t", lambda t, kt_=kt, g_=g, bk=b45: t.matmul(
                        bk.ap, ka[g_][0:65, kt_ * 128:(kt_ + 1) * 128],
                        Qa[0:65, :], start=True, stop=True),
                        wait=[b45.rd, rows_done, ka_neg1], tag=True)
                    b45.wr = evm
                    if kt >= jg * 4:
                        i = kt - jg * 4
                        evm = p.op("v", lambda v, i_=i, bk=b45: v.tensor_tensor(
                            bk.ap[:, i_ * 128:(i_ + 1) * 128],
                            bk.ap[:, i_ * 128:(i_ + 1) * 128], c.maskT, ALU.add),
                            wait=[evm])
                    ee = p.op("s", lambda s, kt_=kt, bk=b45: s.activation(
                        ptb[:, kt_ % 8, :], bk.ap[:, 0:512], AF.Exp, scale=0.125),
                        wait=[evm, pt_done.get(kt % 2)])
                    b45.rd = ee
                    pt_done[kt % 2] = ee
                pt_all = ("s", p.cnt["s"])
                for i in range(4):
                    j = jg * 4 + i
                    lo = half * 8
                    hi = min(j, half * 8 + 7)
                    if hi < lo:
                        continue
                    for kt in range(lo, hi + 1):
                        eav = p.op("t", lambda t, i_=i, kt_=kt, g_=g, lo_=lo, hi_=hi:
                                   t.matmul(
                            avb[i_].ap[:, 0:64], ptb[:, kt_ % 8, i_ * 128:(i_ + 1) * 128],
                            v_tok[:, kt_, g_ * 64:(g_ + 1) * 64],
                            start=(kt_ == lo_), stop=(kt_ == hi_)),
                            wait=([avb[i].rd, pt_all, v_done] if kt == lo else []),
                            tag=(kt == hi))
                    avb[i].wr = eav
                    if half == 0:
                        if j < 8:
                            evo = p.op("v", lambda v, i_=i, h_=h: v.tensor_scalar_mul(
                                aost[:, i_, h_ * 64:(h_ + 1) * 64],
                                avb[i_].ap[:, 0:64], rsl[:, i_:i_ + 1]),
                                wait=[avb[i].wr, rinv_done, aosl_ev.get((i, h))])
                        else:
                            evo = p.op("v", lambda v, i_=i: v.tensor_copy(
                                avacc[:, i_, :], avb[i_].ap[:, 0:64]),
                                wait=[avb[i].wr])
                    else:
                        e6 = p.op("v", lambda v, i_=i: v.tensor_tensor(
                            avacc[:, i_, :], avacc[:, i_, :], avb[i_].ap[:, 0:64],
                            ALU.add), wait=[avb[i].wr])
                        evo = p.op("v", lambda v, i_=i, h_=h: v.tensor_scalar_mul(
                            aost[:, i_, h_ * 64:(h_ + 1) * 64], avacc[:, i_, :],
                            rsl[:, i_:i_ + 1]),
                            wait=[e6, rinv_done, aosl_ev.get((i, h))])
                    avb[i].rd = evo
                    aosl_ev[(i, h)] = evo
            qa_free = ("t", p.cnt["t"])
        ao_done = ("v", p.cnt["v"])
        for i in range(4):
            j = jg * 4 + i
            b6 = c.banks[6]
            aoT_ev = None
            for halfc in range(2):
                for q in range(4):
                    dc = halfc * 4 + q
                    et = p.op("t", lambda t, d=bank_bf(b6, q),
                              s=aost[:, i, dc * 128:(dc + 1) * 128]:
                              t.transpose(d, s, c.identb),
                              wait=[b6.rd, ao_done], tag=(q == 3))
                b6.wr = et
                for q in range(4):
                    dc = halfc * 4 + q
                    ee = p.op("s", lambda s, d=aoT[:, dc, :], sp=bank_bf(b6, q):
                              s.copy(d, sp), wait=[b6.wr], tag=(q == 3))
                b6.rd = ee
            aoT_ev = ee
            if jg == 0 and i == 0:
                c.wo_ld = p.dma("s", lambda y: y.dma_start(
                    wob.rearrange("p cc o -> p (cc o)"),
                    c.s2l0o.rearrange("p a b -> p (a b)")),
                    wait=[c.s2ev["l0o"], ("t", p.cnt["t"])])
            for o in range(2):
                bk = c.banks[o]
                for dc in range(NDC):
                    evm = p.op("t", lambda t, d=dc, o_=o, bk_=bk: t.matmul(
                        bk_.ap, aoT[:, d, :], wob[:, d, o_ * 512:(o_ + 1) * 512],
                        start=(d == 0), stop=(d == 7)),
                        wait=[bk.rd, c.wo_ld, aoT_ev], tag=(dc == 7))
                bk.wr = evm
                evx = p.op("v", lambda v, jj=j, o_=o, bk_=bk: v.tensor_tensor(
                    c.x[:, jj, o_ * 512:(o_ + 1) * 512],
                    c.x[:, jj, o_ * 512:(o_ + 1) * 512], bk_.ap, ALU.add),
                    wait=[bk.wr, xev.get((j, o)), c.last_x_ev])
                bk.rd = evx
                xev[(j, o)] = evx
    c.last_x_ev = ("v", c.p.cnt["v"])



def emit_attn1(c, px):
    p = c.p
    lc = p.dma("y", lambda y: y.dma_start(
        c.curv[0:1, 0:1], c.dram[px + "curv"].rearrange("(a x) -> a x", a=1)))
    b6 = c.banks[6]
    cbv = c.curv.bitcast(BF16)[0:1, 2:3]
    ecst = p.op("v", lambda v: v.tensor_copy(cbv, c.curv[0:1, 0:1]), wait=[lc])
    e1 = p.op("t", lambda t: t.matmul(b6.ap[:, 0:1], c.ones[0:1, 0:128],
                                      cbv, start=True, stop=True),
              wait=[ecst, b6.rd, c.ev_consts])
    b6.wr = e1
    e2 = p.op("s", lambda s: s.copy(c.ccol[:, 0:1], b6.ap[:, 0:1]), wait=[e1])
    b6.rd = e2
    e3 = p.op("s", lambda s: s.activation(c.ccol[:, 1:2], c.ccol[:, 0:1], AF.Sqrt),
              wait=[e2])
    e4 = p.op("v", lambda v: v.reciprocal(c.ccol[:, 2:3], c.ccol[:, 0:1]), wait=[e2])
    sqc = c.ccol[:, 1:2]
    invc = c.ccol[:, 2:3]
    curv_ev = [e3, e4]

    n2c = c.st[:, S_TMP + 4:S_TMP + 5]
    ncc = c.st[:, S_TMP + 5:S_TMP + 6]
    coefc = c.st[:, S_TMP + 6:S_TMP + 7]
    t1c = c.st[:, S_TMP + 7:S_TMP + 8]

    def logmap_tile(j, lnf, tok_bf, ev):
        scr = c.scrf[:, 0:1024]
        a1 = p.op("s", lambda s: s.activation(scr, lnf, AF.Square, accum_out=n2c),
                  wait=[ev])
        a2 = p.op("s", lambda s: s.activation(ncc, n2c, AF.Sqrt), wait=[a1])
        a3 = p.op("v", lambda v: v.tensor_tensor(ncc, ncc, sqc, ALU.mult),
                  wait=[a2] + curv_ev)
        a4 = p.op("s", lambda s: s.activation(t1c, ncc, AF.Square), wait=[a3])
        a5 = p.op("v", lambda v: v.tensor_scalar_add(t1c, t1c, 1.0), wait=[a4])
        a6 = p.op("s", lambda s: s.activation(t1c, t1c, AF.Sqrt), wait=[a5])
        a7 = p.op("v", lambda v: v.tensor_tensor(t1c, t1c, ncc, ALU.add), wait=[a6])
        a8 = p.op("s", lambda s: s.activation(t1c, t1c, AF.Ln), wait=[a7])
        a9 = p.op("v", lambda v: v.reciprocal(coefc, ncc), wait=[a8])
        a10 = p.op("v", lambda v: v.tensor_tensor(coefc, coefc, t1c, ALU.mult),
                   wait=[a9])
        a11 = p.op("v", lambda v: v.tensor_scalar_mul(tok_bf, lnf, coefc), wait=[a10])
        return a11

    emit_ln_all(c, per_tile=logmap_tile)

    qtok = c.pool[:, 0:2048].rearrange("p (j d) -> p j d", j=NT)
    ktok = c.pool[:, 2048:4096].rearrange("p (j d) -> p j d", j=NT)
    Qa = c.pool[0:66, 4096:4608]
    Ka = c.pool[0:66, 4608:6656]
    vch = c.pool[:, 6656:8704].rearrange("p (j d) -> p j d", j=NT)
    ptb = c.pool[:, 8704:12800].rearrange("p (k q) -> p k q", k=8)
    escr = c.pool[:, 12800:14848]
    aoT = c.pool[:, 14848:15872].rearrange("p (cc d) -> p cc d", cc=8)
    qf = c.pool[:, 8704:10752]
    avacc = c.scrf[:, 1024:1280].rearrange("p (i d) -> p i d", i=4)
    ao = c.pool2[:, 0:16384].rearrange("p (j d) -> p j d", j=NT)
    wpb2 = c.pool2[:, 16384:17408].rearrange("p (cc o) -> p cc o", cc=8)
    wob = c.pool2[:, 16384:24576].rearrange("p (cc o) -> p cc o", cc=8)

    bev = []
    for nm, dst in (("wq_b", c.browq), ("wk_b", c.browk), ("wv_b", c.browv),
                    ("wo_b", c.browo)):
        bev.append(p.dma("g", lambda g, d=dst, s=c.wbf[px + nm]: g.dma_start(d, s),
                         wait=[c.fence_ev]))
    brow_ev = bev[-1]

    wqn = c.wbf[px + "wq_w"]; wkn = c.wbf[px + "wk_w"]
    wvn = c.wbf[px + "wv_w"]; won = c.wbf[px + "wo_w"]
    evw = {k: c.ev_prepass[px + k] for k in ("wq_w", "wk_w", "wv_w", "wo_w")}

    n2c = c.st[:, S_TMP + 4:S_TMP + 5]
    ncc = c.st[:, S_TMP + 5:S_TMP + 6]
    coefc = c.st[:, S_TMP + 6:S_TMP + 7]
    t1c = c.st[:, S_TMP + 7:S_TMP + 8]
    n2h = c.st[:, S_AT + 16:S_AT + 32]
    timbf = c.st[:, S_AT + 32:S_AT + 40].bitcast(BF16)
    pairb = c.st[:, S_AT + 40:S_AT + 42].bitcast(BF16)[:, 0:4]
    mcol = c.st[:, S_AT + 42:S_AT + 43]
    mbf = c.st[:, S_AT + 43:S_AT + 44].bitcast(BF16)[:, 0:1]
    biasc = c.st[:, S_AT + 44:S_AT + 45]
    rcol = c.st[:, S_AT + 45:S_AT + 46]
    rinv = c.st[:, S_AT + 46:S_AT + 47]
    rsl = c.st[:, S_AT + 8:S_AT + 12]
    sq_scr = c.scrf[:, 0:64]

    xev = {}
    wp_free = c.fence_ev
    qf_free = c.fence_ev

    def proj_feat_to_tok(cc, wscr, wld_ev, brow, tokdst):
        nonlocal wp_free, qf_free
        qf_free = ("t", p.cnt["t"])
        wp_ld = p.dma("s", lambda y, s_=wscr, c_=cc: y.dma_start(
            wpb2, s_[c_]), wait=[wp_free, wld_ev])
        qf_ev = None
        for tg in range(4):
            bk = c.banks[4 + tg % 2]
            for dc in range(NDC):
                p.op("t", lambda t, d=dc, tg_=tg, bk_=bk: t.matmul(
                    bk_.ap, wpb2[:, d, :],
                    xt_view(c, d, tg_ * 512, tg_ * 512 + 512),
                    start=(d == 0), stop=False),
                    wait=[bk.rd, wp_ld, c.xt_ev], tag=False)
            evm = p.op("t", lambda t, c_=cc, bk_=bk, br=brow: t.matmul(
                bk_.ap, br[0:1, c_ * 128:(c_ + 1) * 128], c.ones[0:1, 0:512],
                start=False, stop=True), wait=[brow_ev], tag=True)
            bk.wr = evm
            ee = p.op("s", lambda s, tg_=tg, bk_=bk: s.copy(
                qf[:, tg_ * 512:(tg_ + 1) * 512], bk_.ap), wait=[bk.wr, qf_free])
            bk.rd = ee
            qf_ev = ee
        wp_free = ("t", p.cnt["t"])
        tev = None
        b7 = c.banks[7]
        for jq in range(4):
            for q in range(4):
                j = jq * 4 + q
                et = p.op("t", lambda t, d=bank_bf(b7, q),
                          s=qf[:, j * 128:(j + 1) * 128]: t.transpose(d, s, c.identb),
                          wait=[b7.rd, qf_ev], tag=(q == 3))
            b7.wr = et
            for q in range(4):
                j = jq * 4 + q
                ec = p.op("s", lambda s, jj=j, sp=bank_bf(b7, q), td=tokdst: s.copy(
                    td[:, jj, :], sp), wait=[b7.wr], tag=(q == 3))
            b7.rd = ec
            tev = ec
        qf_free = ("t", p.cnt["t"])
        return tev

    nb2 = c.st[:, S_AT + 16:S_AT + 32]
    ncb = c.st[:, S_AT + 64:S_AT + 80]
    e1b = c.st[:, S_AT + 80:S_AT + 96]
    e2b = c.st[:, S_AT + 96:S_AT + 112]
    coefb = c.st[:, S_AT + 112:S_AT + 128]
    timf = c.st[:, S_AT + 128:S_AT + 144]
    timbfQ = c.st[:, S_AT + 144:S_AT + 152].bitcast(BF16)
    timbfK = c.st[:, S_AT + 32:S_AT + 40].bitcast(BF16)
    scr2 = c.scrf[:, 0:1024].rearrange("p (j d) -> p j d", j=NT)
    exm_prev = [None]

    war_tim = {"K": None, "Q": None}

    def expmap_batch(tok, hd, timdst, neg, ev_tok, war_key):
        sl3 = tok[:, :, hd * 64:(hd + 1) * 64]
        a1 = p.op("v", lambda v: v.tensor_tensor(scr2, sl3, sl3, ALU.mult),
                  wait=[ev_tok, exm_prev[0], war_tim[war_key]])
        a2 = p.op("v", lambda v: v.tensor_reduce(
            nb2.rearrange("p (j o) -> p j o", o=1), scr2, AX.X, ALU.add), wait=[a1])
        a3 = p.op("s", lambda s: s.activation(ncb, nb2, AF.Sqrt), wait=[a2])
        a4 = p.op("v", lambda v: v.tensor_scalar_mul(ncb, ncb, sqc),
                  wait=[a3] + curv_ev)
        a5 = p.op("s", lambda s: s.activation(e1b, ncb, AF.Exp), wait=[a4])
        a6 = p.op("s", lambda s: s.activation(e2b, ncb, AF.Exp, scale=-1.0), wait=[a4])
        a7 = p.op("v", lambda v: v.tensor_tensor(e1b, e1b, e2b, ALU.subtract),
                  wait=[a5, a6])
        a8 = p.op("v", lambda v: v.reciprocal(e2b, ncb), wait=[a6, a7])
        a9 = p.op("v", lambda v: v.tensor_tensor(coefb, e1b, e2b, ALU.mult),
                  wait=[a7, a8])
        a10 = p.op("v", lambda v: v.tensor_scalar_mul(coefb, coefb, 0.5), wait=[a9])
        ah = a10
        for j in range(NT):
            ah = p.op("v", lambda v, jj=j: v.tensor_scalar_mul(
                tok[:, jj, hd * 64:(hd + 1) * 64], tok[:, jj, hd * 64:(hd + 1) * 64],
                coefb[:, jj:jj + 1]), wait=[a10] if j == 0 else [ah])
        a12 = p.op("v", lambda v: v.tensor_tensor(scr2, sl3, sl3, ALU.mult), wait=[ah])
        a13 = p.op("v", lambda v: v.tensor_reduce(
            nb2.rearrange("p (j o) -> p j o", o=1), scr2, AX.X, ALU.add), wait=[a12])
        a14 = p.op("v", lambda v: v.tensor_single_scalar(timf, nb2, invc, ALU.add),
                   wait=[a13] + curv_ev)
        a15 = p.op("s", lambda s: s.activation(timf, timf, AF.Sqrt), wait=[a14])
        a16 = p.op("v", lambda v, sc=-1.0 if neg else 1.0: v.tensor_scalar_mul(
            timdst, timf, sc), wait=[a15])
        exm_prev[0] = a16
        return a16

    for cc in range(8):
        tq = proj_feat_to_tok(cc, c.s2l1q, c.s2ev[("l1q", cc)], c.browq, qtok)
        tk = proj_feat_to_tok(cc, c.s2l1k, c.s2ev[("l1k", cc)], c.browk, ktok)
        wv_ld = p.dma("s", lambda y, c_=cc: y.dma_start(
            wpb2, c.s2l1v[c_]), wait=[wp_free, c.s2ev[("l1v", cc)]])
        vev = None
        for j in range(NT):
            bk = c.banks[4 + j % 2]
            for dc in range(NDC):
                p.op("t", lambda t, d=dc, jj=j, bk_=bk: t.matmul(
                    bk_.ap[:, 0:128], xt_view(c, d, jj * 128, (jj + 1) * 128),
                    wpb2[:, d, :], start=(d == 0), stop=False),
                    wait=[bk.rd, wv_ld, c.xt_ev], tag=False)
            evm = p.op("t", lambda t, bk_=bk, c_=cc: t.matmul(
                bk_.ap[:, 0:128], c.ones[0:1, 0:128],
                c.browv[0:1, c_ * 128:(c_ + 1) * 128], start=False, stop=True),
                wait=[brow_ev], tag=True)
            bk.wr = evm
            vev = p.op("s", lambda s, jj=j, bk_=bk: s.copy(
                vch[:, jj, :], bk_.ap[:, 0:128]), wait=[bk.wr])
            bk.rd = vev
        wp_free = ("t", p.cnt["t"])
        v_done = vev
        for hd in range(2):
            h = cc * 2 + hd
            ek = expmap_batch(ktok, hd, timbfK, True, tk, "K")
            eka0 = p.op("g", lambda g: g.memset(Ka[64:66, :], -1.0), wait=[ek])
            kev = None
            for j in range(NT):
                sl = ktok[:, j, hd * 64:(hd + 1) * 64]
                et = p.op("t", lambda t, d=bank_bf(b6, 0)[0:64, :], sl_=sl:
                          t.transpose(d, sl_, c.identb), wait=[ek, b6.rd])
                b6.wr = et
                ec = p.op("s", lambda s, jj=j, d=bank_bf(b6, 0)[0:64, :]: s.copy(
                    Ka[0:64, jj * 128:(jj + 1) * 128], d), wait=[et])
                b6.rd = ec
                et2 = p.op("t", lambda t, jj=j, d=bank_bf(b6, 1)[0:1, :]: t.transpose(
                    d, timbfK[:, jj:jj + 1], c.identb), wait=[ek, b6.rd])
                b6.wr = et2
                ec2 = p.op("s", lambda s, jj=j, d=bank_bf(b6, 1)[0:1, :]: s.copy(
                    Ka[64:65, jj * 128:(jj + 1) * 128], d), wait=[et2, eka0])
                b6.rd = ec2
                kev = ec2
            ka_done = kev
            war_tim["K"] = ka_done
            eq = expmap_batch(qtok, hd, timbfQ, False, tq, "Q")
            for jg in range(4):
                qa_evs = []
                for i in range(4):
                    j = jg * 4 + i
                    sl = qtok[:, j, hd * 64:(hd + 1) * 64]
                    et = p.op("t", lambda t, d=bank_bf(b6, 0)[0:64, :], sl_=sl:
                              t.transpose(d, sl_, c.identb), wait=[eq, b6.rd])
                    b6.wr = et
                    ec = p.op("s", lambda s, i_=i, d=bank_bf(b6, 0)[0:64, :]: s.copy(
                        Qa[0:64, i_ * 128:(i_ + 1) * 128], d), wait=[et])
                    b6.rd = ec
                    et2 = p.op("t", lambda t, jj=j, d=bank_bf(b6, 1)[0:1, :]:
                               t.transpose(d, timbfQ[:, jj:jj + 1], c.identb),
                               wait=[eq, b6.rd])
                    b6.wr = et2
                    ec2 = p.op("s", lambda s, i_=i, d=bank_bf(b6, 1)[0:1, :]: s.copy(
                        Qa[64:65, i_ * 128:(i_ + 1) * 128], d), wait=[et2])
                    b6.rd = ec2
                    qa_evs.append(ec2)
                qa_done = qa_evs[-1]
                row_evs = []
                stat_free = None
                for i in range(4):
                    j = jg * 4 + i
                    off = 0 if i % 2 == 0 else 4
                    reg = c.pa if off == 0 else c.pb
                    wl2 = [qa_done, ka_done, stat_free] + \
                          [c.banks[off + q].rd for q in range(4)]
                    for kb in range(4):
                        evm = p.op("t", lambda t, kb_=kb, i_=i, r_=reg: t.matmul(
                            r_[:, kb_ * 512:(kb_ + 1) * 512],
                            Qa[0:65, i_ * 128:(i_ + 1) * 128],
                            Ka[0:65, kb_ * 512:(kb_ + 1) * 512],
                            start=True, stop=True), wait=wl2, tag=(kb == 3))
                    for kb in range(4):
                        c.banks[off + kb].wr = evm
                    e1 = p.op("v", lambda v, r_=reg: v.tensor_reduce(
                        mcol, r_[:, 0:2048], AX.X, ALU.min), wait=[evm])
                    e2 = p.op("v", lambda v: v.tensor_copy(mbf, mcol), wait=[e1])
                    e3 = p.op("s", lambda s: s.mul(biasc, mbf, 0.125), wait=[e2])
                    e4 = p.op("s", lambda s, r_=reg: s.activation(
                        escr, r_[:, 0:2048], AF.Exp, scale=-0.125, bias=biasc,
                        accum_out=rcol), wait=[e3])
                    for kb in range(4):
                        c.banks[off + kb].rd = e4
                    e5 = p.op("v", lambda v: v.reciprocal(rinv, rcol), wait=[e4])
                    p.op("v", lambda v, i_=i: v.tensor_copy(
                        rsl[:, i_:i_ + 1], rinv), wait=[e5])
                    e6 = p.op("v", lambda v, jj=j: v.tensor_copy(
                        pairb[:, 2:3], timbfQ[:, jj:jj + 1]), wait=[eq])
                    e7 = p.op("v", lambda v: v.tensor_copy(pairb[:, 3:4], mbf),
                              wait=[e2, e6])
                    et = p.op("t", lambda t, d=bank_bf(b6, 2)[0:2, :]: t.transpose(
                        d, pairb[:, 2:4], c.identb), wait=[e7, b6.rd])
                    b6.wr = et
                    erow = p.op("s", lambda s, i_=i, d=bank_bf(b6, 2)[0:2, :]: s.copy(
                        Qa[64:66, i_ * 128:(i_ + 1) * 128], d), wait=[et, e4])
                    b6.rd = erow
                    row_evs.append(erow)
                    stat_free = erow
                rows_done = row_evs[-1]
                war_tim["Q"] = rows_done
                rinv_done = ("v", p.cnt["v"])
                avb = [c.banks[q] for q in range(4)]
                for half in range(2):
                    kts = list(range(half * 8, half * 8 + 8))
                    pt_done = {}
                    for kt in kts:
                        b45 = c.banks[4 + kt % 2]
                        evm = p.op("t", lambda t, kt_=kt, bk=b45: t.matmul(
                            bk.ap, Ka[0:66, kt_ * 128:(kt_ + 1) * 128], Qa[0:66, :],
                            start=True, stop=True),
                            wait=[b45.rd, rows_done], tag=True)
                        b45.wr = evm
                        ee = p.op("s", lambda s, kt_=kt, bk=b45: s.activation(
                            ptb[:, kt_ % 8, :], bk.ap, AF.Exp, scale=-0.125),
                            wait=[evm, pt_done.get(kt % 2)])
                        b45.rd = ee
                        pt_done[kt % 2] = ee
                    pt_all = ("s", p.cnt["s"])
                    for i in range(4):
                        lo, hi = half * 8, half * 8 + 7
                        for kt in range(lo, hi + 1):
                            eav = p.op("t", lambda t, i_=i, kt_=kt, hd_=hd, lo_=lo,
                                       hi_=hi: t.matmul(
                                avb[i_].ap[:, 0:64],
                                ptb[:, kt_ % 8, i_ * 128:(i_ + 1) * 128],
                                vch[:, kt_, hd_ * 64:(hd_ + 1) * 64],
                                start=(kt_ == lo_), stop=(kt_ == hi_)),
                                wait=([avb[i].rd, pt_all, v_done] if kt == lo
                                      else []),
                                tag=(kt == hi))
                        avb[i].wr = eav
                        if half == 0:
                            evo = p.op("v", lambda v, i_=i: v.tensor_copy(
                                avacc[:, i_, :], avb[i_].ap[:, 0:64]),
                                wait=[avb[i].wr])
                        else:
                            e6 = p.op("v", lambda v, i_=i: v.tensor_tensor(
                                avacc[:, i_, :], avacc[:, i_, :],
                                avb[i_].ap[:, 0:64], ALU.add), wait=[avb[i].wr])
                            j = jg * 4 + i
                            evo = p.op("v", lambda v, i_=i, jj=j, h_=h:
                                       v.tensor_scalar_mul(
                                ao[:, jj, h_ * 64:(h_ + 1) * 64], avacc[:, i_, :],
                                rsl[:, i_:i_ + 1]), wait=[e6, rinv_done])
                        avb[i].rd = evo
    ao_done = ("v", p.cnt["v"])
    wo_ld = p.dma("s", lambda y: y.dma_start(
        wob.rearrange("p cc o -> p (cc o)"), c.s2l1o.rearrange("p a b -> p (a b)")),
        wait=[c.s2ev["l1o"], wp_free, ("t", p.cnt["t"])])
    b7 = c.banks[7]
    for j in range(NT):
        aoT_ev = None
        for halfc in range(2):
            for q in range(4):
                dc = halfc * 4 + q
                et = p.op("t", lambda t, d=bank_bf(b7, q),
                          s=ao[:, j, dc * 128:(dc + 1) * 128]:
                          t.transpose(d, s, c.identb),
                          wait=[b7.rd, ao_done], tag=(q == 3))
            b7.wr = et
            for q in range(4):
                dc = halfc * 4 + q
                ee = p.op("s", lambda s, d=aoT[:, dc, :], sp=bank_bf(b7, q):
                          s.copy(d, sp), wait=[b7.wr], tag=(q == 3))
            b7.rd = ee
        aoT_ev = ee
        for o in range(2):
            bk = c.banks[o]
            for dc in range(NDC):
                p.op("t", lambda t, d=dc, o_=o, bk_=bk: t.matmul(
                    bk_.ap, aoT[:, d, :], wob[:, d, o_ * 512:(o_ + 1) * 512],
                    start=(d == 0), stop=False),
                    wait=[bk.rd, wo_ld, aoT_ev], tag=False)
            evm = p.op("t", lambda t, o_=o, bk_=bk: t.matmul(
                bk_.ap, c.ones[0:1, 0:128], c.browo[0:1, o_ * 512:(o_ + 1) * 512],
                start=False, stop=True), wait=[brow_ev], tag=True)
            bk.wr = evm
            evx = p.op("v", lambda v, jj=j, o_=o, bk_=bk: v.tensor_tensor(
                c.x[:, jj, o_ * 512:(o_ + 1) * 512],
                c.x[:, jj, o_ * 512:(o_ + 1) * 512], bk_.ap, ALU.add),
                wait=[bk.wr, xev.get((j, o)), c.last_x_ev])
            bk.rd = evx
            xev[(j, o)] = evx
    c.last_x_ev = ("v", c.p.cnt["v"])


SUB_PREFIX = {"f0a": "L0_ffn_", "f0b": "L0_ffn_", "a0": "L0_attn_",
              "f1a": "L1_ffn_", "f1b": "L1_ffn_", "a1": "L1_attn_"}
SUB_WNAMES = {
    "f0a": ["router_weight", "gate_up_proj", "down_proj", "shared_gate_w",
            "shared_up_w", "shared_down_w", "shared_expert_gate_w"],
    "a0": ["W_q", "W_k", "W_v", "W_o"],
    "f1a": ["w_router", "w_up", "w_gate", "w_down", "ws_up", "ws_gate", "ws_down"],
    "a1": ["curv", "wq_w", "wq_b", "wk_w", "wk_b", "wv_w", "wv_b", "wo_w", "wo_b"],
}
SUB_WNAMES["f0b"] = SUB_WNAMES["f0a"]
SUB_WNAMES["f1b"] = SUB_WNAMES["f1a"]
EMIT = {"f0a": lambda c: emit_ffn0(c, "L0_ffn_"),
        "f0b": lambda c: emit_ffn0(c, "L0_ffn_"),
        "a0": lambda c: emit_attn0(c, "L0_attn_"),
        "f1a": lambda c: emit_ffn1(c, "L1_ffn_"),
        "f1b": lambda c: emit_ffn1(c, "L1_ffn_"),
        "a1": lambda c: emit_attn1(c, "L1_attn_")}

INPUT_SHAPES = {
    "L0_attn_W_q": [1024, 1024], "L0_attn_W_k": [256, 1024],
    "L0_attn_W_v": [256, 1024], "L0_attn_W_o": [1024, 1024],
    "L0_ffn_router_weight": [8, 1024], "L0_ffn_gate_up_proj": [8, 1024, 1024],
    "L0_ffn_down_proj": [8, 1024, 512], "L0_ffn_shared_gate_w": [2048, 1024],
    "L0_ffn_shared_up_w": [2048, 1024], "L0_ffn_shared_down_w": [1024, 2048],
    "L0_ffn_shared_expert_gate_w": [1, 1024],
    "L1_attn_curv": [], "L1_attn_wq_w": [1024, 1024], "L1_attn_wq_b": [1024],
    "L1_attn_wk_w": [1024, 1024], "L1_attn_wk_b": [1024],
    "L1_attn_wv_w": [1024, 1024], "L1_attn_wv_b": [1024],
    "L1_attn_wo_w": [1024, 1024], "L1_attn_wo_b": [1024],
    "L1_ffn_w_router": [1024, 8], "L1_ffn_w_up": [7, 1024, 4096],
    "L1_ffn_w_gate": [7, 1024, 4096], "L1_ffn_w_down": [7, 4096, 1024],
    "L1_ffn_ws_up": [1024, 4096], "L1_ffn_ws_gate": [1024, 4096],
    "L1_ffn_ws_down": [4096, 1024],
}


def build_group(subs, taps=()):
    nc = bass.Bass("TRN2", target_bir_lowering=False, debug=False)
    p = Prog(nc)
    c = alloc_ctx(nc, p)
    c._taps = list(taps)
    needed = []
    for s in subs:
        for w in SUB_WNAMES[s]:
            full = SUB_PREFIX[s] + w
            if full not in needed:
                needed.append(full)
    for full in needed:
        shp = INPUT_SHAPES[full]
        if full == "L1_attn_curv":
            c.dram[full] = nc.dram_tensor(full, [1], F32, kind="ExternalInput").ap()
        else:
            c.dram[full] = nc.dram_tensor(full, shp, F32, kind="ExternalInput").ap()
    c.dram["x_in"] = nc.dram_tensor("x_in", [T, D], F32, kind="ExternalInput").ap()
    c.dram["x_out"] = nc.dram_tensor("x_out", [T, D], F32, kind="ExternalOutput").ap()
    if any(s == "a0" for s in subs):
        c.dram["rope_cs"] = nc.dram_tensor("rope_cs", [T, 1024], F32,
                                           kind="ExternalInput").ap()
    emit_consts(c)
    emit_prepass_v2(c, subs)
    exl = p.dma("y", lambda y: y.dma_start(
        c.x, c.dram["x_in"].rearrange("(j p) d -> p j d", p=128)))
    c.last_x_ev = exl
    c.fence_ev = None
    for s in subs:
        fence_w = [(k, p.cnt[k]) for k in ("t", "v", "s", "g")]
        c.fence_ev = p.op("g", lambda g: g.nop(), wait=fence_w)
        EMIT[s](c)
    sev = p.dma("y", lambda y: y.dma_start(
        c.dram["x_out"].rearrange("(j p) d -> p j d", p=128), c.x),
        wait=[c.last_x_ev])
    for name, getter in c._taps:
        ap = getter(c)
        shp = list(ap.shape)
        dt_ = ap.dtype
        od = nc.dram_tensor("tap_" + name, shp, dt_, kind="ExternalOutput").ap()
        allw = [(k, p.cnt[k]) for k in ("t", "v", "s", "g")]
        sev = p.dma("y", lambda y, o=od, a=ap: y.dma_start(o, a), wait=allw)
    p.op("y", lambda y: y.nop(), wait=[sev], tag=False)
    p.finalize()
    nc.finalize()
    return nc


GROUPS = [["f0a", "a0", "f0b", "f1a", "a1", "f1b"]]

_ROPE = None


def _rope_table():
    global _ROPE
    if _ROPE is None:
        theta = 1.0 / 10000.0 ** (np.arange(512, dtype=np.float64) / 512.0)
        ang = np.arange(T, dtype=np.float64)[:, None] * theta[None, :]
        _ROPE = np.ascontiguousarray(
            np.concatenate([np.cos(ang), np.sin(ang)], 1).astype(np.float32))
    return _ROPE


_CACHE = {}


def kernel(**inputs):
    X = np.ascontiguousarray(np.asarray(inputs["L0_attn_X"], np.float32))
    xs = [np.ascontiguousarray(X[i]) for i in range(NCORE)]
    for subs in GROUPS:
        key = tuple(subs)
        if key not in _CACHE:
            _CACHE[key] = build_group(subs)
        nc = _CACHE[key]
        needed = []
        for s in subs:
            for w in SUB_WNAMES[s]:
                full = SUB_PREFIX[s] + w
                if full not in needed:
                    needed.append(full)
        base = {}
        for full in needed:
            a = np.ascontiguousarray(np.asarray(inputs[full], np.float32))
            if full == "L1_attn_curv":
                a = a.reshape(1)
            base[full] = a
        if any(s == "a0" for s in subs):
            base["rope_cs"] = _rope_table()
        in_maps = [dict(base, x_in=xs[i]) for i in range(NCORE)]
        res = run_bass_kernel_spmd(nc, in_maps, core_ids=list(range(NCORE)))
        xs = [np.asarray(res.results[i]["x_out"], np.float32) for i in range(NCORE)]
    return np.stack(xs, 0)


# version 07
# speedup vs baseline: 1.2903x; 1.2903x over previous
import numpy as np
import ml_dtypes
import concourse.bass as bass
import concourse.mybir as mybir
from concourse.bass_utils import run_bass_kernel_spmd

F32 = mybir.dt.float32
BF16 = mybir.dt.bfloat16
AF = mybir.ActivationFunctionType
ALU = mybir.AluOpType
AX = mybir.AxisListType

D = 1024
T = 2048
NT = 16
NDC = 8
NCORE = 8

S_MEAN, S_NEGM, S_VAR, S_RSTD = 0, 16, 32, 48
S_CW = 64
S_SEG = 192
S_TMP = 208
S_AT = 256


class Prog:
    def __init__(self, nc):
        self.nc = nc
        self.qs = {k: [] for k in "tvsgy"}
        keys = ["t", "v", "s", "g", "y", "gd", "yd", "sd"]
        self.sems = {k: nc.alloc_semaphore("sem_" + k) for k in keys}
        self.cnt = {k: 0 for k in keys}
        self.waited = {e: {} for e in "tvsgy"}

    def _push(self, eng, fn, wait, inc):
        wl = []
        seen = self.waited[eng]
        for w in wait:
            if w is None:
                continue
            k, v = w
            if seen.get(k, 0) >= v:
                continue
            seen[k] = v
            wl.append((k, v))
        self.qs[eng].append((fn, wl, inc))

    def op(self, eng, fn, wait=(), tag=True):
        ev = None
        inc = None
        if tag:
            self.cnt[eng] += 1
            inc = (eng, 1)
            ev = (eng, self.cnt[eng])
        self._push(eng, fn, wait, inc)
        return ev

    def dma(self, eng, fn, wait=()):
        k = eng + "d"
        self.cnt[k] += 16
        self._push(eng, fn, wait, (k, 16))
        return (k, self.cnt[k])

    def finalize(self):
        nc = self.nc
        sems = self.sems
        for e in "tvsgy":
            for _, wl, _ in self.qs[e]:
                for k, v in wl:
                    assert v <= self.cnt[k], ("wait beyond plan", e, k, v, self.cnt[k])

        def runner(q):
            def f(eng):
                for fn, wl, inc in q:
                    for k, v in wl:
                        eng.wait_ge(sems[k], v)
                    inst = fn(eng)
                    if inc is not None:
                        inst.then_inc(sems[inc[0]], inc[1])
            return f

        with nc.Block() as blk:
            blk.tensor(runner(self.qs["t"]))
            blk.vector(runner(self.qs["v"]))
            blk.scalar(runner(self.qs["s"]))
            blk.gpsimd(runner(self.qs["g"]))
            blk.sync(runner(self.qs["y"]))


class Bank:
    __slots__ = ("ap", "wr", "rd")

    def __init__(self, ap):
        self.ap = ap
        self.wr = None
        self.rd = None


def bank_bf(bank, slot):
    return bank.ap[:, slot * 64:(slot + 1) * 64].bitcast(BF16)


class Ctx:
    pass


def alloc_ctx(nc, p):
    c = Ctx()
    c.nc = nc
    c.p = p
    sb = lambda n, s, d: nc.alloc_sbuf_tensor(n, s, d).ap()
    c.x = sb("x_res", [128, NT, D], F32)
    c.xt = sb("xt_fm", [128, 8 * T], BF16)
    c.pool = sb("pool", [128, 16384], BF16)
    c.pool2 = sb("pool2", [128, 24576], BF16)
    c.scrf = sb("scrf", [128, 3072], F32)
    c.st = sb("stats", [128, 512], F32)
    c.cstf = sb("cstf", [128, 384], F32)
    c.cstb = sb("cstb", [128, 768], BF16)
    c.browq = sb("browq", [1, 1024], BF16)
    c.browk = sb("browk", [1, 1024], BF16)
    c.browv = sb("browv", [1, 1024], BF16)
    c.browo = sb("browo", [1, 1024], BF16)
    c.wr8 = sb("wr8", [128, 80], BF16)
    c.rt8 = sb("rt8", [9, 1024], BF16)
    c.curv = sb("curv", [1, 4], F32)
    c.ccol = sb("ccol", [128, 4], F32)
    c.pa = nc.alloc_psum_tensor("pa", [128, 2048], F32).ap()
    c.pb = nc.alloc_psum_tensor("pb", [128, 2048], F32).ap()
    c.banks = [Bank(c.pa[:, i * 512:(i + 1) * 512]) for i in range(4)] + \
              [Bank(c.pb[:, i * 512:(i + 1) * 512]) for i in range(4)]
    c.stg = sb("stg", [128, 2048], BF16)
    c.stg_free = [None, None]
    c.stg_par = 0
    c.s2ev = {}
    c.dram = {}
    c.wbf = {}
    c.ev_prepass = {}
    c.last_x_ev = None
    c.xt_ev = None
    c.fence_ev = None
    return c


def emit_consts(c):
    p = c.p
    def mk_ident(ap):
        e1 = p.op("g", lambda g, a=ap: g.memset(a, 0.0))
        return p.op("g", lambda g, a=ap: g.affine_select(
            out=a, in_=a, compare_op=ALU.not_equal, fill=1.0, base=0,
            pattern=[[-1, 128]], channel_multiplier=1), wait=[e1])
    mk_ident(c.cstf[:, 0:128])
    e1 = p.op("g", lambda g: g.memset(c.cstf[:, 128:256], 0.0))
    p.op("g", lambda g: g.affine_select(
        out=c.cstf[:, 128:256], in_=c.cstf[:, 128:256], compare_op=ALU.is_ge,
        fill=-1e9, base=0, pattern=[[-1, 128]], channel_multiplier=1), wait=[e1])
    e2 = p.op("g", lambda g: g.memset(c.cstf[:, 256:384], 0.0))
    p.op("g", lambda g: g.affine_select(
        out=c.cstf[:, 256:384], in_=c.cstf[:, 256:384], compare_op=ALU.is_ge,
        fill=-1e9, base=0, pattern=[[1, 128]], channel_multiplier=-1), wait=[e2])
    mk_ident(c.cstb[:, 0:128])
    ev = p.op("g", lambda g: g.memset(c.cstb[:, 128:640], 1.0))
    c.ev_consts = ev
    c.identf = c.cstf[:, 0:128]
    c.identb = c.cstb[:, 0:128]
    c.maskQ = c.cstf[:, 128:256]
    c.maskT = c.cstf[:, 256:384]
    c.ones = c.cstb[:, 128:640]


def emit_cast(c, full_names):
    p = c.p
    for n in full_names:
        if n in c.wbf:
            continue
        src = c.dram[n]
        shp = list(src.shape)
        if len(shp) == 1:
            shp = [1, shp[0]]
            src = src.rearrange("(a d) -> a d", a=1)
        dst = c.nc.dram_tensor("bf_" + n, shp, BF16).ap()
        if len(shp) == 3:
            evs = []
            for e in range(shp[0]):
                evs.append(p.dma("g", lambda g, a=dst[e], b=src[e]: g.dma_start(a, b)))
                c.ev_prepass[(n, e)] = evs[-1]
            c.ev_prepass[n] = evs[-1]
        else:
            c.ev_prepass[n] = p.dma("g", lambda g, a=dst, b=src: g.dma_start(a, b))
        c.wbf[n] = dst


def emit_rearrange_cast(c, key, src, dst, ev=None):
    c.s2ev[key] = c.p.dma("g", lambda g, a=dst, b=src: g.dma_start(a, b),
                          wait=[ev])


def emit_pre_transpose(c, key, pieces):
    p = c.p
    ev = None
    for src, dst, wev in pieces:
        b = c.stg_par % 2
        R = src.shape[0]
        stga = c.stg[:, b * 1024: b * 1024 + R]
        e1 = p.dma("y", lambda s, a=stga, b_=src: s.dma_start_transpose(a, b_),
                   wait=[wev, c.stg_free[b]])
        e2 = p.dma("y", lambda s, a=dst, b_=stga: s.dma_start(a, b_), wait=[e1])
        c.stg_free[b] = e2
        c.stg_par += 1
        ev = e2
    c.s2ev[key] = ev


def emit_prepass_casts(c, subs):
    done = set(subs)
    if "f0a" in done or "f0b" in done:
        px = "L0_ffn_"
        emit_cast(c, [px + n for n in ("router_weight", "shared_expert_gate_w",
                                       "gate_up_proj", "down_proj",
                                       "shared_gate_w", "shared_up_w",
                                       "shared_down_w")])
    if "a0" in done:
        emit_cast(c, ["L0_attn_" + n for n in ("W_q", "W_k", "W_v", "W_o")])
    if "f1a" in done or "f1b" in done:
        px = "L1_ffn_"
        emit_cast(c, [px + "w_router"])
        nc = c.nc
        c.s2gug = nc.dram_tensor("s2gug", [8, 16, 128, 2, 8, 256], BF16).ap()
        c.s2wd = nc.dram_tensor("s2wd", [8, 4, 128, 8, 1024], BF16).ap()
        for ei in range(8):
            if ei < 7:
                usrc = c.dram[px + "w_up"][ei]
                gsrc = c.dram[px + "w_gate"][ei]
                dsrc = c.dram[px + "w_down"][ei]
            else:
                usrc = c.dram[px + "ws_up"]
                gsrc = c.dram[px + "ws_gate"]
                dsrc = c.dram[px + "ws_down"]
            for hp in range(16):
                emit_rearrange_cast(c, ("gug", ei), usrc[:, hp * 256:(hp + 1) * 256]
                                    .rearrange("(cc p) h -> p cc h", p=128),
                                    c.s2gug[ei, hp][:, 0])
                emit_rearrange_cast(c, ("gug", ei), gsrc[:, hp * 256:(hp + 1) * 256]
                                    .rearrange("(cc p) h -> p cc h", p=128),
                                    c.s2gug[ei, hp][:, 1])
            for q in range(4):
                emit_rearrange_cast(c, ("wd", ei), dsrc[q * 1024:(q + 1) * 1024, :]
                                    .rearrange("(hh p) o -> p hh o", p=128),
                                    c.s2wd[ei, q])
    if "a1" in done:
        emit_cast(c, ["L1_attn_" + n for n in ("wq_w", "wk_w", "wv_w", "wo_w",
                                               "wq_b", "wk_b", "wv_b", "wo_b")])


def emit_prepass_family(c, s):
    nc = c.nc
    dram = lambda n, sh: nc.dram_tensor(n, sh, BF16).ap()
    if s in ("f0a", "f0b"):
        px = "L0_ffn_"
        c.s2gu0 = dram("s2gu0", [8, 128, 8, 1024])
        c.s2dn0 = dram("s2dn0", [8, 128, 4, 1024])
        c.s2sg0 = dram("s2sg0", [2, 4, 128, 8, 512])
        c.s2sd0 = dram("s2sd0", [128, 16, 1024])
        gu_bf = c.wbf[px + "gate_up_proj"]
        dn_bf = c.wbf[px + "down_proj"]
        for e in range(8):
            emit_pre_transpose(c, ("gu0", e), [
                (gu_bf[e][:, dc * 128:(dc + 1) * 128], c.s2gu0[e][:, dc, :],
                 c.ev_prepass[(px + "gate_up_proj", e)]) for dc in range(8)])
            emit_pre_transpose(c, ("dn0", e), [
                (dn_bf[e][:, ic * 128:(ic + 1) * 128], c.s2dn0[e][:, ic, :],
                 c.ev_prepass[(px + "down_proj", e)]) for ic in range(4)])
        for part, nm in ((0, "shared_gate_w"), (1, "shared_up_w")):
            for grp in range(4):
                emit_pre_transpose(c, ("sg0", part, grp), [
                    (c.wbf[px + nm][grp * 512:(grp + 1) * 512,
                                    dc * 128:(dc + 1) * 128],
                     c.s2sg0[part, grp][:, dc, :], c.ev_prepass[px + nm])
                    for dc in range(8)])
        emit_pre_transpose(c, "sd0", [
            (c.wbf[px + "shared_down_w"][:, ic * 128:(ic + 1) * 128],
             c.s2sd0[:, ic, :], c.ev_prepass[px + "shared_down_w"])
            for ic in range(16)])
    elif s == "a0":
        px = "L0_attn_"
        c.s2l0q = dram("s2l0q", [128, 8, 1024])
        c.s2l0k = dram("s2l0k", [128, 8, 256])
        c.s2l0v = dram("s2l0v", [128, 8, 256])
        c.s2l0o = dram("s2l0o", [128, 8, 1024])
        for key, nm, scr in (("l0k", "W_k", c.s2l0k), ("l0v", "W_v", c.s2l0v),
                             ("l0q", "W_q", c.s2l0q), ("l0o", "W_o", c.s2l0o)):
            emit_pre_transpose(c, key, [
                (c.wbf[px + nm][:, dc * 128:(dc + 1) * 128], scr[:, dc, :],
                 c.ev_prepass[px + nm]) for dc in range(8)])
    elif s in ("f1a", "f1b"):
        pass
    elif s == "a1":
        px = "L1_attn_"
        c.s2l1q = dram("s2l1q", [8, 128, 8, 128])
        c.s2l1k = dram("s2l1k", [8, 128, 8, 128])
        c.s2l1v = dram("s2l1v", [8, 128, 8, 128])
        c.s2l1o = dram("s2l1o", [128, 8, 1024])
        for key, nm, scr in (("l1q", "wq_w", c.s2l1q), ("l1k", "wk_w", c.s2l1k),
                             ("l1v", "wv_w", c.s2l1v)):
            for cc in range(8):
                emit_pre_transpose(c, (key, cc), [
                    (c.wbf[px + nm][cc * 128:(cc + 1) * 128,
                                    dc * 128:(dc + 1) * 128],
                     scr[cc][:, dc, :], c.ev_prepass[px + nm]) for dc in range(8)])
        emit_pre_transpose(c, "l1o", [
            (c.wbf[px + "wo_w"][:, dc * 128:(dc + 1) * 128], c.s2l1o[:, dc, :],
             c.ev_prepass[px + "wo_w"]) for dc in range(8)])



def emit_tok_to_feat(c, j, src_bf, bank, wait=()):
    p = c.p
    ev = None
    for half in range(2):
        tw = [bank.rd] + list(wait)
        for i in range(4):
            dc = half * 4 + i
            tev = p.op("t", lambda t, d=bank_bf(bank, i),
                       s=src_bf[:, dc * 128:(dc + 1) * 128]:
                       t.transpose(d, s, c.identb), wait=tw, tag=(i == 3))
        bank.wr = tev
        for i in range(4):
            dc = half * 4 + i
            eev = p.op("s", lambda s, d=c.xt[:, dc * T + j * 128: dc * T + (j + 1) * 128],
                       sp=bank_bf(bank, i): s.copy(d, sp),
                       wait=[bank.wr], tag=(i == 3))
        bank.rd = eev
        ev = eev
    return ev


def emit_ln_all(c, per_tile=None):
    p = c.p
    st = c.st
    mean_all = st[:, S_MEAN:S_MEAN + 16]
    negm_all = st[:, S_NEGM:S_NEGM + 16]
    var_all = st[:, S_VAR:S_VAR + 16]
    rstd_all = st[:, S_RSTD:S_RSTD + 16]
    scr = c.scrf[:, 0:1024]
    red = None
    for j in range(NT):
        red = p.op("v", lambda v, jj=j: v.reduce_sum(
            mean_all[:, jj:jj + 1], c.x[:, jj, :], AX.X), wait=[c.last_x_ev])
    em = p.op("v", lambda v: v.tensor_scalar_mul(mean_all, mean_all, 1.0 / 1024.0),
              wait=[red])
    en = p.op("s", lambda s: s.mul(negm_all, mean_all, -1.0), wait=[em])
    sq = None
    for j in range(NT):
        sq = p.op("s", lambda s, jj=j: s.activation(
            scr, c.x[:, jj, :], AF.Square, bias=negm_all[:, jj:jj + 1], scale=1.0,
            accum_out=var_all[:, jj:jj + 1]), wait=[en])
    ev1 = p.op("v", lambda v: v.tensor_scalar(out=rstd_all, in0=var_all,
                                              scalar1=1.0 / 1024.0, scalar2=1e-5,
                                              op0=ALU.mult, op1=ALU.add), wait=[sq])
    ev2 = p.op("v", lambda v: v.reciprocal(rstd_all, rstd_all), wait=[ev1])
    ev3 = p.op("s", lambda s: s.activation(rstd_all, rstd_all, AF.Sqrt), wait=[ev2])
    tok_bf = c.pool[:, 0:1024]
    lnf = c.scrf[:, 1024:2048]
    lnev = None
    for j in range(NT):
        dst = lnf if per_tile is not None else tok_bf
        e = p.op("v", lambda v, jj=j, d=dst: v.tensor_scalar(
            out=d, in0=c.x[:, jj, :], scalar1=mean_all[:, jj:jj + 1],
            scalar2=rstd_all[:, jj:jj + 1], op0=ALU.subtract, op1=ALU.mult),
            wait=[ev3, lnev, c.fence_ev, em])
        if per_tile is not None:
            e = per_tile(j, lnf, tok_bf, e)
        lnev = emit_tok_to_feat(c, j, tok_bf, c.banks[7], wait=[e])
    c.xt_ev = lnev


def xt_view(c, dc, t0, t1):
    return c.xt[:, dc * T + t0: dc * T + t1]



def emit_router_l1(c, px):
    p = c.p
    lw = p.dma("y", lambda y: y.dma_start(
        c.wr8[:, 0:64].rearrange("p (cc e) -> p cc e", e=8),
        c.wbf[px + "w_router"].rearrange("(cc p) e -> p cc e", p=128)),
        wait=[c.ev_prepass[px + "w_router"], c.fence_ev])
    bank = c.banks[0]
    cw = c.st[:, S_CW:S_CW + 128].rearrange("p (j e) -> p j e", e=8)
    ev = None
    prev_read = None
    for j in range(NT):
        for dc in range(NDC):
            evm = p.op("t", lambda t, d=dc, jj=j: t.matmul(
                bank.ap[:, 0:8], xt_view(c, d, jj * 128, (jj + 1) * 128),
                c.wr8[:, d * 8:(d + 1) * 8], start=(d == 0), stop=(d == NDC - 1)),
                wait=[bank.rd, lw, c.xt_ev, prev_read], tag=(dc == NDC - 1))
        bank.wr = evm
        lg = bank.ap[:, 0:8]
        mcol = c.st[:, S_TMP:S_TMP + 1]
        sumc = c.st[:, S_TMP + 1:S_TMP + 2]
        m1 = c.st[:, S_TMP + 2:S_TMP + 3]
        escr = c.scrf[:, 0:8]
        e1 = p.op("v", lambda v: v.tensor_reduce(mcol, lg, AX.X, ALU.max),
                  wait=[bank.wr, prev_read])
        e2 = p.op("s", lambda s: s.mul(mcol, mcol, -1.0), wait=[e1])
        e3 = p.op("s", lambda s: s.activation(escr, lg, AF.Exp, bias=mcol, scale=1.0,
                                              accum_out=sumc), wait=[e2, prev_read])
        bank.rd = e3
        e4 = p.op("v", lambda v: v.tensor_reduce(m1, escr, AX.X, ALU.max), wait=[e3])
        e5 = p.op("v", lambda v: v.reciprocal(sumc, sumc), wait=[e3, e4])
        e6 = p.op("v", lambda v, jj=j: v.tensor_scalar(
            out=cw[:, jj, :], in0=escr, scalar1=m1, scalar2=sumc,
            op0=ALU.is_ge, op1=ALU.mult), wait=[e5, e4])
        prev_read = e6
        ev = e6
    return ev


def emit_router_l0(c, px):
    p = c.p
    lw1 = p.dma("y", lambda y: y.dma_start(c.rt8[0:8, :], c.wbf[px + "router_weight"]),
                wait=[c.ev_prepass[px + "router_weight"], c.fence_ev])
    lw2 = p.dma("y", lambda y: y.dma_start(c.rt8[8:9, :],
                                           c.wbf[px + "shared_expert_gate_w"]),
                wait=[c.ev_prepass[px + "shared_expert_gate_w"], c.fence_ev])
    bank = c.banks[0]
    wr9 = c.wr8.rearrange("p (cc e) -> p cc e", e=10)
    for dcg in range(2):
        for i in range(4):
            dc = dcg * 4 + i
            tev = p.op("t", lambda t, d=bank_bf(bank, i)[:, 0:9],
                       s=c.rt8[0:9, dc * 128:(dc + 1) * 128]:
                       t.transpose(d, s, c.identb[0:9, 0:9]),
                       wait=[bank.rd, lw1, lw2], tag=(i == 3))
        bank.wr = tev
        for i in range(4):
            dc = dcg * 4 + i
            eev = p.op("s", lambda s, d=wr9[:, dc, 0:9],
                       sp=bank_bf(bank, i)[:, 0:9]: s.copy(d, sp),
                       wait=[bank.wr], tag=(i == 3))
        bank.rd = eev
    lwr = bank.rd
    cw = c.st[:, S_CW:S_CW + 128].rearrange("p (j e) -> p j e", e=8)
    seg = c.st[:, S_SEG:S_SEG + 16]
    ev = None
    prev_read = None
    for j in range(NT):
        for dc in range(NDC):
            evm = p.op("t", lambda t, d=dc, jj=j: t.matmul(
                bank.ap[:, 0:9], xt_view(c, d, jj * 128, (jj + 1) * 128),
                wr9[:, d, 0:9], start=(d == 0), stop=(d == NDC - 1)),
                wait=[bank.rd, lwr, c.xt_ev], tag=(dc == NDC - 1))
        bank.wr = evm
        e8 = p.op("s", lambda s, jj=j: s.activation(
            seg[:, jj:jj + 1], bank.ap[:, 8:9], AF.Sigmoid, scale=1.0),
            wait=[bank.wr])
        e8b = p.op("s", lambda s, jj=j: s.mul(seg[:, jj:jj + 1], seg[:, jj:jj + 1], 0.5),
                   wait=[e8])
        lg = bank.ap[:, 0:8]
        mcol = c.st[:, S_TMP:S_TMP + 1]
        den = c.st[:, S_TMP + 1:S_TMP + 2]
        escr = c.scrf[:, 0:8]
        s8 = c.scrf[:, 8:16]
        keep = c.scrf[:, 16:24]
        e1 = p.op("v", lambda v: v.tensor_reduce(mcol, lg, AX.X, ALU.max),
                  wait=[bank.wr, prev_read])
        e2 = p.op("s", lambda s: s.mul(mcol, mcol, -1.0), wait=[e1])
        e3 = p.op("s", lambda s: s.activation(escr, lg, AF.Exp, bias=mcol, scale=1.0),
                  wait=[e2, prev_read])
        bank.rd = e3
        e4 = p.op("v", lambda v: v.max(s8, escr), wait=[e3])
        e5 = p.op("v", lambda v: v.tensor_tensor(den, s8[:, 0:1], s8[:, 1:2], ALU.add),
                  wait=[e4])
        e6 = p.op("v", lambda v: v.reciprocal(den, den), wait=[e5])
        e7 = p.op("v", lambda v: v.tensor_single_scalar(keep, escr, s8[:, 1:2],
                                                        ALU.is_ge), wait=[e4, e3])
        e9 = p.op("v", lambda v: v.tensor_tensor(keep, keep, escr, ALU.mult), wait=[e7])
        e10 = p.op("v", lambda v, jj=j: v.tensor_scalar(
            out=cw[:, jj, :], in0=keep, scalar1=den, scalar2=0.5,
            op0=ALU.mult, op1=ALU.mult), wait=[e9, e6])
        prev_read = e10
        ev = e10
    return ev



def emit_ffn1(c, px):
    p = c.p
    emit_ln_all(c)
    rt_ev = emit_router_l1(c, px)

    wu = c.pool2[:, 0:8192].rearrange("p (b ug cc h) -> p b ug cc h", b=2, ug=2,
                                      cc=8)
    wd = c.pool2[:, 8192:24576].rearrange("p (b hh o) -> p b hh o", b=2, hh=8)
    wu_ev = [None, None]
    wu_free = [c.fence_ev, c.fence_ev]
    wd_ev = [None, None]
    wd_free = [c.fence_ev, c.fence_ev]
    h_t = c.pool.rearrange("p (hh tt) -> p hh tt", hh=32)
    xev = {}
    prev_sil_read = [None, None]

    experts = list(range(7)) + ["sh"]
    for tg in range(4):
        t0 = tg * 512
        for e in experts:
            ei = 7 if e == "sh" else e
            evug = c.s2ev[("gug", ei)]
            evwd = c.s2ev[("wd", ei)]
            h_done_q = {}
            for hp in range(16):
                b = hp % 2
                wu_ev[b] = p.dma("y", lambda y, e_=ei, hp_=hp, b_=b: y.dma_start(
                    wu[:, b_], c.s2gug[e_, hp_]),
                    wait=[wu_free[b], evug])
                for hi in range(2):
                    hh = hp * 2 + hi
                    bu = c.banks[hh % 2]
                    bg = c.banks[2 + hh % 2]
                    sil = c.scrf[:, 1024 + 512 * (hh % 2): 1536 + 512 * (hh % 2)]
                    for dc in range(NDC):
                        evu = p.op("t", lambda t, d=dc, b_=b, hi_=hi, bk=bu,
                                   t0_=t0: t.matmul(
                            bk.ap, wu[:, b_, 0, d, hi_ * 128:(hi_ + 1) * 128],
                            xt_view(c, d, t0_, t0_ + 512),
                            start=(d == 0), stop=(d == 7)),
                            wait=[bu.rd, wu_ev[b], c.xt_ev], tag=(dc == 7))
                    bu.wr = evu
                    for dc in range(NDC):
                        evg = p.op("t", lambda t, d=dc, b_=b, hi_=hi, bk=bg,
                                   t0_=t0: t.matmul(
                            bk.ap, wu[:, b_, 1, d, hi_ * 128:(hi_ + 1) * 128],
                            xt_view(c, d, t0_, t0_ + 512),
                            start=(d == 0), stop=(d == 7)),
                            wait=[bg.rd, wu_ev[b], c.xt_ev], tag=(dc == 7))
                    bg.wr = evg
                    es = p.op("s", lambda s, bk=bg, sl=sil: s.activation(
                        sl, bk.ap, AF.Silu), wait=[bg.wr, prev_sil_read[hh % 2]])
                    bg.rd = es
                    eh = p.op("v", lambda v, hh_=hh, bk=bu, sl=sil: v.tensor_tensor(
                        h_t[:, hh_, :], bk.ap, sl, ALU.mult), wait=[es, bu.wr])
                    bu.rd = eh
                    prev_sil_read[hh % 2] = eh
                    if hi == 1:
                        wu_free[b] = eh
                    if hh % 8 == 7:
                        h_done_q[hh // 8] = eh
            for q in range(4):
                b = q % 2
                wd_ev[b] = p.dma("y", lambda y, e_=ei, q_=q, b_=b: y.dma_start(
                    wd[:, b_], c.s2wd[e_, q_]),
                    wait=[wd_free[b], evwd])
                ev_last = None
                for i2 in range(2):
                    for o in range(2):
                        for half in range(2):
                            i = half * 2 + i2
                            bk = c.banks[4 + i2 * 2 + o]
                            for hh8 in range(8):
                                hh = q * 8 + hh8
                                ev_last = p.op("t", lambda t, hh_=hh, hh8_=hh8, i_=i,
                                               o_=o, b_=b, bk_=bk: t.matmul(
                                    bk_.ap, h_t[:, hh_, i_ * 128:(i_ + 1) * 128],
                                    wd[:, b_, hh8_, o_ * 512:(o_ + 1) * 512],
                                    start=(hh8_ == 0), stop=(hh8_ == 7)),
                                    wait=([bk.rd, wd_ev[b], h_done_q[q]]
                                          if hh8 == 0 else []),
                                    tag=(hh8 == 7))
                            bk.wr = ev_last
                            j = tg * 4 + i
                            sc = (0.5 if e == "sh" else
                                  c.st[:, S_CW + j * 8 + e:S_CW + j * 8 + e + 1])
                            evx = p.op("v", lambda v, jj=j, o_=o, bk_=bk, sc_=sc:
                                       v.scalar_tensor_tensor(
                                           out=c.x[:, jj, o_ * 512:(o_ + 1) * 512],
                                           in0=bk_.ap, scalar=sc_,
                                           in1=c.x[:, jj, o_ * 512:(o_ + 1) * 512],
                                           op0=ALU.mult, op1=ALU.add),
                                       wait=[bk.wr, rt_ev, xev.get((j, o))])
                            bk.rd = evx
                            xev[(j, o)] = evx
                wd_free[b] = ev_last
    c.last_x_ev = ("v", c.p.cnt["v"])



def emit_ffn0(c, px, parts=("experts", "shared"), tgs=(0, 1, 2, 3)):
    p = c.p
    emit_ln_all(c)
    rt_ev = emit_router_l0(c, px)
    cw = c.st[:, S_CW:S_CW + 128].rearrange("p (j e) -> p j e", e=8)
    seg = c.st[:, S_SEG:S_SEG + 16]

    gu = c.pool2[:, 0:16384].rearrange("p (b cc o) -> p b cc o", b=2, cc=8)
    dn = c.pool2[:, 16384:24576].rearrange("p (b cc o) -> p b cc o", b=2, cc=4)
    gu_ev = [None, None]; dn_ev = [None, None]
    gu_free = [c.fence_ev, c.fence_ev]; dn_free = [c.fence_ev, c.fence_ev]
    h_t = c.pool.rearrange("p (hh tt) -> p hh tt", hh=32)
    xev = {}
    prev_h_read = None
    sd_free_ev = c.fence_ev

    for tg in tgs:
        t0 = tg * 512
        for e in (range(8) if "experts" in parts else []):
            b = e % 2
            gu_ev[b] = p.dma("y", lambda y, e_=e, b_=b: y.dma_start(
                gu[:, b_], c.s2gu0[e_]),
                wait=[gu_free[b], c.s2ev[("gu0", e)]])
            dn_ev[b] = p.dma("y", lambda y, e_=e, b_=b: y.dma_start(
                dn[:, b_], c.s2dn0[e_]),
                wait=[dn_free[b], c.s2ev[("dn0", e)]])
            sil_ev = None
            for oc in range(8):
                bk = c.banks[oc % 2]
                for dc in range(NDC):
                    evm = p.op("t", lambda t, d=dc, o_=oc, b_=b, bk_=bk, t0_=t0:
                               t.matmul(
                        bk_.ap, gu[:, b_, d, o_ * 128:(o_ + 1) * 128],
                        xt_view(c, d, t0_, t0_ + 512),
                        start=(d == 0), stop=(d == 7)),
                        wait=[bk.rd, gu_ev[b], c.xt_ev], tag=(dc == 7))
                bk.wr = evm
                if oc < 4:
                    es = p.op("s", lambda s, o_=oc, bk_=bk: s.activation(
                        h_t[:, 16 + o_, :], bk_.ap, AF.Silu),
                        wait=[bk.wr, prev_h_read])
                    bk.rd = es
                    sil_ev = es
                else:
                    eh = p.op("v", lambda v, o_=oc, bk_=bk: v.tensor_tensor(
                        h_t[:, o_ - 4, :], bk_.ap, h_t[:, 16 + o_ - 4, :], ALU.mult),
                        wait=[bk.wr, sil_ev])
                    bk.rd = eh
            h_done = eh
            gu_free[b] = eh
            ev_last = None
            for i in range(4):
                jj = tg * 4 + i
                for o in range(2):
                    bk = c.banks[4 + (i % 2) * 2 + o]
                    for ic in range(4):
                        ev_last = p.op("t", lambda t, i_=i, o_=o, ic_=ic, b_=b, bk_=bk:
                                       t.matmul(
                            bk_.ap, h_t[:, ic_, i_ * 128:(i_ + 1) * 128],
                            dn[:, b_, ic_, o_ * 512:(o_ + 1) * 512],
                            start=(ic_ == 0), stop=(ic_ == 3)),
                            wait=[bk.rd, dn_ev[b], h_done], tag=(ic == 3))
                    bk.wr = ev_last
                    evx = p.op("v", lambda v, jj_=jj, o_=o, bk_=bk, e_=e:
                               v.scalar_tensor_tensor(
                                   out=c.x[:, jj_, o_ * 512:(o_ + 1) * 512],
                                   in0=bk_.ap, scalar=c.st[:, S_CW + jj_ * 8 + e_:S_CW + jj_ * 8 + e_ + 1],
                                   in1=c.x[:, jj_, o_ * 512:(o_ + 1) * 512],
                                   op0=ALU.mult, op1=ALU.add),
                               wait=[bk.wr, rt_ev, xev.get((jj, o))])
                    bk.rd = evx
                    xev[(jj, o)] = evx
            dn_free[b] = ev_last
            prev_h_read = ev_last
        if "shared" not in parts:
            continue
        shb = c.pool2[:, 0:8192].rearrange("p (b cc o) -> p b cc o", b=2, cc=8)
        sh_ev = [gu_ev[0], gu_ev[1]]
        sh_free = [gu_free[0], gu_free[1]]
        sil_done = {}
        for part, nm in ((0, "shared_gate_w"), (1, "shared_up_w")):
            for grp in range(4):
                b = grp % 2
                sh_ev[b] = p.dma("y", lambda y, b_=b, pt=part, g_=grp: y.dma_start(
                    shb[:, b_], c.s2sg0[pt, g_]),
                    wait=[sh_free[b], c.s2ev[("sg0", part, grp)]])
                for oc in range(4):
                    hc = grp * 4 + oc
                    bk = c.banks[oc % 2]
                    for dc in range(NDC):
                        evm = p.op("t", lambda t, d=dc, o_=oc, b_=b, bk_=bk, t0_=t0:
                                   t.matmul(
                            bk_.ap, shb[:, b_, d, o_ * 128:(o_ + 1) * 128],
                            xt_view(c, d, t0_, t0_ + 512),
                            start=(d == 0), stop=(d == 7)),
                            wait=[bk.rd, sh_ev[b], c.xt_ev], tag=(dc == 7))
                    bk.wr = evm
                    if part == 0:
                        es = p.op("s", lambda s, h_=hc, bk_=bk: s.activation(
                            h_t[:, 16 + h_, :], bk_.ap, AF.Silu),
                            wait=[bk.wr, prev_h_read])
                        bk.rd = es
                        sil_done[hc] = es
                    else:
                        eh = p.op("v", lambda v, h_=hc, bk_=bk: v.tensor_tensor(
                            h_t[:, h_, :], bk_.ap, h_t[:, 16 + h_, :], ALU.mult),
                            wait=[bk.wr, sil_done[hc], prev_h_read])
                        bk.rd = eh
                sh_free[b] = bk.rd
        h_done = bk.rd
        sdb = c.pool2[:, 8192:24576].rearrange("p (cc o) -> p cc o", cc=16)
        sd_ld = p.dma("y", lambda y: y.dma_start(sdb, c.s2sd0),
                      wait=[sd_free_ev, gu_free[0], gu_free[1], dn_free[0],
                            dn_free[1], c.s2ev["sd0"]])
        ev_last = None
        for i in range(4):
            jj = tg * 4 + i
            for o in range(2):
                bk = c.banks[4 + (i % 2) * 2 + o]
                for ic in range(16):
                    ev_last = p.op("t", lambda t, ic_=ic, i_=i, o_=o, bk_=bk:
                                   t.matmul(
                        bk_.ap, h_t[:, ic_, i_ * 128:(i_ + 1) * 128],
                        sdb[:, ic_, o_ * 512:(o_ + 1) * 512],
                        start=(ic_ == 0), stop=(ic_ == 15)),
                        wait=([bk.rd, sd_ld, h_done] if ic == 0 else []),
                        tag=(ic == 15))
                bk.wr = ev_last
                evx = p.op("v", lambda v, jj_=jj, o_=o, bk_=bk: v.scalar_tensor_tensor(
                    out=c.x[:, jj_, o_ * 512:(o_ + 1) * 512],
                    in0=bk_.ap, scalar=seg[:, jj_:jj_ + 1],
                    in1=c.x[:, jj_, o_ * 512:(o_ + 1) * 512],
                    op0=ALU.mult, op1=ALU.add),
                    wait=[bk.wr, rt_ev, xev.get((jj, o))])
                bk.rd = evx
                xev[(jj, o)] = evx
        sd_free_ev = ev_last
        prev_h_read = ev_last
        gu_free = [ev_last, ev_last]
        dn_free = [ev_last, ev_last]
    c.last_x_ev = ("v", c.p.cnt["v"])




def attn_core(c, layer, heads_iter, ka_of, vtok_of, rinv_sl, aowrite):
    raise NotImplementedError


def emit_attn0(c, px):
    p = c.p
    ropebuf = c.scrf[:, 2048:3072]
    tmp1 = c.scrf[:, 0:512]
    tmp2 = c.scrf[:, 512:1024]
    rope_evs = {}

    def rope_tile(j, lnf, tok_bf, ev):
        ld = p.dma("y", lambda y, jj=j: y.dma_start(
            ropebuf, c.dram["rope_cs"][jj * 128:(jj + 1) * 128, :]),
            wait=[rope_evs.get("last")])
        xe = lnf[:, 0:1024:2]
        xo = lnf[:, 1:1024:2]
        cosv = ropebuf[:, 0:512]
        sinv = ropebuf[:, 512:1024]
        e1 = p.op("v", lambda v: v.tensor_tensor(tmp1, xe, cosv, ALU.mult), wait=[ev, ld])
        e2 = p.op("v", lambda v: v.tensor_tensor(tmp2, xo, sinv, ALU.mult), wait=[ev, ld])
        e3 = p.op("v", lambda v: v.tensor_tensor(tok_bf[:, 0:512], tmp1, tmp2,
                                                 ALU.subtract), wait=[e1, e2])
        e4 = p.op("v", lambda v: v.tensor_tensor(tmp1, xe, sinv, ALU.mult), wait=[e3])
        e5 = p.op("v", lambda v: v.tensor_tensor(tmp2, xo, cosv, ALU.mult), wait=[e3])
        e6 = p.op("v", lambda v: v.tensor_tensor(tok_bf[:, 512:1024], tmp1, tmp2,
                                                 ALU.add), wait=[e4, e5])
        rope_evs["last"] = e6
        return e6

    emit_ln_all(c, per_tile=rope_tile)

    ka = [c.pool2[0:65, g * 2048:(g + 1) * 2048] for g in range(4)]
    v_tok = c.pool2[:, 8192:12288].rearrange("p (j o) -> p j o", o=256)
    wkT = c.pool2[:, 12288:14336].rearrange("p (cc o) -> p cc o", cc=8)
    wvT = c.pool2[:, 14336:16384].rearrange("p (cc o) -> p cc o", cc=8)
    wqT = c.pool2[:, 16384:24576].rearrange("p (cc o) -> p cc o", cc=8)
    wob = wqT
    q_t = c.pool.rearrange("p (cc tt) -> p cc tt", cc=8)

    wq = c.wbf[px + "W_q"]; wk = c.wbf[px + "W_k"]
    wv = c.wbf[px + "W_v"]; wo = c.wbf[px + "W_o"]
    evw = {k: c.ev_prepass[px + k] for k in ("W_q", "W_k", "W_v", "W_o")}

    e1 = p.dma("y", lambda y: y.dma_start(
        wkT.rearrange("p cc o -> p (cc o)"), c.s2l0k.rearrange("p a b -> p (a b)")),
        wait=[c.s2ev["l0k"], c.fence_ev])
    e2 = p.dma("y", lambda y: y.dma_start(
        wvT.rearrange("p cc o -> p (cc o)"), c.s2l0v.rearrange("p a b -> p (a b)")),
        wait=[c.s2ev["l0v"], c.fence_ev])
    kv_ld = e2
    wq_ld = p.dma("y", lambda y: y.dma_start(
        wqT.rearrange("p cc o -> p (cc o)"), c.s2l0q.rearrange("p a b -> p (a b)")),
        wait=[c.s2ev["l0q"], c.fence_ev])

    for tg in range(4):
        t0 = tg * 512
        for kc in range(2):
            bk = c.banks[kc]
            for dc in range(NDC):
                evm = p.op("t", lambda t, d=dc, k_=kc, t0_=t0, bk_=bk: t.matmul(
                    bk_.ap, wkT[:, d, k_ * 128:(k_ + 1) * 128],
                    xt_view(c, d, t0_, t0_ + 512), start=(d == 0), stop=(d == 7)),
                    wait=[bk.rd, kv_ld, c.xt_ev], tag=(dc == 7))
            bk.wr = evm
            e1 = p.op("s", lambda s, k_=kc, t0_=t0, bk_=bk: s.copy(
                ka[2 * k_][0:64, t0_:t0_ + 512], bk_.ap[0:64, :]), wait=[bk.wr])
            e2 = p.op("s", lambda s, k_=kc, t0_=t0, bk_=bk: s.copy(
                ka[2 * k_ + 1][0:64, t0_:t0_ + 512], bk_.ap[64:128, :]), wait=[bk.wr])
            bk.rd = e2
    k_done = e2
    mevs = [p.op("g", lambda gg, g_=g: gg.memset(ka[g_][64:65, :], -1.0),
                 wait=[c.fence_ev]) for g in range(4)]
    ka_neg1 = mevs[-1]

    vev = None
    for j in range(NT):
        bk = c.banks[2 + j % 2]
        for dc in range(NDC):
            evm = p.op("t", lambda t, d=dc, jj=j, bk_=bk: t.matmul(
                bk_.ap[:, 0:256], xt_view(c, d, jj * 128, (jj + 1) * 128),
                wvT[:, d, :], start=(d == 0), stop=(d == 7)),
                wait=[bk.rd, kv_ld, c.xt_ev], tag=(dc == 7))
        bk.wr = evm
        vev = p.op("s", lambda s, jj=j, bk_=bk: s.copy(v_tok[:, jj, :],
                                                       bk_.ap[:, 0:256]),
                   wait=[bk.wr])
        bk.rd = vev
    v_done = vev

    qev = None
    for oc in range(NDC):
        for tg in range(4):
            t0 = tg * 512
            bk = c.banks[4 + tg % 2]
            for dc in range(NDC):
                evm = p.op("t", lambda t, d=dc, o_=oc, t0_=t0, bk_=bk: t.matmul(
                    bk_.ap, wqT[:, d, o_ * 128:(o_ + 1) * 128],
                    xt_view(c, d, t0_, t0_ + 512), start=(d == 0), stop=(d == 7)),
                    wait=[bk.rd, wq_ld, c.xt_ev], tag=(dc == 7))
            bk.wr = evm
            ev = p.op("s", lambda s, o_=oc, t0_=t0, bk_=bk: s.copy(
                q_t[:, o_, t0_:t0_ + 512], bk_.ap), wait=[bk.wr])
            bk.rd = ev
            qev = ev

    Qa = c.xt[0:65, 0:512]
    escr = c.xt[:, 512:2560]
    ptb = c.xt[:, 2560:6656].rearrange("p (k q) -> p k q", k=8)
    aost = c.xt[:, 6656:10752].rearrange("p (i d) -> p i d", i=4)
    aoT = c.xt[:, 10752:11776].rearrange("p (cc d) -> p cc d", cc=8)
    avacc = c.scrf[:, 1024:1280].rearrange("p (i d) -> p i d", i=4)

    mcol = c.st[:, S_AT:S_AT + 1]
    mbf = c.st[:, S_AT + 1:S_AT + 2].bitcast(BF16)[:, 0:1]
    biasc = c.st[:, S_AT + 2:S_AT + 3]
    rcol = c.st[:, S_AT + 3:S_AT + 4]
    rinv = c.st[:, S_AT + 4:S_AT + 5]
    rsl = c.st[:, S_AT + 8:S_AT + 12]

    xev = {}
    qa_free = c.fence_ev
    stat_free = None
    for jg in range(4):
        aosl_ev = {}
        for h in range(16):
            g = h // 4
            qa_evs = []
            for i in range(4):
                j = jg * 4 + i
                qa_evs.append(p.op("g", lambda gg, i_=i, jj=j, h_=h:
                                   gg.tensor_copy(
                    Qa[0:64, i_ * 128:(i_ + 1) * 128],
                    q_t[64 * (h_ % 2):64 * (h_ % 2) + 64, h_ // 2,
                        jj * 128:(jj + 1) * 128]),
                    wait=[qev, qa_free]))
            qa_done = qa_evs[-1]
            row_evs = []
            for i in range(4):
                j = jg * 4 + i
                L = (j + 1) * 128
                nb = (L + 511) // 512
                off = 0 if i % 2 == 0 else 4
                reg = c.pa if off == 0 else c.pb
                wl = [qa_done, k_done, ka_neg1] + \
                     [c.banks[off + q].rd for q in range(nb)]
                for kb in range(nb):
                    n = min(512, L - kb * 512)
                    evm = p.op("t", lambda t, kb_=kb, n_=n, i_=i, g_=g, r_=reg:
                               t.matmul(
                        r_[:, kb_ * 512:kb_ * 512 + n_],
                        Qa[0:64, i_ * 128:(i_ + 1) * 128],
                        ka[g_][0:64, kb_ * 512:kb_ * 512 + n_],
                        start=True, stop=True),
                        wait=wl, tag=(kb == nb - 1))
                for kb in range(nb):
                    c.banks[off + kb].wr = evm
                em = p.op("v", lambda v, jj=j, r_=reg: v.tensor_tensor(
                    r_[:, jj * 128:(jj + 1) * 128],
                    r_[:, jj * 128:(jj + 1) * 128], c.maskQ, ALU.add),
                    wait=[evm, c.ev_consts])
                e1 = p.op("v", lambda v, L_=L, r_=reg: v.tensor_reduce(
                    mcol, r_[:, 0:L_], AX.X, ALU.max), wait=[em, stat_free])
                e2 = p.op("v", lambda v: v.tensor_copy(mbf, mcol), wait=[e1])
                e3 = p.op("s", lambda s: s.mul(biasc, mbf, -0.125), wait=[e2])
                e4 = p.op("s", lambda s, L_=L, r_=reg: s.activation(
                    escr[:, 0:L_], r_[:, 0:L_], AF.Exp, scale=0.125, bias=biasc,
                    accum_out=rcol), wait=[e3])
                for kb in range(nb):
                    c.banks[off + kb].rd = e4
                e5 = p.op("v", lambda v: v.reciprocal(rinv, rcol), wait=[e4])
                e5b = p.op("v", lambda v, i_=i: v.tensor_copy(
                    rsl[:, i_:i_ + 1], rinv), wait=[e5])
                b6 = c.banks[6]
                et = p.op("t", lambda t, d=bank_bf(b6, 0)[0:1, :]: t.transpose(
                    d, mbf, c.identb), wait=[e2, b6.rd, e4])
                b6.wr = et
                erow = p.op("s", lambda s, i_=i, d=bank_bf(b6, 0)[0:1, :]: s.copy(
                    Qa[64:65, i_ * 128:(i_ + 1) * 128], d), wait=[et, qa_done])
                b6.rd = erow
                row_evs.append(erow)
                stat_free = erow
            rows_done = row_evs[-1]
            rinv_done = ("v", p.cnt["v"])
            nkt = jg * 4 + 4
            avb = [c.banks[q] for q in range(4)]
            for half in range((nkt + 7) // 8):
                kts = list(range(half * 8, min(nkt, half * 8 + 8)))
                pt_done = {}
                for kt in kts:
                    b45 = c.banks[4 + kt % 2]
                    evm = p.op("t", lambda t, kt_=kt, g_=g, bk=b45: t.matmul(
                        bk.ap, ka[g_][0:65, kt_ * 128:(kt_ + 1) * 128],
                        Qa[0:65, :], start=True, stop=True),
                        wait=[b45.rd, rows_done, ka_neg1], tag=True)
                    b45.wr = evm
                    if kt >= jg * 4:
                        i = kt - jg * 4
                        evm = p.op("v", lambda v, i_=i, bk=b45: v.tensor_tensor(
                            bk.ap[:, i_ * 128:(i_ + 1) * 128],
                            bk.ap[:, i_ * 128:(i_ + 1) * 128], c.maskT, ALU.add),
                            wait=[evm])
                    ee = p.op("s", lambda s, kt_=kt, bk=b45: s.activation(
                        ptb[:, kt_ % 8, :], bk.ap[:, 0:512], AF.Exp, scale=0.125),
                        wait=[evm, pt_done.get(kt % 2)])
                    b45.rd = ee
                    pt_done[kt % 2] = ee
                pt_all = ("s", p.cnt["s"])
                for i in range(4):
                    j = jg * 4 + i
                    lo = half * 8
                    hi = min(j, half * 8 + 7)
                    if hi < lo:
                        continue
                    for kt in range(lo, hi + 1):
                        eav = p.op("t", lambda t, i_=i, kt_=kt, g_=g, lo_=lo, hi_=hi:
                                   t.matmul(
                            avb[i_].ap[:, 0:64], ptb[:, kt_ % 8, i_ * 128:(i_ + 1) * 128],
                            v_tok[:, kt_, g_ * 64:(g_ + 1) * 64],
                            start=(kt_ == lo_), stop=(kt_ == hi_)),
                            wait=([avb[i].rd, pt_all, v_done] if kt == lo else []),
                            tag=(kt == hi))
                    avb[i].wr = eav
                    if half == 0:
                        if j < 8:
                            evo = p.op("v", lambda v, i_=i, h_=h: v.tensor_scalar_mul(
                                aost[:, i_, h_ * 64:(h_ + 1) * 64],
                                avb[i_].ap[:, 0:64], rsl[:, i_:i_ + 1]),
                                wait=[avb[i].wr, rinv_done, aosl_ev.get((i, h))])
                        else:
                            evo = p.op("v", lambda v, i_=i: v.tensor_copy(
                                avacc[:, i_, :], avb[i_].ap[:, 0:64]),
                                wait=[avb[i].wr])
                    else:
                        e6 = p.op("v", lambda v, i_=i: v.tensor_tensor(
                            avacc[:, i_, :], avacc[:, i_, :], avb[i_].ap[:, 0:64],
                            ALU.add), wait=[avb[i].wr])
                        evo = p.op("v", lambda v, i_=i, h_=h: v.tensor_scalar_mul(
                            aost[:, i_, h_ * 64:(h_ + 1) * 64], avacc[:, i_, :],
                            rsl[:, i_:i_ + 1]),
                            wait=[e6, rinv_done, aosl_ev.get((i, h))])
                    avb[i].rd = evo
                    aosl_ev[(i, h)] = evo
            qa_free = ("t", p.cnt["t"])
        ao_done = ("v", p.cnt["v"])
        for i in range(4):
            j = jg * 4 + i
            b6 = c.banks[6]
            aoT_ev = None
            for halfc in range(2):
                for q in range(4):
                    dc = halfc * 4 + q
                    et = p.op("t", lambda t, d=bank_bf(b6, q),
                              s=aost[:, i, dc * 128:(dc + 1) * 128]:
                              t.transpose(d, s, c.identb),
                              wait=[b6.rd, ao_done], tag=(q == 3))
                b6.wr = et
                for q in range(4):
                    dc = halfc * 4 + q
                    ee = p.op("s", lambda s, d=aoT[:, dc, :], sp=bank_bf(b6, q):
                              s.copy(d, sp), wait=[b6.wr], tag=(q == 3))
                b6.rd = ee
            aoT_ev = ee
            if jg == 0 and i == 0:
                c.wo_ld = p.dma("y", lambda y: y.dma_start(
                    wob.rearrange("p cc o -> p (cc o)"),
                    c.s2l0o.rearrange("p a b -> p (a b)")),
                    wait=[c.s2ev["l0o"], ("t", p.cnt["t"])])
            for o in range(2):
                bk = c.banks[o]
                for dc in range(NDC):
                    evm = p.op("t", lambda t, d=dc, o_=o, bk_=bk: t.matmul(
                        bk_.ap, aoT[:, d, :], wob[:, d, o_ * 512:(o_ + 1) * 512],
                        start=(d == 0), stop=(d == 7)),
                        wait=[bk.rd, c.wo_ld, aoT_ev], tag=(dc == 7))
                bk.wr = evm
                evx = p.op("v", lambda v, jj=j, o_=o, bk_=bk: v.tensor_tensor(
                    c.x[:, jj, o_ * 512:(o_ + 1) * 512],
                    c.x[:, jj, o_ * 512:(o_ + 1) * 512], bk_.ap, ALU.add),
                    wait=[bk.wr, xev.get((j, o)), c.last_x_ev])
                bk.rd = evx
                xev[(j, o)] = evx
    c.last_x_ev = ("v", c.p.cnt["v"])



def emit_attn1(c, px):
    p = c.p
    lc = p.dma("y", lambda y: y.dma_start(
        c.curv[0:1, 0:1], c.dram[px + "curv"].rearrange("(a x) -> a x", a=1)))
    b6 = c.banks[6]
    cbv = c.curv.bitcast(BF16)[0:1, 2:3]
    ecst = p.op("v", lambda v: v.tensor_copy(cbv, c.curv[0:1, 0:1]), wait=[lc])
    e1 = p.op("t", lambda t: t.matmul(b6.ap[:, 0:1], c.ones[0:1, 0:128],
                                      cbv, start=True, stop=True),
              wait=[ecst, b6.rd, c.ev_consts])
    b6.wr = e1
    e2 = p.op("s", lambda s: s.copy(c.ccol[:, 0:1], b6.ap[:, 0:1]), wait=[e1])
    b6.rd = e2
    e3 = p.op("s", lambda s: s.activation(c.ccol[:, 1:2], c.ccol[:, 0:1], AF.Sqrt),
              wait=[e2])
    e4 = p.op("v", lambda v: v.reciprocal(c.ccol[:, 2:3], c.ccol[:, 0:1]), wait=[e2])
    sqc = c.ccol[:, 1:2]
    invc = c.ccol[:, 2:3]
    curv_ev = [e3, e4]

    n2c = c.st[:, S_TMP + 4:S_TMP + 5]
    ncc = c.st[:, S_TMP + 5:S_TMP + 6]
    coefc = c.st[:, S_TMP + 6:S_TMP + 7]
    t1c = c.st[:, S_TMP + 7:S_TMP + 8]

    def logmap_tile(j, lnf, tok_bf, ev):
        scr = c.scrf[:, 0:1024]
        a1 = p.op("s", lambda s: s.activation(scr, lnf, AF.Square, accum_out=n2c),
                  wait=[ev])
        a2 = p.op("s", lambda s: s.activation(ncc, n2c, AF.Sqrt), wait=[a1])
        a3 = p.op("v", lambda v: v.tensor_tensor(ncc, ncc, sqc, ALU.mult),
                  wait=[a2] + curv_ev)
        a4 = p.op("s", lambda s: s.activation(t1c, ncc, AF.Square), wait=[a3])
        a5 = p.op("v", lambda v: v.tensor_scalar_add(t1c, t1c, 1.0), wait=[a4])
        a6 = p.op("s", lambda s: s.activation(t1c, t1c, AF.Sqrt), wait=[a5])
        a7 = p.op("v", lambda v: v.tensor_tensor(t1c, t1c, ncc, ALU.add), wait=[a6])
        a8 = p.op("s", lambda s: s.activation(t1c, t1c, AF.Ln), wait=[a7])
        a9 = p.op("v", lambda v: v.reciprocal(coefc, ncc), wait=[a8])
        a10 = p.op("v", lambda v: v.tensor_tensor(coefc, coefc, t1c, ALU.mult),
                   wait=[a9])
        a11 = p.op("v", lambda v: v.tensor_scalar_mul(tok_bf, lnf, coefc), wait=[a10])
        return a11

    emit_ln_all(c, per_tile=logmap_tile)

    qtok = c.pool[:, 0:2048].rearrange("p (j d) -> p j d", j=NT)
    ktok = c.pool[:, 2048:4096].rearrange("p (j d) -> p j d", j=NT)
    Qa = c.pool[0:66, 4096:4608]
    Ka = c.pool[0:66, 4608:6656]
    vch = c.pool[:, 6656:8704].rearrange("p (j d) -> p j d", j=NT)
    ptb = c.pool[:, 8704:12800].rearrange("p (k q) -> p k q", k=8)
    escr = c.pool[:, 12800:14848]
    aoT = c.pool[:, 14848:15872].rearrange("p (cc d) -> p cc d", cc=8)
    qf = c.pool[:, 8704:10752]
    avacc = c.scrf[:, 1024:1280].rearrange("p (i d) -> p i d", i=4)
    ao = c.pool2[:, 0:16384].rearrange("p (j d) -> p j d", j=NT)
    wpb2 = c.pool2[:, 16384:17408].rearrange("p (cc o) -> p cc o", cc=8)
    wob = c.pool2[:, 16384:24576].rearrange("p (cc o) -> p cc o", cc=8)

    bev = []
    for nm, dst in (("wq_b", c.browq), ("wk_b", c.browk), ("wv_b", c.browv),
                    ("wo_b", c.browo)):
        bev.append(p.dma("g", lambda g, d=dst, s=c.wbf[px + nm]: g.dma_start(d, s),
                         wait=[c.fence_ev]))
    brow_ev = bev[-1]

    wqn = c.wbf[px + "wq_w"]; wkn = c.wbf[px + "wk_w"]
    wvn = c.wbf[px + "wv_w"]; won = c.wbf[px + "wo_w"]
    evw = {k: c.ev_prepass[px + k] for k in ("wq_w", "wk_w", "wv_w", "wo_w")}

    n2c = c.st[:, S_TMP + 4:S_TMP + 5]
    ncc = c.st[:, S_TMP + 5:S_TMP + 6]
    coefc = c.st[:, S_TMP + 6:S_TMP + 7]
    t1c = c.st[:, S_TMP + 7:S_TMP + 8]
    n2h = c.st[:, S_AT + 16:S_AT + 32]
    timbf = c.st[:, S_AT + 32:S_AT + 40].bitcast(BF16)
    pairb = c.st[:, S_AT + 40:S_AT + 42].bitcast(BF16)[:, 0:4]
    mcol = c.st[:, S_AT + 42:S_AT + 43]
    mbf = c.st[:, S_AT + 43:S_AT + 44].bitcast(BF16)[:, 0:1]
    biasc = c.st[:, S_AT + 44:S_AT + 45]
    rcol = c.st[:, S_AT + 45:S_AT + 46]
    rinv = c.st[:, S_AT + 46:S_AT + 47]
    rsl = c.st[:, S_AT + 8:S_AT + 12]
    sq_scr = c.scrf[:, 0:64]

    xev = {}
    wp_free = c.fence_ev
    qf_free = c.fence_ev

    def proj_feat_to_tok(cc, wscr, wld_ev, brow, tokdst):
        nonlocal wp_free, qf_free
        qf_free = ("t", p.cnt["t"])
        wp_ld = p.dma("y", lambda y, s_=wscr, c_=cc: y.dma_start(
            wpb2, s_[c_]), wait=[wp_free, wld_ev])
        qf_ev = None
        for tg in range(4):
            bk = c.banks[4 + tg % 2]
            for dc in range(NDC):
                p.op("t", lambda t, d=dc, tg_=tg, bk_=bk: t.matmul(
                    bk_.ap, wpb2[:, d, :],
                    xt_view(c, d, tg_ * 512, tg_ * 512 + 512),
                    start=(d == 0), stop=False),
                    wait=[bk.rd, wp_ld, c.xt_ev], tag=False)
            evm = p.op("t", lambda t, c_=cc, bk_=bk, br=brow: t.matmul(
                bk_.ap, br[0:1, c_ * 128:(c_ + 1) * 128], c.ones[0:1, 0:512],
                start=False, stop=True), wait=[brow_ev], tag=True)
            bk.wr = evm
            ee = p.op("s", lambda s, tg_=tg, bk_=bk: s.copy(
                qf[:, tg_ * 512:(tg_ + 1) * 512], bk_.ap), wait=[bk.wr, qf_free])
            bk.rd = ee
            qf_ev = ee
        wp_free = ("t", p.cnt["t"])
        tev = None
        b7 = c.banks[7]
        for jq in range(4):
            for q in range(4):
                j = jq * 4 + q
                et = p.op("t", lambda t, d=bank_bf(b7, q),
                          s=qf[:, j * 128:(j + 1) * 128]: t.transpose(d, s, c.identb),
                          wait=[b7.rd, qf_ev], tag=(q == 3))
            b7.wr = et
            for q in range(4):
                j = jq * 4 + q
                ec = p.op("s", lambda s, jj=j, sp=bank_bf(b7, q), td=tokdst: s.copy(
                    td[:, jj, :], sp), wait=[b7.wr], tag=(q == 3))
            b7.rd = ec
            tev = ec
        qf_free = ("t", p.cnt["t"])
        return tev

    nb2 = c.st[:, S_AT + 16:S_AT + 32]
    ncb = c.st[:, S_AT + 64:S_AT + 80]
    e1b = c.st[:, S_AT + 80:S_AT + 96]
    e2b = c.st[:, S_AT + 96:S_AT + 112]
    coefb = c.st[:, S_AT + 112:S_AT + 128]
    timf = c.st[:, S_AT + 128:S_AT + 144]
    timbfQ = c.st[:, S_AT + 144:S_AT + 152].bitcast(BF16)
    timbfK = c.st[:, S_AT + 32:S_AT + 40].bitcast(BF16)
    scr2 = c.scrf[:, 0:1024].rearrange("p (j d) -> p j d", j=NT)
    exm_prev = [None]

    war_tim = {"K": None, "Q": None}

    def expmap_batch(tok, hd, timdst, neg, ev_tok, war_key):
        sl3 = tok[:, :, hd * 64:(hd + 1) * 64]
        a1 = p.op("v", lambda v: v.tensor_tensor(scr2, sl3, sl3, ALU.mult),
                  wait=[ev_tok, exm_prev[0], war_tim[war_key]])
        a2 = p.op("v", lambda v: v.tensor_reduce(
            nb2.rearrange("p (j o) -> p j o", o=1), scr2, AX.X, ALU.add), wait=[a1])
        a3 = p.op("s", lambda s: s.activation(ncb, nb2, AF.Sqrt), wait=[a2])
        a4 = p.op("v", lambda v: v.tensor_scalar_mul(ncb, ncb, sqc),
                  wait=[a3] + curv_ev)
        a5 = p.op("s", lambda s: s.activation(e1b, ncb, AF.Exp), wait=[a4])
        a6 = p.op("s", lambda s: s.activation(e2b, ncb, AF.Exp, scale=-1.0), wait=[a4])
        a7 = p.op("v", lambda v: v.tensor_tensor(e1b, e1b, e2b, ALU.subtract),
                  wait=[a5, a6])
        a8 = p.op("v", lambda v: v.reciprocal(e2b, ncb), wait=[a6, a7])
        a9 = p.op("v", lambda v: v.tensor_tensor(coefb, e1b, e2b, ALU.mult),
                  wait=[a7, a8])
        a10 = p.op("v", lambda v: v.tensor_scalar_mul(coefb, coefb, 0.5), wait=[a9])
        ah = a10
        for j in range(NT):
            ah = p.op("v", lambda v, jj=j: v.tensor_scalar_mul(
                tok[:, jj, hd * 64:(hd + 1) * 64], tok[:, jj, hd * 64:(hd + 1) * 64],
                coefb[:, jj:jj + 1]), wait=[a10] if j == 0 else [ah])
        a12 = p.op("v", lambda v: v.tensor_tensor(scr2, sl3, sl3, ALU.mult), wait=[ah])
        a13 = p.op("v", lambda v: v.tensor_reduce(
            nb2.rearrange("p (j o) -> p j o", o=1), scr2, AX.X, ALU.add), wait=[a12])
        a14 = p.op("v", lambda v: v.tensor_single_scalar(timf, nb2, invc, ALU.add),
                   wait=[a13] + curv_ev)
        a15 = p.op("s", lambda s: s.activation(timf, timf, AF.Sqrt), wait=[a14])
        a16 = p.op("v", lambda v, sc=-1.0 if neg else 1.0: v.tensor_scalar_mul(
            timdst, timf, sc), wait=[a15])
        exm_prev[0] = a16
        return a16

    for cc in range(8):
        tq = proj_feat_to_tok(cc, c.s2l1q, c.s2ev[("l1q", cc)], c.browq, qtok)
        tk = proj_feat_to_tok(cc, c.s2l1k, c.s2ev[("l1k", cc)], c.browk, ktok)
        wv_ld = p.dma("y", lambda y, c_=cc: y.dma_start(
            wpb2, c.s2l1v[c_]), wait=[wp_free, c.s2ev[("l1v", cc)]])
        vev = None
        for j in range(NT):
            bk = c.banks[4 + j % 2]
            for dc in range(NDC):
                p.op("t", lambda t, d=dc, jj=j, bk_=bk: t.matmul(
                    bk_.ap[:, 0:128], xt_view(c, d, jj * 128, (jj + 1) * 128),
                    wpb2[:, d, :], start=(d == 0), stop=False),
                    wait=[bk.rd, wv_ld, c.xt_ev], tag=False)
            evm = p.op("t", lambda t, bk_=bk, c_=cc: t.matmul(
                bk_.ap[:, 0:128], c.ones[0:1, 0:128],
                c.browv[0:1, c_ * 128:(c_ + 1) * 128], start=False, stop=True),
                wait=[brow_ev], tag=True)
            bk.wr = evm
            vev = p.op("s", lambda s, jj=j, bk_=bk: s.copy(
                vch[:, jj, :], bk_.ap[:, 0:128]), wait=[bk.wr])
            bk.rd = vev
        wp_free = ("t", p.cnt["t"])
        v_done = vev
        for hd in range(2):
            h = cc * 2 + hd
            ek = expmap_batch(ktok, hd, timbfK, True, tk, "K")
            eka0 = p.op("g", lambda g: g.memset(Ka[64:66, :], -1.0), wait=[ek])
            kev = None
            for j in range(NT):
                sl = ktok[:, j, hd * 64:(hd + 1) * 64]
                et = p.op("t", lambda t, d=bank_bf(b6, 0)[0:64, :], sl_=sl:
                          t.transpose(d, sl_, c.identb), wait=[ek, b6.rd])
                b6.wr = et
                ec = p.op("s", lambda s, jj=j, d=bank_bf(b6, 0)[0:64, :]: s.copy(
                    Ka[0:64, jj * 128:(jj + 1) * 128], d), wait=[et])
                b6.rd = ec
                et2 = p.op("t", lambda t, jj=j, d=bank_bf(b6, 1)[0:1, :]: t.transpose(
                    d, timbfK[:, jj:jj + 1], c.identb), wait=[ek, b6.rd])
                b6.wr = et2
                ec2 = p.op("s", lambda s, jj=j, d=bank_bf(b6, 1)[0:1, :]: s.copy(
                    Ka[64:65, jj * 128:(jj + 1) * 128], d), wait=[et2, eka0])
                b6.rd = ec2
                kev = ec2
            ka_done = kev
            war_tim["K"] = ka_done
            eq = expmap_batch(qtok, hd, timbfQ, False, tq, "Q")
            for jg in range(4):
                qa_evs = []
                for i in range(4):
                    j = jg * 4 + i
                    sl = qtok[:, j, hd * 64:(hd + 1) * 64]
                    et = p.op("t", lambda t, d=bank_bf(b6, 0)[0:64, :], sl_=sl:
                              t.transpose(d, sl_, c.identb), wait=[eq, b6.rd])
                    b6.wr = et
                    ec = p.op("s", lambda s, i_=i, d=bank_bf(b6, 0)[0:64, :]: s.copy(
                        Qa[0:64, i_ * 128:(i_ + 1) * 128], d), wait=[et])
                    b6.rd = ec
                    et2 = p.op("t", lambda t, jj=j, d=bank_bf(b6, 1)[0:1, :]:
                               t.transpose(d, timbfQ[:, jj:jj + 1], c.identb),
                               wait=[eq, b6.rd])
                    b6.wr = et2
                    ec2 = p.op("s", lambda s, i_=i, d=bank_bf(b6, 1)[0:1, :]: s.copy(
                        Qa[64:65, i_ * 128:(i_ + 1) * 128], d), wait=[et2])
                    b6.rd = ec2
                    qa_evs.append(ec2)
                qa_done = qa_evs[-1]
                row_evs = []
                stat_free = None
                for i in range(4):
                    j = jg * 4 + i
                    off = 0 if i % 2 == 0 else 4
                    reg = c.pa if off == 0 else c.pb
                    wl2 = [qa_done, ka_done, stat_free] + \
                          [c.banks[off + q].rd for q in range(4)]
                    for kb in range(4):
                        evm = p.op("t", lambda t, kb_=kb, i_=i, r_=reg: t.matmul(
                            r_[:, kb_ * 512:(kb_ + 1) * 512],
                            Qa[0:65, i_ * 128:(i_ + 1) * 128],
                            Ka[0:65, kb_ * 512:(kb_ + 1) * 512],
                            start=True, stop=True), wait=wl2, tag=(kb == 3))
                    for kb in range(4):
                        c.banks[off + kb].wr = evm
                    e1 = p.op("v", lambda v, r_=reg: v.tensor_reduce(
                        mcol, r_[:, 0:2048], AX.X, ALU.min), wait=[evm])
                    e2 = p.op("v", lambda v: v.tensor_copy(mbf, mcol), wait=[e1])
                    e3 = p.op("s", lambda s: s.mul(biasc, mbf, 0.125), wait=[e2])
                    e4 = p.op("s", lambda s, r_=reg: s.activation(
                        escr, r_[:, 0:2048], AF.Exp, scale=-0.125, bias=biasc,
                        accum_out=rcol), wait=[e3])
                    for kb in range(4):
                        c.banks[off + kb].rd = e4
                    e5 = p.op("v", lambda v: v.reciprocal(rinv, rcol), wait=[e4])
                    p.op("v", lambda v, i_=i: v.tensor_copy(
                        rsl[:, i_:i_ + 1], rinv), wait=[e5])
                    e6 = p.op("v", lambda v, jj=j: v.tensor_copy(
                        pairb[:, 2:3], timbfQ[:, jj:jj + 1]), wait=[eq])
                    e7 = p.op("v", lambda v: v.tensor_copy(pairb[:, 3:4], mbf),
                              wait=[e2, e6])
                    et = p.op("t", lambda t, d=bank_bf(b6, 2)[0:2, :]: t.transpose(
                        d, pairb[:, 2:4], c.identb), wait=[e7, b6.rd])
                    b6.wr = et
                    erow = p.op("s", lambda s, i_=i, d=bank_bf(b6, 2)[0:2, :]: s.copy(
                        Qa[64:66, i_ * 128:(i_ + 1) * 128], d), wait=[et, e4])
                    b6.rd = erow
                    row_evs.append(erow)
                    stat_free = erow
                rows_done = row_evs[-1]
                war_tim["Q"] = rows_done
                rinv_done = ("v", p.cnt["v"])
                avb = [c.banks[q] for q in range(4)]
                for half in range(2):
                    kts = list(range(half * 8, half * 8 + 8))
                    pt_done = {}
                    for kt in kts:
                        b45 = c.banks[4 + kt % 2]
                        evm = p.op("t", lambda t, kt_=kt, bk=b45: t.matmul(
                            bk.ap, Ka[0:66, kt_ * 128:(kt_ + 1) * 128], Qa[0:66, :],
                            start=True, stop=True),
                            wait=[b45.rd, rows_done], tag=True)
                        b45.wr = evm
                        ee = p.op("s", lambda s, kt_=kt, bk=b45: s.activation(
                            ptb[:, kt_ % 8, :], bk.ap, AF.Exp, scale=-0.125),
                            wait=[evm, pt_done.get(kt % 2)])
                        b45.rd = ee
                        pt_done[kt % 2] = ee
                    pt_all = ("s", p.cnt["s"])
                    for i in range(4):
                        lo, hi = half * 8, half * 8 + 7
                        for kt in range(lo, hi + 1):
                            eav = p.op("t", lambda t, i_=i, kt_=kt, hd_=hd, lo_=lo,
                                       hi_=hi: t.matmul(
                                avb[i_].ap[:, 0:64],
                                ptb[:, kt_ % 8, i_ * 128:(i_ + 1) * 128],
                                vch[:, kt_, hd_ * 64:(hd_ + 1) * 64],
                                start=(kt_ == lo_), stop=(kt_ == hi_)),
                                wait=([avb[i].rd, pt_all, v_done] if kt == lo
                                      else []),
                                tag=(kt == hi))
                        avb[i].wr = eav
                        if half == 0:
                            evo = p.op("v", lambda v, i_=i: v.tensor_copy(
                                avacc[:, i_, :], avb[i_].ap[:, 0:64]),
                                wait=[avb[i].wr])
                        else:
                            e6 = p.op("v", lambda v, i_=i: v.tensor_tensor(
                                avacc[:, i_, :], avacc[:, i_, :],
                                avb[i_].ap[:, 0:64], ALU.add), wait=[avb[i].wr])
                            j = jg * 4 + i
                            evo = p.op("v", lambda v, i_=i, jj=j, h_=h:
                                       v.tensor_scalar_mul(
                                ao[:, jj, h_ * 64:(h_ + 1) * 64], avacc[:, i_, :],
                                rsl[:, i_:i_ + 1]), wait=[e6, rinv_done])
                        avb[i].rd = evo
    ao_done = ("v", p.cnt["v"])
    wo_ld = p.dma("y", lambda y: y.dma_start(
        wob.rearrange("p cc o -> p (cc o)"), c.s2l1o.rearrange("p a b -> p (a b)")),
        wait=[c.s2ev["l1o"], wp_free, ("t", p.cnt["t"])])
    b7 = c.banks[7]
    for j in range(NT):
        aoT_ev = None
        for halfc in range(2):
            for q in range(4):
                dc = halfc * 4 + q
                et = p.op("t", lambda t, d=bank_bf(b7, q),
                          s=ao[:, j, dc * 128:(dc + 1) * 128]:
                          t.transpose(d, s, c.identb),
                          wait=[b7.rd, ao_done], tag=(q == 3))
            b7.wr = et
            for q in range(4):
                dc = halfc * 4 + q
                ee = p.op("s", lambda s, d=aoT[:, dc, :], sp=bank_bf(b7, q):
                          s.copy(d, sp), wait=[b7.wr], tag=(q == 3))
            b7.rd = ee
        aoT_ev = ee
        for o in range(2):
            bk = c.banks[o]
            for dc in range(NDC):
                p.op("t", lambda t, d=dc, o_=o, bk_=bk: t.matmul(
                    bk_.ap, aoT[:, d, :], wob[:, d, o_ * 512:(o_ + 1) * 512],
                    start=(d == 0), stop=False),
                    wait=[bk.rd, wo_ld, aoT_ev], tag=False)
            evm = p.op("t", lambda t, o_=o, bk_=bk: t.matmul(
                bk_.ap, c.ones[0:1, 0:128], c.browo[0:1, o_ * 512:(o_ + 1) * 512],
                start=False, stop=True), wait=[brow_ev], tag=True)
            bk.wr = evm
            evx = p.op("v", lambda v, jj=j, o_=o, bk_=bk: v.tensor_tensor(
                c.x[:, jj, o_ * 512:(o_ + 1) * 512],
                c.x[:, jj, o_ * 512:(o_ + 1) * 512], bk_.ap, ALU.add),
                wait=[bk.wr, xev.get((j, o)), c.last_x_ev])
            bk.rd = evx
            xev[(j, o)] = evx
    c.last_x_ev = ("v", c.p.cnt["v"])


SUB_PREFIX = {"f0a": "L0_ffn_", "f0b": "L0_ffn_", "a0": "L0_attn_",
              "f1a": "L1_ffn_", "f1b": "L1_ffn_", "a1": "L1_attn_"}
SUB_WNAMES = {
    "f0a": ["router_weight", "gate_up_proj", "down_proj", "shared_gate_w",
            "shared_up_w", "shared_down_w", "shared_expert_gate_w"],
    "a0": ["W_q", "W_k", "W_v", "W_o"],
    "f1a": ["w_router", "w_up", "w_gate", "w_down", "ws_up", "ws_gate", "ws_down"],
    "a1": ["curv", "wq_w", "wq_b", "wk_w", "wk_b", "wv_w", "wv_b", "wo_w", "wo_b"],
}
SUB_WNAMES["f0b"] = SUB_WNAMES["f0a"]
SUB_WNAMES["f1b"] = SUB_WNAMES["f1a"]
EMIT = {"f0a": lambda c: emit_ffn0(c, "L0_ffn_"),
        "f0b": lambda c: emit_ffn0(c, "L0_ffn_"),
        "a0": lambda c: emit_attn0(c, "L0_attn_"),
        "f1a": lambda c: emit_ffn1(c, "L1_ffn_"),
        "f1b": lambda c: emit_ffn1(c, "L1_ffn_"),
        "a1": lambda c: emit_attn1(c, "L1_attn_")}

INPUT_SHAPES = {
    "L0_attn_W_q": [1024, 1024], "L0_attn_W_k": [256, 1024],
    "L0_attn_W_v": [256, 1024], "L0_attn_W_o": [1024, 1024],
    "L0_ffn_router_weight": [8, 1024], "L0_ffn_gate_up_proj": [8, 1024, 1024],
    "L0_ffn_down_proj": [8, 1024, 512], "L0_ffn_shared_gate_w": [2048, 1024],
    "L0_ffn_shared_up_w": [2048, 1024], "L0_ffn_shared_down_w": [1024, 2048],
    "L0_ffn_shared_expert_gate_w": [1, 1024],
    "L1_attn_curv": [], "L1_attn_wq_w": [1024, 1024], "L1_attn_wq_b": [1024],
    "L1_attn_wk_w": [1024, 1024], "L1_attn_wk_b": [1024],
    "L1_attn_wv_w": [1024, 1024], "L1_attn_wv_b": [1024],
    "L1_attn_wo_w": [1024, 1024], "L1_attn_wo_b": [1024],
    "L1_ffn_w_router": [1024, 8], "L1_ffn_w_up": [7, 1024, 4096],
    "L1_ffn_w_gate": [7, 1024, 4096], "L1_ffn_w_down": [7, 4096, 1024],
    "L1_ffn_ws_up": [1024, 4096], "L1_ffn_ws_gate": [1024, 4096],
    "L1_ffn_ws_down": [4096, 1024],
}


def build_group(subs, taps=()):
    nc = bass.Bass("TRN2", target_bir_lowering=False, debug=False)
    p = Prog(nc)
    c = alloc_ctx(nc, p)
    c._taps = list(taps)
    needed = []
    for s in subs:
        for w in SUB_WNAMES[s]:
            full = SUB_PREFIX[s] + w
            if full not in needed:
                needed.append(full)
    for full in needed:
        shp = INPUT_SHAPES[full]
        if full == "L1_attn_curv":
            c.dram[full] = nc.dram_tensor(full, [1], F32, kind="ExternalInput").ap()
        else:
            c.dram[full] = nc.dram_tensor(full, shp, F32, kind="ExternalInput").ap()
    c.dram["x_in"] = nc.dram_tensor("x_in", [T, D], F32, kind="ExternalInput").ap()
    c.dram["x_out"] = nc.dram_tensor("x_out", [T, D], F32, kind="ExternalOutput").ap()
    if any(s == "a0" for s in subs):
        c.dram["rope_cs"] = nc.dram_tensor("rope_cs", [T, 1024], F32,
                                           kind="ExternalInput").ap()
    emit_consts(c)
    emit_prepass_casts(c, subs)
    exl = p.dma("y", lambda y: y.dma_start(
        c.x, c.dram["x_in"].rearrange("(j p) d -> p j d", p=128)))
    c.last_x_ev = exl
    c.fence_ev = None
    pre_done = set()
    for s in subs:
        fence_w = [(k, p.cnt[k]) for k in ("t", "v", "s", "g")]
        c.fence_ev = p.op("g", lambda g: g.nop(), wait=fence_w)
        fam = SUB_PREFIX[s]
        if fam not in pre_done:
            pre_done.add(fam)
            emit_prepass_family(c, s)
        EMIT[s](c)
    sev = p.dma("y", lambda y: y.dma_start(
        c.dram["x_out"].rearrange("(j p) d -> p j d", p=128), c.x),
        wait=[c.last_x_ev])
    for name, getter in c._taps:
        ap = getter(c)
        shp = list(ap.shape)
        dt_ = ap.dtype
        od = nc.dram_tensor("tap_" + name, shp, dt_, kind="ExternalOutput").ap()
        allw = [(k, p.cnt[k]) for k in ("t", "v", "s", "g")]
        sev = p.dma("y", lambda y, o=od, a=ap: y.dma_start(o, a), wait=allw)
    p.op("y", lambda y: y.nop(), wait=[sev], tag=False)
    p.finalize()
    nc.finalize()
    return nc


GROUPS = [["f0a", "a0", "f0b", "f1a", "a1", "f1b"]]

_ROPE = None


def _rope_table():
    global _ROPE
    if _ROPE is None:
        theta = 1.0 / 10000.0 ** (np.arange(512, dtype=np.float64) / 512.0)
        ang = np.arange(T, dtype=np.float64)[:, None] * theta[None, :]
        _ROPE = np.ascontiguousarray(
            np.concatenate([np.cos(ang), np.sin(ang)], 1).astype(np.float32))
    return _ROPE


_CACHE = {}


def kernel(**inputs):
    X = np.ascontiguousarray(np.asarray(inputs["L0_attn_X"], np.float32))
    xs = [np.ascontiguousarray(X[i]) for i in range(NCORE)]
    for subs in GROUPS:
        key = tuple(subs)
        if key not in _CACHE:
            _CACHE[key] = build_group(subs)
        nc = _CACHE[key]
        needed = []
        for s in subs:
            for w in SUB_WNAMES[s]:
                full = SUB_PREFIX[s] + w
                if full not in needed:
                    needed.append(full)
        base = {}
        for full in needed:
            a = np.ascontiguousarray(np.asarray(inputs[full], np.float32))
            if full == "L1_attn_curv":
                a = a.reshape(1)
            base[full] = a
        if any(s == "a0" for s in subs):
            base["rope_cs"] = _rope_table()
        in_maps = [dict(base, x_in=xs[i]) for i in range(NCORE)]
        res = run_bass_kernel_spmd(nc, in_maps, core_ids=list(range(NCORE)))
        xs = [np.asarray(res.results[i]["x_out"], np.float32) for i in range(NCORE)]
    return np.stack(xs, 0)


# version 08
# speedup vs baseline: 1.3541x; 1.0494x over previous
import numpy as np
import ml_dtypes
import concourse.bass as bass
import concourse.mybir as mybir
from concourse.bass_utils import run_bass_kernel_spmd

F32 = mybir.dt.float32
BF16 = mybir.dt.bfloat16
AF = mybir.ActivationFunctionType
ALU = mybir.AluOpType
AX = mybir.AxisListType

D = 1024
T = 2048
NT = 16
NDC = 8
NCORE = 8

S_MEAN, S_NEGM, S_VAR, S_RSTD = 0, 16, 32, 48
S_CW = 64
S_SEG = 192
S_TMP = 208
S_AT = 256


class Prog:
    def __init__(self, nc):
        self.nc = nc
        self.qs = {k: [] for k in "tvsgy"}
        keys = ["t", "v", "s", "g", "y", "gd", "yd", "sd"]
        self.sems = {k: nc.alloc_semaphore("sem_" + k) for k in keys}
        self.cnt = {k: 0 for k in keys}
        self.waited = {e: {} for e in "tvsgy"}

    def _push(self, eng, fn, wait, inc):
        wl = []
        seen = self.waited[eng]
        for w in wait:
            if w is None:
                continue
            k, v = w
            if seen.get(k, 0) >= v:
                continue
            seen[k] = v
            wl.append((k, v))
        self.qs[eng].append((fn, wl, inc))

    def op(self, eng, fn, wait=(), tag=True):
        ev = None
        inc = None
        if tag:
            self.cnt[eng] += 1
            inc = (eng, 1)
            ev = (eng, self.cnt[eng])
        self._push(eng, fn, wait, inc)
        return ev

    def dma(self, eng, fn, wait=()):
        k = eng + "d"
        self.cnt[k] += 16
        self._push(eng, fn, wait, (k, 16))
        return (k, self.cnt[k])

    def finalize(self):
        nc = self.nc
        sems = self.sems
        for e in "tvsgy":
            for _, wl, _ in self.qs[e]:
                for k, v in wl:
                    assert v <= self.cnt[k], ("wait beyond plan", e, k, v, self.cnt[k])

        def runner(q):
            def f(eng):
                for fn, wl, inc in q:
                    for k, v in wl:
                        eng.wait_ge(sems[k], v)
                    inst = fn(eng)
                    if inc is not None:
                        inst.then_inc(sems[inc[0]], inc[1])
            return f

        with nc.Block() as blk:
            blk.tensor(runner(self.qs["t"]))
            blk.vector(runner(self.qs["v"]))
            blk.scalar(runner(self.qs["s"]))
            blk.gpsimd(runner(self.qs["g"]))
            blk.sync(runner(self.qs["y"]))


class Bank:
    __slots__ = ("ap", "wr", "rd")

    def __init__(self, ap):
        self.ap = ap
        self.wr = None
        self.rd = None


def bank_bf(bank, slot):
    return bank.ap[:, slot * 64:(slot + 1) * 64].bitcast(BF16)


class Ctx:
    pass


def alloc_ctx(nc, p):
    c = Ctx()
    c.nc = nc
    c.p = p
    sb = lambda n, s, d: nc.alloc_sbuf_tensor(n, s, d).ap()
    c.x = sb("x_res", [128, NT, D], F32)
    c.xt = sb("xt_fm", [128, 8 * T], BF16)
    c.pool = sb("pool", [128, 16384], BF16)
    c.pool2 = sb("pool2", [128, 24576], BF16)
    c.scrf = sb("scrf", [128, 3072], F32)
    c.st = sb("stats", [128, 512], F32)
    c.cstf = sb("cstf", [128, 384], F32)
    c.cstb = sb("cstb", [128, 768], BF16)
    c.browq = sb("browq", [1, 1024], BF16)
    c.browk = sb("browk", [1, 1024], BF16)
    c.browv = sb("browv", [1, 1024], BF16)
    c.browo = sb("browo", [1, 1024], BF16)
    c.wr8 = sb("wr8", [128, 80], BF16)
    c.rt8 = sb("rt8", [9, 1024], BF16)
    c.curv = sb("curv", [1, 4], F32)
    c.ccol = sb("ccol", [128, 4], F32)
    c.pa = nc.alloc_psum_tensor("pa", [128, 2048], F32).ap()
    c.pb = nc.alloc_psum_tensor("pb", [128, 2048], F32).ap()
    c.banks = [Bank(c.pa[:, i * 512:(i + 1) * 512]) for i in range(4)] + \
              [Bank(c.pb[:, i * 512:(i + 1) * 512]) for i in range(4)]
    c.stg = sb("stg", [128, 2048], BF16)
    c.stg_free = [None, None]
    c.stg_par = 0
    c.s2ev = {}
    c.dram = {}
    c.wbf = {}
    c.ev_prepass = {}
    c.last_x_ev = None
    c.xt_ev = None
    c.fence_ev = None
    return c


def emit_consts(c):
    p = c.p
    def mk_ident(ap):
        e1 = p.op("g", lambda g, a=ap: g.memset(a, 0.0))
        return p.op("g", lambda g, a=ap: g.affine_select(
            out=a, in_=a, compare_op=ALU.not_equal, fill=1.0, base=0,
            pattern=[[-1, 128]], channel_multiplier=1), wait=[e1])
    mk_ident(c.cstf[:, 0:128])
    e1 = p.op("g", lambda g: g.memset(c.cstf[:, 128:256], 0.0))
    p.op("g", lambda g: g.affine_select(
        out=c.cstf[:, 128:256], in_=c.cstf[:, 128:256], compare_op=ALU.is_ge,
        fill=-1e9, base=0, pattern=[[-1, 128]], channel_multiplier=1), wait=[e1])
    e2 = p.op("g", lambda g: g.memset(c.cstf[:, 256:384], 0.0))
    p.op("g", lambda g: g.affine_select(
        out=c.cstf[:, 256:384], in_=c.cstf[:, 256:384], compare_op=ALU.is_ge,
        fill=-1e9, base=0, pattern=[[1, 128]], channel_multiplier=-1), wait=[e2])
    mk_ident(c.cstb[:, 0:128])
    ev = p.op("g", lambda g: g.memset(c.cstb[:, 128:640], 1.0))
    c.ev_consts = ev
    c.identf = c.cstf[:, 0:128]
    c.identb = c.cstb[:, 0:128]
    c.maskQ = c.cstf[:, 128:256]
    c.maskT = c.cstf[:, 256:384]
    c.ones = c.cstb[:, 128:640]


def emit_cast(c, full_names):
    p = c.p
    for n in full_names:
        if n in c.wbf:
            continue
        src = c.dram[n]
        shp = list(src.shape)
        if len(shp) == 1:
            shp = [1, shp[0]]
            src = src.rearrange("(a d) -> a d", a=1)
        dst = c.nc.dram_tensor("bf_" + n, shp, BF16).ap()
        if len(shp) == 3:
            evs = []
            for e in range(shp[0]):
                evs.append(p.dma("g", lambda g, a=dst[e], b=src[e]: g.dma_start(a, b)))
                c.ev_prepass[(n, e)] = evs[-1]
            c.ev_prepass[n] = evs[-1]
        else:
            c.ev_prepass[n] = p.dma("g", lambda g, a=dst, b=src: g.dma_start(a, b))
        c.wbf[n] = dst


def emit_rearrange_cast(c, key, src, dst, ev=None):
    c.s2ev[key] = c.p.dma("g", lambda g, a=dst, b=src: g.dma_start(a, b),
                          wait=[ev])


def emit_pre_transpose(c, key, pieces):
    p = c.p
    ev = None
    for src, dst, wev in pieces:
        b = c.stg_par % 2
        R = src.shape[0]
        stga = c.stg[:, b * 1024: b * 1024 + R]
        e1 = p.dma("y", lambda s, a=stga, b_=src: s.dma_start_transpose(a, b_),
                   wait=[wev, c.stg_free[b]])
        e2 = p.dma("y", lambda s, a=dst, b_=stga: s.dma_start(a, b_), wait=[e1])
        c.stg_free[b] = e2
        c.stg_par += 1
        ev = e2
    c.s2ev[key] = ev


def emit_prepass_casts(c, subs):
    done = set(subs)
    if "f0a" in done or "f0b" in done:
        px = "L0_ffn_"
        emit_cast(c, [px + n for n in ("router_weight", "shared_expert_gate_w",
                                       "gate_up_proj", "down_proj",
                                       "shared_gate_w", "shared_up_w",
                                       "shared_down_w")])
    if "a0" in done:
        emit_cast(c, ["L0_attn_" + n for n in ("W_q", "W_k", "W_v", "W_o")])
    if "f1a" in done or "f1b" in done:
        px = "L1_ffn_"
        emit_cast(c, [px + "w_router"])
        nc = c.nc
        c.s2gug = nc.dram_tensor("s2gug", [8, 16, 128, 2, 8, 256], BF16).ap()
        c.s2wd = nc.dram_tensor("s2wd", [8, 4, 128, 8, 1024], BF16).ap()
        for ei in range(8):
            if ei < 7:
                usrc = c.dram[px + "w_up"][ei]
                gsrc = c.dram[px + "w_gate"][ei]
                dsrc = c.dram[px + "w_down"][ei]
            else:
                usrc = c.dram[px + "ws_up"]
                gsrc = c.dram[px + "ws_gate"]
                dsrc = c.dram[px + "ws_down"]
            for hp in range(16):
                emit_rearrange_cast(c, ("gug", ei), usrc[:, hp * 256:(hp + 1) * 256]
                                    .rearrange("(cc p) h -> p cc h", p=128),
                                    c.s2gug[ei, hp][:, 0])
                emit_rearrange_cast(c, ("gug", ei), gsrc[:, hp * 256:(hp + 1) * 256]
                                    .rearrange("(cc p) h -> p cc h", p=128),
                                    c.s2gug[ei, hp][:, 1])
            for q in range(4):
                emit_rearrange_cast(c, ("wd", ei), dsrc[q * 1024:(q + 1) * 1024, :]
                                    .rearrange("(hh p) o -> p hh o", p=128),
                                    c.s2wd[ei, q])
    if "a1" in done:
        emit_cast(c, ["L1_attn_" + n for n in ("wq_w", "wk_w", "wv_w", "wo_w",
                                               "wq_b", "wk_b", "wv_b", "wo_b")])


def emit_prepass_family(c, s):
    nc = c.nc
    dram = lambda n, sh: nc.dram_tensor(n, sh, BF16).ap()
    if s in ("f0a", "f0b"):
        px = "L0_ffn_"
        c.s2gu0 = dram("s2gu0", [8, 128, 8, 1024])
        c.s2dn0 = dram("s2dn0", [8, 128, 4, 1024])
        c.s2sg0 = dram("s2sg0", [2, 4, 128, 8, 512])
        c.s2sd0 = dram("s2sd0", [128, 16, 1024])
        gu_bf = c.wbf[px + "gate_up_proj"]
        dn_bf = c.wbf[px + "down_proj"]
        for e in range(8):
            emit_pre_transpose(c, ("gu0", e), [
                (gu_bf[e][:, dc * 128:(dc + 1) * 128], c.s2gu0[e][:, dc, :],
                 c.ev_prepass[(px + "gate_up_proj", e)]) for dc in range(8)])
            emit_pre_transpose(c, ("dn0", e), [
                (dn_bf[e][:, ic * 128:(ic + 1) * 128], c.s2dn0[e][:, ic, :],
                 c.ev_prepass[(px + "down_proj", e)]) for ic in range(4)])
        for part, nm in ((0, "shared_gate_w"), (1, "shared_up_w")):
            for grp in range(4):
                emit_pre_transpose(c, ("sg0", part, grp), [
                    (c.wbf[px + nm][grp * 512:(grp + 1) * 512,
                                    dc * 128:(dc + 1) * 128],
                     c.s2sg0[part, grp][:, dc, :], c.ev_prepass[px + nm])
                    for dc in range(8)])
        emit_pre_transpose(c, "sd0", [
            (c.wbf[px + "shared_down_w"][:, ic * 128:(ic + 1) * 128],
             c.s2sd0[:, ic, :], c.ev_prepass[px + "shared_down_w"])
            for ic in range(16)])
    elif s == "a0":
        px = "L0_attn_"
        c.s2l0q = dram("s2l0q", [128, 8, 1024])
        c.s2l0k = dram("s2l0k", [128, 8, 256])
        c.s2l0v = dram("s2l0v", [128, 8, 256])
        c.s2l0o = dram("s2l0o", [128, 8, 1024])
        for key, nm, scr in (("l0k", "W_k", c.s2l0k), ("l0v", "W_v", c.s2l0v),
                             ("l0q", "W_q", c.s2l0q), ("l0o", "W_o", c.s2l0o)):
            emit_pre_transpose(c, key, [
                (c.wbf[px + nm][:, dc * 128:(dc + 1) * 128], scr[:, dc, :],
                 c.ev_prepass[px + nm]) for dc in range(8)])
    elif s in ("f1a", "f1b"):
        pass
    elif s == "a1":
        px = "L1_attn_"
        c.s2l1q = dram("s2l1q", [8, 128, 8, 128])
        c.s2l1k = dram("s2l1k", [8, 128, 8, 128])
        c.s2l1v = dram("s2l1v", [8, 128, 8, 128])
        c.s2l1o = dram("s2l1o", [128, 8, 1024])
        for key, nm, scr in (("l1q", "wq_w", c.s2l1q), ("l1k", "wk_w", c.s2l1k),
                             ("l1v", "wv_w", c.s2l1v)):
            for cc in range(8):
                emit_pre_transpose(c, (key, cc), [
                    (c.wbf[px + nm][cc * 128:(cc + 1) * 128,
                                    dc * 128:(dc + 1) * 128],
                     scr[cc][:, dc, :], c.ev_prepass[px + nm]) for dc in range(8)])
        emit_pre_transpose(c, "l1o", [
            (c.wbf[px + "wo_w"][:, dc * 128:(dc + 1) * 128], c.s2l1o[:, dc, :],
             c.ev_prepass[px + "wo_w"]) for dc in range(8)])



def emit_tok_to_feat(c, j, src_bf, bank, wait=()):
    p = c.p
    ev = None
    for half in range(2):
        tw = [bank.rd] + list(wait)
        for i in range(4):
            dc = half * 4 + i
            tev = p.op("t", lambda t, d=bank_bf(bank, i),
                       s=src_bf[:, dc * 128:(dc + 1) * 128]:
                       t.transpose(d, s, c.identb), wait=tw, tag=(i == 3))
        bank.wr = tev
        for i in range(4):
            dc = half * 4 + i
            eev = p.op("s", lambda s, d=c.xt[:, dc * T + j * 128: dc * T + (j + 1) * 128],
                       sp=bank_bf(bank, i): s.copy(d, sp),
                       wait=[bank.wr], tag=(i == 3))
        bank.rd = eev
        ev = eev
    return ev


def emit_ln_all(c, per_tile=None):
    p = c.p
    st = c.st
    mean_all = st[:, S_MEAN:S_MEAN + 16]
    negm_all = st[:, S_NEGM:S_NEGM + 16]
    var_all = st[:, S_VAR:S_VAR + 16]
    rstd_all = st[:, S_RSTD:S_RSTD + 16]
    scr = c.scrf[:, 0:1024]
    red = None
    for j in range(NT):
        red = p.op("v", lambda v, jj=j: v.reduce_sum(
            mean_all[:, jj:jj + 1], c.x[:, jj, :], AX.X), wait=[c.last_x_ev])
    em = p.op("v", lambda v: v.tensor_scalar_mul(mean_all, mean_all, 1.0 / 1024.0),
              wait=[red])
    en = p.op("s", lambda s: s.mul(negm_all, mean_all, -1.0), wait=[em])
    sq = None
    for j in range(NT):
        sq = p.op("s", lambda s, jj=j: s.activation(
            scr, c.x[:, jj, :], AF.Square, bias=negm_all[:, jj:jj + 1], scale=1.0,
            accum_out=var_all[:, jj:jj + 1]), wait=[en])
    ev1 = p.op("v", lambda v: v.tensor_scalar(out=rstd_all, in0=var_all,
                                              scalar1=1.0 / 1024.0, scalar2=1e-5,
                                              op0=ALU.mult, op1=ALU.add), wait=[sq])
    ev2 = p.op("v", lambda v: v.reciprocal(rstd_all, rstd_all), wait=[ev1])
    ev3 = p.op("s", lambda s: s.activation(rstd_all, rstd_all, AF.Sqrt), wait=[ev2])
    tok_bf = c.pool[:, 0:1024]
    lnf = c.scrf[:, 1024:2048]
    lnev = None
    for j in range(NT):
        dst = lnf if per_tile is not None else tok_bf
        e = p.op("v", lambda v, jj=j, d=dst: v.tensor_scalar(
            out=d, in0=c.x[:, jj, :], scalar1=mean_all[:, jj:jj + 1],
            scalar2=rstd_all[:, jj:jj + 1], op0=ALU.subtract, op1=ALU.mult),
            wait=[ev3, lnev, c.fence_ev, em])
        if per_tile is not None:
            e = per_tile(j, lnf, tok_bf, e)
        lnev = emit_tok_to_feat(c, j, tok_bf, c.banks[7], wait=[e])
    c.xt_ev = lnev


def xt_view(c, dc, t0, t1):
    return c.xt[:, dc * T + t0: dc * T + t1]



def emit_router_l1(c, px):
    p = c.p
    lw = p.dma("y", lambda y: y.dma_start(
        c.wr8[:, 0:64].rearrange("p (cc e) -> p cc e", e=8),
        c.wbf[px + "w_router"].rearrange("(cc p) e -> p cc e", p=128)),
        wait=[c.ev_prepass[px + "w_router"], c.fence_ev])
    bank = c.banks[0]
    cw = c.st[:, S_CW:S_CW + 128].rearrange("p (j e) -> p j e", e=8)
    ev = None
    prev_read = None
    for j in range(NT):
        for dc in range(NDC):
            evm = p.op("t", lambda t, d=dc, jj=j: t.matmul(
                bank.ap[:, 0:8], xt_view(c, d, jj * 128, (jj + 1) * 128),
                c.wr8[:, d * 8:(d + 1) * 8], start=(d == 0), stop=(d == NDC - 1)),
                wait=[bank.rd, lw, c.xt_ev, prev_read], tag=(dc == NDC - 1))
        bank.wr = evm
        lg = bank.ap[:, 0:8]
        mcol = c.st[:, S_TMP:S_TMP + 1]
        sumc = c.st[:, S_TMP + 1:S_TMP + 2]
        m1 = c.st[:, S_TMP + 2:S_TMP + 3]
        escr = c.scrf[:, 0:8]
        e1 = p.op("v", lambda v: v.tensor_reduce(mcol, lg, AX.X, ALU.max),
                  wait=[bank.wr, prev_read])
        e2 = p.op("s", lambda s: s.mul(mcol, mcol, -1.0), wait=[e1])
        e3 = p.op("s", lambda s: s.activation(escr, lg, AF.Exp, bias=mcol, scale=1.0,
                                              accum_out=sumc), wait=[e2, prev_read])
        bank.rd = e3
        e4 = p.op("v", lambda v: v.tensor_reduce(m1, escr, AX.X, ALU.max), wait=[e3])
        e5 = p.op("v", lambda v: v.reciprocal(sumc, sumc), wait=[e3, e4])
        e6 = p.op("v", lambda v, jj=j: v.tensor_scalar(
            out=cw[:, jj, :], in0=escr, scalar1=m1, scalar2=sumc,
            op0=ALU.is_ge, op1=ALU.mult), wait=[e5, e4])
        prev_read = e6
        ev = e6
    return ev


def emit_router_l0(c, px):
    p = c.p
    lw1 = p.dma("y", lambda y: y.dma_start(c.rt8[0:8, :], c.wbf[px + "router_weight"]),
                wait=[c.ev_prepass[px + "router_weight"], c.fence_ev])
    lw2 = p.dma("y", lambda y: y.dma_start(c.rt8[8:9, :],
                                           c.wbf[px + "shared_expert_gate_w"]),
                wait=[c.ev_prepass[px + "shared_expert_gate_w"], c.fence_ev])
    bank = c.banks[0]
    wr9 = c.wr8.rearrange("p (cc e) -> p cc e", e=10)
    for dcg in range(2):
        for i in range(4):
            dc = dcg * 4 + i
            tev = p.op("t", lambda t, d=bank_bf(bank, i)[:, 0:9],
                       s=c.rt8[0:9, dc * 128:(dc + 1) * 128]:
                       t.transpose(d, s, c.identb[0:9, 0:9]),
                       wait=[bank.rd, lw1, lw2], tag=(i == 3))
        bank.wr = tev
        for i in range(4):
            dc = dcg * 4 + i
            eev = p.op("s", lambda s, d=wr9[:, dc, 0:9],
                       sp=bank_bf(bank, i)[:, 0:9]: s.copy(d, sp),
                       wait=[bank.wr], tag=(i == 3))
        bank.rd = eev
    lwr = bank.rd
    cw = c.st[:, S_CW:S_CW + 128].rearrange("p (j e) -> p j e", e=8)
    seg = c.st[:, S_SEG:S_SEG + 16]
    ev = None
    prev_read = None
    for j in range(NT):
        for dc in range(NDC):
            evm = p.op("t", lambda t, d=dc, jj=j: t.matmul(
                bank.ap[:, 0:9], xt_view(c, d, jj * 128, (jj + 1) * 128),
                wr9[:, d, 0:9], start=(d == 0), stop=(d == NDC - 1)),
                wait=[bank.rd, lwr, c.xt_ev], tag=(dc == NDC - 1))
        bank.wr = evm
        e8 = p.op("s", lambda s, jj=j: s.activation(
            seg[:, jj:jj + 1], bank.ap[:, 8:9], AF.Sigmoid, scale=1.0),
            wait=[bank.wr])
        e8b = p.op("s", lambda s, jj=j: s.mul(seg[:, jj:jj + 1], seg[:, jj:jj + 1], 0.5),
                   wait=[e8])
        lg = bank.ap[:, 0:8]
        mcol = c.st[:, S_TMP:S_TMP + 1]
        den = c.st[:, S_TMP + 1:S_TMP + 2]
        escr = c.scrf[:, 0:8]
        s8 = c.scrf[:, 8:16]
        keep = c.scrf[:, 16:24]
        e1 = p.op("v", lambda v: v.tensor_reduce(mcol, lg, AX.X, ALU.max),
                  wait=[bank.wr, prev_read])
        e2 = p.op("s", lambda s: s.mul(mcol, mcol, -1.0), wait=[e1])
        e3 = p.op("s", lambda s: s.activation(escr, lg, AF.Exp, bias=mcol, scale=1.0),
                  wait=[e2, prev_read])
        bank.rd = e3
        e4 = p.op("v", lambda v: v.max(s8, escr), wait=[e3])
        e5 = p.op("v", lambda v: v.tensor_tensor(den, s8[:, 0:1], s8[:, 1:2], ALU.add),
                  wait=[e4])
        e6 = p.op("v", lambda v: v.reciprocal(den, den), wait=[e5])
        e7 = p.op("v", lambda v: v.tensor_single_scalar(keep, escr, s8[:, 1:2],
                                                        ALU.is_ge), wait=[e4, e3])
        e9 = p.op("v", lambda v: v.tensor_tensor(keep, keep, escr, ALU.mult), wait=[e7])
        e10 = p.op("v", lambda v, jj=j: v.tensor_scalar(
            out=cw[:, jj, :], in0=keep, scalar1=den, scalar2=0.5,
            op0=ALU.mult, op1=ALU.mult), wait=[e9, e6])
        prev_read = e10
        ev = e10
    return ev



def emit_ffn1(c, px):
    p = c.p
    emit_ln_all(c)
    rt_ev = emit_router_l1(c, px)

    wu = c.pool2[:, 0:8192].rearrange("p (b ug cc h) -> p b ug cc h", b=2, ug=2,
                                      cc=8)
    wd = c.pool2[:, 8192:24576].rearrange("p (b hh o) -> p b hh o", b=2, hh=8)
    wu_ev = [None, None]
    wu_free = [c.fence_ev, c.fence_ev]
    wd_ev = [None, None]
    wd_free = [c.fence_ev, c.fence_ev]
    h_t = c.pool.rearrange("p (hh tt) -> p hh tt", hh=32)
    xev = {}
    prev_sil_read = [None, None]

    experts = list(range(7)) + ["sh"]
    for tg in range(4):
        t0 = tg * 512
        for e in experts:
            ei = 7 if e == "sh" else e
            evug = c.s2ev[("gug", ei)]
            evwd = c.s2ev[("wd", ei)]
            h_done_q = {}
            for hp in range(16):
                b = hp % 2
                wu_ev[b] = p.dma("y", lambda y, e_=ei, hp_=hp, b_=b: y.dma_start(
                    wu[:, b_], c.s2gug[e_, hp_]),
                    wait=[wu_free[b], evug])
                for hi in range(2):
                    hh = hp * 2 + hi
                    bu = c.banks[hh % 2]
                    bg = c.banks[2 + hh % 2]
                    sil = c.scrf[:, 1024 + 512 * (hh % 2): 1536 + 512 * (hh % 2)]
                    for dc in range(NDC):
                        evu = p.op("t", lambda t, d=dc, b_=b, hi_=hi, bk=bu,
                                   t0_=t0: t.matmul(
                            bk.ap, wu[:, b_, 0, d, hi_ * 128:(hi_ + 1) * 128],
                            xt_view(c, d, t0_, t0_ + 512),
                            start=(d == 0), stop=(d == 7)),
                            wait=[bu.rd, wu_ev[b], c.xt_ev], tag=(dc == 7))
                    bu.wr = evu
                    for dc in range(NDC):
                        evg = p.op("t", lambda t, d=dc, b_=b, hi_=hi, bk=bg,
                                   t0_=t0: t.matmul(
                            bk.ap, wu[:, b_, 1, d, hi_ * 128:(hi_ + 1) * 128],
                            xt_view(c, d, t0_, t0_ + 512),
                            start=(d == 0), stop=(d == 7)),
                            wait=[bg.rd, wu_ev[b], c.xt_ev], tag=(dc == 7))
                    bg.wr = evg
                    es = p.op("s", lambda s, bk=bg, sl=sil: s.activation(
                        sl, bk.ap, AF.Silu), wait=[bg.wr, prev_sil_read[hh % 2]])
                    bg.rd = es
                    eh = p.op("v", lambda v, hh_=hh, bk=bu, sl=sil: v.tensor_tensor(
                        h_t[:, hh_, :], bk.ap, sl, ALU.mult), wait=[es, bu.wr])
                    bu.rd = eh
                    prev_sil_read[hh % 2] = eh
                    if hi == 1:
                        wu_free[b] = eh
                    if hh % 8 == 7:
                        h_done_q[hh // 8] = eh
            for q in range(4):
                b = q % 2
                wd_ev[b] = p.dma("y", lambda y, e_=ei, q_=q, b_=b: y.dma_start(
                    wd[:, b_], c.s2wd[e_, q_]),
                    wait=[wd_free[b], evwd])
                ev_last = None
                for i2 in range(2):
                    for o in range(2):
                        for half in range(2):
                            i = half * 2 + i2
                            bk = c.banks[4 + i2 * 2 + o]
                            for hh8 in range(8):
                                hh = q * 8 + hh8
                                ev_last = p.op("t", lambda t, hh_=hh, hh8_=hh8, i_=i,
                                               o_=o, b_=b, bk_=bk: t.matmul(
                                    bk_.ap, h_t[:, hh_, i_ * 128:(i_ + 1) * 128],
                                    wd[:, b_, hh8_, o_ * 512:(o_ + 1) * 512],
                                    start=(hh8_ == 0), stop=(hh8_ == 7)),
                                    wait=([bk.rd, wd_ev[b], h_done_q[q]]
                                          if hh8 == 0 else []),
                                    tag=(hh8 == 7))
                            bk.wr = ev_last
                            j = tg * 4 + i
                            sc = (0.5 if e == "sh" else
                                  c.st[:, S_CW + j * 8 + e:S_CW + j * 8 + e + 1])
                            evx = p.op("v", lambda v, jj=j, o_=o, bk_=bk, sc_=sc:
                                       v.scalar_tensor_tensor(
                                           out=c.x[:, jj, o_ * 512:(o_ + 1) * 512],
                                           in0=bk_.ap, scalar=sc_,
                                           in1=c.x[:, jj, o_ * 512:(o_ + 1) * 512],
                                           op0=ALU.mult, op1=ALU.add),
                                       wait=[bk.wr, rt_ev, xev.get((j, o))])
                            bk.rd = evx
                            xev[(j, o)] = evx
                wd_free[b] = ev_last
    c.last_x_ev = ("v", c.p.cnt["v"])



def emit_ffn0(c, px, parts=("experts", "shared"), tgs=(0, 1, 2, 3)):
    p = c.p
    emit_ln_all(c)
    rt_ev = emit_router_l0(c, px)
    cw = c.st[:, S_CW:S_CW + 128].rearrange("p (j e) -> p j e", e=8)
    seg = c.st[:, S_SEG:S_SEG + 16]

    gu = c.pool2[:, 0:16384].rearrange("p (b cc o) -> p b cc o", b=2, cc=8)
    dn = c.pool2[:, 16384:24576].rearrange("p (b cc o) -> p b cc o", b=2, cc=4)
    gu_ev = [None, None]; dn_ev = [None, None]
    gu_free = [c.fence_ev, c.fence_ev]; dn_free = [c.fence_ev, c.fence_ev]
    h_t = c.pool.rearrange("p (hh tt) -> p hh tt", hh=32)
    xev = {}
    prev_h_read = None
    sd_free_ev = c.fence_ev

    for tg in tgs:
        t0 = tg * 512
        for e in (range(8) if "experts" in parts else []):
            b = e % 2
            gu_ev[b] = p.dma("y", lambda y, e_=e, b_=b: y.dma_start(
                gu[:, b_], c.s2gu0[e_]),
                wait=[gu_free[b], c.s2ev[("gu0", e)]])
            dn_ev[b] = p.dma("y", lambda y, e_=e, b_=b: y.dma_start(
                dn[:, b_], c.s2dn0[e_]),
                wait=[dn_free[b], c.s2ev[("dn0", e)]])
            sil_ev = None
            for oc in range(8):
                bk = c.banks[oc % 2]
                for dc in range(NDC):
                    evm = p.op("t", lambda t, d=dc, o_=oc, b_=b, bk_=bk, t0_=t0:
                               t.matmul(
                        bk_.ap, gu[:, b_, d, o_ * 128:(o_ + 1) * 128],
                        xt_view(c, d, t0_, t0_ + 512),
                        start=(d == 0), stop=(d == 7)),
                        wait=[bk.rd, gu_ev[b], c.xt_ev], tag=(dc == 7))
                bk.wr = evm
                if oc < 4:
                    es = p.op("s", lambda s, o_=oc, bk_=bk: s.activation(
                        h_t[:, 16 + o_, :], bk_.ap, AF.Silu),
                        wait=[bk.wr, prev_h_read])
                    bk.rd = es
                    sil_ev = es
                else:
                    eh = p.op("v", lambda v, o_=oc, bk_=bk: v.tensor_tensor(
                        h_t[:, o_ - 4, :], bk_.ap, h_t[:, 16 + o_ - 4, :], ALU.mult),
                        wait=[bk.wr, sil_ev])
                    bk.rd = eh
            h_done = eh
            gu_free[b] = eh
            ev_last = None
            for i in range(4):
                jj = tg * 4 + i
                for o in range(2):
                    bk = c.banks[4 + (i % 2) * 2 + o]
                    for ic in range(4):
                        ev_last = p.op("t", lambda t, i_=i, o_=o, ic_=ic, b_=b, bk_=bk:
                                       t.matmul(
                            bk_.ap, h_t[:, ic_, i_ * 128:(i_ + 1) * 128],
                            dn[:, b_, ic_, o_ * 512:(o_ + 1) * 512],
                            start=(ic_ == 0), stop=(ic_ == 3)),
                            wait=[bk.rd, dn_ev[b], h_done], tag=(ic == 3))
                    bk.wr = ev_last
                    evx = p.op("v", lambda v, jj_=jj, o_=o, bk_=bk, e_=e:
                               v.scalar_tensor_tensor(
                                   out=c.x[:, jj_, o_ * 512:(o_ + 1) * 512],
                                   in0=bk_.ap, scalar=c.st[:, S_CW + jj_ * 8 + e_:S_CW + jj_ * 8 + e_ + 1],
                                   in1=c.x[:, jj_, o_ * 512:(o_ + 1) * 512],
                                   op0=ALU.mult, op1=ALU.add),
                               wait=[bk.wr, rt_ev, xev.get((jj, o))])
                    bk.rd = evx
                    xev[(jj, o)] = evx
            dn_free[b] = ev_last
            prev_h_read = ev_last
        if "shared" not in parts:
            continue
        shb = c.pool2[:, 0:8192].rearrange("p (b cc o) -> p b cc o", b=2, cc=8)
        sh_ev = [gu_ev[0], gu_ev[1]]
        sh_free = [gu_free[0], gu_free[1]]
        sil_done = {}
        for part, nm in ((0, "shared_gate_w"), (1, "shared_up_w")):
            for grp in range(4):
                b = grp % 2
                sh_ev[b] = p.dma("y", lambda y, b_=b, pt=part, g_=grp: y.dma_start(
                    shb[:, b_], c.s2sg0[pt, g_]),
                    wait=[sh_free[b], c.s2ev[("sg0", part, grp)]])
                for oc in range(4):
                    hc = grp * 4 + oc
                    bk = c.banks[oc % 2]
                    for dc in range(NDC):
                        evm = p.op("t", lambda t, d=dc, o_=oc, b_=b, bk_=bk, t0_=t0:
                                   t.matmul(
                            bk_.ap, shb[:, b_, d, o_ * 128:(o_ + 1) * 128],
                            xt_view(c, d, t0_, t0_ + 512),
                            start=(d == 0), stop=(d == 7)),
                            wait=[bk.rd, sh_ev[b], c.xt_ev], tag=(dc == 7))
                    bk.wr = evm
                    if part == 0:
                        es = p.op("s", lambda s, h_=hc, bk_=bk: s.activation(
                            h_t[:, 16 + h_, :], bk_.ap, AF.Silu),
                            wait=[bk.wr, prev_h_read])
                        bk.rd = es
                        sil_done[hc] = es
                    else:
                        eh = p.op("v", lambda v, h_=hc, bk_=bk: v.tensor_tensor(
                            h_t[:, h_, :], bk_.ap, h_t[:, 16 + h_, :], ALU.mult),
                            wait=[bk.wr, sil_done[hc], prev_h_read])
                        bk.rd = eh
                sh_free[b] = bk.rd
        h_done = bk.rd
        sdb = c.pool2[:, 8192:24576].rearrange("p (cc o) -> p cc o", cc=16)
        sd_ld = p.dma("y", lambda y: y.dma_start(sdb, c.s2sd0),
                      wait=[sd_free_ev, gu_free[0], gu_free[1], dn_free[0],
                            dn_free[1], c.s2ev["sd0"]])
        ev_last = None
        for i in range(4):
            jj = tg * 4 + i
            for o in range(2):
                bk = c.banks[4 + (i % 2) * 2 + o]
                for ic in range(16):
                    ev_last = p.op("t", lambda t, ic_=ic, i_=i, o_=o, bk_=bk:
                                   t.matmul(
                        bk_.ap, h_t[:, ic_, i_ * 128:(i_ + 1) * 128],
                        sdb[:, ic_, o_ * 512:(o_ + 1) * 512],
                        start=(ic_ == 0), stop=(ic_ == 15)),
                        wait=([bk.rd, sd_ld, h_done] if ic == 0 else []),
                        tag=(ic == 15))
                bk.wr = ev_last
                evx = p.op("v", lambda v, jj_=jj, o_=o, bk_=bk: v.scalar_tensor_tensor(
                    out=c.x[:, jj_, o_ * 512:(o_ + 1) * 512],
                    in0=bk_.ap, scalar=seg[:, jj_:jj_ + 1],
                    in1=c.x[:, jj_, o_ * 512:(o_ + 1) * 512],
                    op0=ALU.mult, op1=ALU.add),
                    wait=[bk.wr, rt_ev, xev.get((jj, o))])
                bk.rd = evx
                xev[(jj, o)] = evx
        sd_free_ev = ev_last
        prev_h_read = ev_last
        gu_free = [ev_last, ev_last]
        dn_free = [ev_last, ev_last]
    c.last_x_ev = ("v", c.p.cnt["v"])




def attn_core(c, layer, heads_iter, ka_of, vtok_of, rinv_sl, aowrite):
    raise NotImplementedError


def emit_attn0(c, px):
    p = c.p
    ropebuf = c.scrf[:, 2048:3072]
    tmp1 = c.scrf[:, 0:512]
    tmp2 = c.scrf[:, 512:1024]
    rope_evs = {}

    def rope_tile(j, lnf, tok_bf, ev):
        ld = p.dma("y", lambda y, jj=j: y.dma_start(
            ropebuf, c.dram["rope_cs"][jj * 128:(jj + 1) * 128, :]),
            wait=[rope_evs.get("last")])
        xe = lnf[:, 0:1024:2]
        xo = lnf[:, 1:1024:2]
        cosv = ropebuf[:, 0:512]
        sinv = ropebuf[:, 512:1024]
        e1 = p.op("v", lambda v: v.tensor_tensor(tmp1, xe, cosv, ALU.mult), wait=[ev, ld])
        e2 = p.op("v", lambda v: v.tensor_tensor(tmp2, xo, sinv, ALU.mult), wait=[ev, ld])
        e3 = p.op("v", lambda v: v.tensor_tensor(tok_bf[:, 0:512], tmp1, tmp2,
                                                 ALU.subtract), wait=[e1, e2])
        e4 = p.op("v", lambda v: v.tensor_tensor(tmp1, xe, sinv, ALU.mult), wait=[e3])
        e5 = p.op("v", lambda v: v.tensor_tensor(tmp2, xo, cosv, ALU.mult), wait=[e3])
        e6 = p.op("v", lambda v: v.tensor_tensor(tok_bf[:, 512:1024], tmp1, tmp2,
                                                 ALU.add), wait=[e4, e5])
        rope_evs["last"] = e6
        return e6

    emit_ln_all(c, per_tile=rope_tile)

    ka = [c.pool2[0:65, g * 2048:(g + 1) * 2048] for g in range(4)]
    v_tok = c.pool2[:, 8192:12288].rearrange("p (j o) -> p j o", o=256)
    wkT = c.pool2[:, 12288:14336].rearrange("p (cc o) -> p cc o", cc=8)
    wvT = c.pool2[:, 14336:16384].rearrange("p (cc o) -> p cc o", cc=8)
    wqT = c.pool2[:, 16384:24576].rearrange("p (cc o) -> p cc o", cc=8)
    wob = wqT
    q_t = c.pool.rearrange("p (cc tt) -> p cc tt", cc=8)

    wq = c.wbf[px + "W_q"]; wk = c.wbf[px + "W_k"]
    wv = c.wbf[px + "W_v"]; wo = c.wbf[px + "W_o"]
    evw = {k: c.ev_prepass[px + k] for k in ("W_q", "W_k", "W_v", "W_o")}

    e1 = p.dma("y", lambda y: y.dma_start(
        wkT.rearrange("p cc o -> p (cc o)"), c.s2l0k.rearrange("p a b -> p (a b)")),
        wait=[c.s2ev["l0k"], c.fence_ev])
    e2 = p.dma("y", lambda y: y.dma_start(
        wvT.rearrange("p cc o -> p (cc o)"), c.s2l0v.rearrange("p a b -> p (a b)")),
        wait=[c.s2ev["l0v"], c.fence_ev])
    kv_ld = e2
    wq_ld = p.dma("y", lambda y: y.dma_start(
        wqT.rearrange("p cc o -> p (cc o)"), c.s2l0q.rearrange("p a b -> p (a b)")),
        wait=[c.s2ev["l0q"], c.fence_ev])

    for tg in range(4):
        t0 = tg * 512
        for kc in range(2):
            bk = c.banks[kc]
            for dc in range(NDC):
                evm = p.op("t", lambda t, d=dc, k_=kc, t0_=t0, bk_=bk: t.matmul(
                    bk_.ap, wkT[:, d, k_ * 128:(k_ + 1) * 128],
                    xt_view(c, d, t0_, t0_ + 512), start=(d == 0), stop=(d == 7)),
                    wait=[bk.rd, kv_ld, c.xt_ev], tag=(dc == 7))
            bk.wr = evm
            e1 = p.op("s", lambda s, k_=kc, t0_=t0, bk_=bk: s.copy(
                ka[2 * k_][0:64, t0_:t0_ + 512], bk_.ap[0:64, :]), wait=[bk.wr])
            e2 = p.op("s", lambda s, k_=kc, t0_=t0, bk_=bk: s.copy(
                ka[2 * k_ + 1][0:64, t0_:t0_ + 512], bk_.ap[64:128, :]), wait=[bk.wr])
            bk.rd = e2
    k_done = e2
    mevs = [p.op("g", lambda gg, g_=g: gg.memset(ka[g_][64:65, :], -1.0),
                 wait=[c.fence_ev]) for g in range(4)]
    ka_neg1 = mevs[-1]

    vev = None
    for j in range(NT):
        bk = c.banks[2 + j % 2]
        for dc in range(NDC):
            evm = p.op("t", lambda t, d=dc, jj=j, bk_=bk: t.matmul(
                bk_.ap[:, 0:256], xt_view(c, d, jj * 128, (jj + 1) * 128),
                wvT[:, d, :], start=(d == 0), stop=(d == 7)),
                wait=[bk.rd, kv_ld, c.xt_ev], tag=(dc == 7))
        bk.wr = evm
        vev = p.op("s", lambda s, jj=j, bk_=bk: s.copy(v_tok[:, jj, :],
                                                       bk_.ap[:, 0:256]),
                   wait=[bk.wr])
        bk.rd = vev
    v_done = vev

    qev = None
    for oc in range(NDC):
        for tg in range(4):
            t0 = tg * 512
            bk = c.banks[4 + tg % 2]
            for dc in range(NDC):
                evm = p.op("t", lambda t, d=dc, o_=oc, t0_=t0, bk_=bk: t.matmul(
                    bk_.ap, wqT[:, d, o_ * 128:(o_ + 1) * 128],
                    xt_view(c, d, t0_, t0_ + 512), start=(d == 0), stop=(d == 7)),
                    wait=[bk.rd, wq_ld, c.xt_ev], tag=(dc == 7))
            bk.wr = evm
            ev = p.op("s", lambda s, o_=oc, t0_=t0, bk_=bk: s.copy(
                q_t[:, o_, t0_:t0_ + 512], bk_.ap), wait=[bk.wr])
            bk.rd = ev
            qev = ev

    Qa = c.xt[0:65, 0:512]
    escr = c.xt[:, 512:2560]
    ptb = c.xt[:, 2560:6656].rearrange("p (k q) -> p k q", k=8)
    aost = c.xt[:, 6656:10752].rearrange("p (i d) -> p i d", i=4)
    aoT = c.xt[:, 10752:11776].rearrange("p (cc d) -> p cc d", cc=8)
    avacc = c.scrf[:, 1024:1280].rearrange("p (i d) -> p i d", i=4)

    mcol = c.st[:, S_AT:S_AT + 1]
    mbf = c.st[:, S_AT + 1:S_AT + 2].bitcast(BF16)[:, 0:1]
    biasc = c.st[:, S_AT + 2:S_AT + 3]
    rcol = c.st[:, S_AT + 3:S_AT + 4]
    rinv = c.st[:, S_AT + 4:S_AT + 5]
    rsl = c.st[:, S_AT + 8:S_AT + 12]

    xev = {}
    qa_free = c.fence_ev
    stat_free = None
    for jg in range(4):
        aosl_ev = {}
        for h in range(16):
            g = h // 4
            qa_evs = []
            for i in range(4):
                j = jg * 4 + i
                qa_evs.append(p.op("s", lambda ss, i_=i, jj=j, h_=h: ss.copy(
                    Qa[0:64, i_ * 128:(i_ + 1) * 128],
                    q_t[64 * (h_ % 2):64 * (h_ % 2) + 64, h_ // 2,
                        jj * 128:(jj + 1) * 128]),
                    wait=[qev, qa_free]))
            qa_done = qa_evs[-1]
            row_evs = []
            for i in range(4):
                j = jg * 4 + i
                L = (j + 1) * 128
                nb = (L + 511) // 512
                off = 0 if i % 2 == 0 else 4
                reg = c.pa if off == 0 else c.pb
                wl = [qa_done, k_done, ka_neg1] + \
                     [c.banks[off + q].rd for q in range(nb)]
                for kb in range(nb):
                    n = min(512, L - kb * 512)
                    evm = p.op("t", lambda t, kb_=kb, n_=n, i_=i, g_=g, r_=reg:
                               t.matmul(
                        r_[:, kb_ * 512:kb_ * 512 + n_],
                        Qa[0:64, i_ * 128:(i_ + 1) * 128],
                        ka[g_][0:64, kb_ * 512:kb_ * 512 + n_],
                        start=True, stop=True),
                        wait=wl, tag=(kb == nb - 1))
                for kb in range(nb):
                    c.banks[off + kb].wr = evm
                em = p.op("v", lambda v, jj=j, r_=reg: v.tensor_tensor(
                    r_[:, jj * 128:(jj + 1) * 128],
                    r_[:, jj * 128:(jj + 1) * 128], c.maskQ, ALU.add),
                    wait=[evm, c.ev_consts])
                e1 = p.op("v", lambda v, L_=L, r_=reg: v.tensor_reduce(
                    mcol, r_[:, 0:L_], AX.X, ALU.max), wait=[em, stat_free])
                e2 = p.op("v", lambda v: v.tensor_copy(mbf, mcol), wait=[e1])
                e3 = p.op("s", lambda s: s.mul(biasc, mbf, -0.125), wait=[e2])
                e4 = p.op("s", lambda s, L_=L, r_=reg: s.activation(
                    escr[:, 0:L_], r_[:, 0:L_], AF.Exp, scale=0.125, bias=biasc,
                    accum_out=rcol), wait=[e3])
                for kb in range(nb):
                    c.banks[off + kb].rd = e4
                e5 = p.op("v", lambda v: v.reciprocal(rinv, rcol), wait=[e4])
                e5b = p.op("v", lambda v, i_=i: v.tensor_copy(
                    rsl[:, i_:i_ + 1], rinv), wait=[e5])
                b6 = c.banks[6]
                et = p.op("t", lambda t, d=bank_bf(b6, 0)[0:1, :]: t.transpose(
                    d, mbf, c.identb), wait=[e2, b6.rd, e4])
                b6.wr = et
                erow = p.op("s", lambda s, i_=i, d=bank_bf(b6, 0)[0:1, :]: s.copy(
                    Qa[64:65, i_ * 128:(i_ + 1) * 128], d), wait=[et, qa_done])
                b6.rd = erow
                row_evs.append(erow)
                stat_free = erow
            rows_done = row_evs[-1]
            rinv_done = ("v", p.cnt["v"])
            nkt = jg * 4 + 4
            avb = [c.banks[q] for q in range(4)]
            for half in range((nkt + 7) // 8):
                kts = list(range(half * 8, min(nkt, half * 8 + 8)))
                pt_done = {}
                for kt in kts:
                    b45 = c.banks[4 + kt % 2]
                    evm = p.op("t", lambda t, kt_=kt, g_=g, bk=b45: t.matmul(
                        bk.ap, ka[g_][0:65, kt_ * 128:(kt_ + 1) * 128],
                        Qa[0:65, :], start=True, stop=True),
                        wait=[b45.rd, rows_done, ka_neg1], tag=True)
                    b45.wr = evm
                    if kt >= jg * 4:
                        i = kt - jg * 4
                        evm = p.op("v", lambda v, i_=i, bk=b45: v.tensor_tensor(
                            bk.ap[:, i_ * 128:(i_ + 1) * 128],
                            bk.ap[:, i_ * 128:(i_ + 1) * 128], c.maskT, ALU.add),
                            wait=[evm])
                    ee = p.op("s", lambda s, kt_=kt, bk=b45: s.activation(
                        ptb[:, kt_ % 8, :], bk.ap[:, 0:512], AF.Exp, scale=0.125),
                        wait=[evm, pt_done.get(kt % 2)])
                    b45.rd = ee
                    pt_done[kt % 2] = ee
                pt_all = ("s", p.cnt["s"])
                for i in range(4):
                    j = jg * 4 + i
                    lo = half * 8
                    hi = min(j, half * 8 + 7)
                    if hi < lo:
                        continue
                    for kt in range(lo, hi + 1):
                        eav = p.op("t", lambda t, i_=i, kt_=kt, g_=g, lo_=lo, hi_=hi:
                                   t.matmul(
                            avb[i_].ap[:, 0:64], ptb[:, kt_ % 8, i_ * 128:(i_ + 1) * 128],
                            v_tok[:, kt_, g_ * 64:(g_ + 1) * 64],
                            start=(kt_ == lo_), stop=(kt_ == hi_)),
                            wait=([avb[i].rd, pt_all, v_done] if kt == lo else []),
                            tag=(kt == hi))
                    avb[i].wr = eav
                    if half == 0:
                        if j < 8:
                            evo = p.op("v", lambda v, i_=i, h_=h: v.tensor_scalar_mul(
                                aost[:, i_, h_ * 64:(h_ + 1) * 64],
                                avb[i_].ap[:, 0:64], rsl[:, i_:i_ + 1]),
                                wait=[avb[i].wr, rinv_done, aosl_ev.get((i, h))])
                        else:
                            evo = p.op("v", lambda v, i_=i: v.tensor_copy(
                                avacc[:, i_, :], avb[i_].ap[:, 0:64]),
                                wait=[avb[i].wr])
                    else:
                        e6 = p.op("v", lambda v, i_=i: v.tensor_tensor(
                            avacc[:, i_, :], avacc[:, i_, :], avb[i_].ap[:, 0:64],
                            ALU.add), wait=[avb[i].wr])
                        evo = p.op("v", lambda v, i_=i, h_=h: v.tensor_scalar_mul(
                            aost[:, i_, h_ * 64:(h_ + 1) * 64], avacc[:, i_, :],
                            rsl[:, i_:i_ + 1]),
                            wait=[e6, rinv_done, aosl_ev.get((i, h))])
                    avb[i].rd = evo
                    aosl_ev[(i, h)] = evo
            qa_free = ("t", p.cnt["t"])
        ao_done = ("v", p.cnt["v"])
        for i in range(4):
            j = jg * 4 + i
            b6 = c.banks[6]
            aoT_ev = None
            for halfc in range(2):
                for q in range(4):
                    dc = halfc * 4 + q
                    et = p.op("t", lambda t, d=bank_bf(b6, q),
                              s=aost[:, i, dc * 128:(dc + 1) * 128]:
                              t.transpose(d, s, c.identb),
                              wait=[b6.rd, ao_done], tag=(q == 3))
                b6.wr = et
                for q in range(4):
                    dc = halfc * 4 + q
                    ee = p.op("s", lambda s, d=aoT[:, dc, :], sp=bank_bf(b6, q):
                              s.copy(d, sp), wait=[b6.wr], tag=(q == 3))
                b6.rd = ee
            aoT_ev = ee
            if jg == 0 and i == 0:
                c.wo_ld = p.dma("y", lambda y: y.dma_start(
                    wob.rearrange("p cc o -> p (cc o)"),
                    c.s2l0o.rearrange("p a b -> p (a b)")),
                    wait=[c.s2ev["l0o"], ("t", p.cnt["t"])])
            for o in range(2):
                bk = c.banks[o]
                for dc in range(NDC):
                    evm = p.op("t", lambda t, d=dc, o_=o, bk_=bk: t.matmul(
                        bk_.ap, aoT[:, d, :], wob[:, d, o_ * 512:(o_ + 1) * 512],
                        start=(d == 0), stop=(d == 7)),
                        wait=[bk.rd, c.wo_ld, aoT_ev], tag=(dc == 7))
                bk.wr = evm
                evx = p.op("v", lambda v, jj=j, o_=o, bk_=bk: v.tensor_tensor(
                    c.x[:, jj, o_ * 512:(o_ + 1) * 512],
                    c.x[:, jj, o_ * 512:(o_ + 1) * 512], bk_.ap, ALU.add),
                    wait=[bk.wr, xev.get((j, o)), c.last_x_ev])
                bk.rd = evx
                xev[(j, o)] = evx
    c.last_x_ev = ("v", c.p.cnt["v"])



def emit_attn1(c, px):
    p = c.p
    lc = p.dma("y", lambda y: y.dma_start(
        c.curv[0:1, 0:1], c.dram[px + "curv"].rearrange("(a x) -> a x", a=1)))
    b6 = c.banks[6]
    cbv = c.curv.bitcast(BF16)[0:1, 2:3]
    ecst = p.op("v", lambda v: v.tensor_copy(cbv, c.curv[0:1, 0:1]), wait=[lc])
    e1 = p.op("t", lambda t: t.matmul(b6.ap[:, 0:1], c.ones[0:1, 0:128],
                                      cbv, start=True, stop=True),
              wait=[ecst, b6.rd, c.ev_consts])
    b6.wr = e1
    e2 = p.op("s", lambda s: s.copy(c.ccol[:, 0:1], b6.ap[:, 0:1]), wait=[e1])
    b6.rd = e2
    e3 = p.op("s", lambda s: s.activation(c.ccol[:, 1:2], c.ccol[:, 0:1], AF.Sqrt),
              wait=[e2])
    e4 = p.op("v", lambda v: v.reciprocal(c.ccol[:, 2:3], c.ccol[:, 0:1]), wait=[e2])
    sqc = c.ccol[:, 1:2]
    invc = c.ccol[:, 2:3]
    curv_ev = [e3, e4]

    n2c = c.st[:, S_TMP + 4:S_TMP + 5]
    ncc = c.st[:, S_TMP + 5:S_TMP + 6]
    coefc = c.st[:, S_TMP + 6:S_TMP + 7]
    t1c = c.st[:, S_TMP + 7:S_TMP + 8]

    def logmap_tile(j, lnf, tok_bf, ev):
        scr = c.scrf[:, 0:1024]
        a1 = p.op("s", lambda s: s.activation(scr, lnf, AF.Square, accum_out=n2c),
                  wait=[ev])
        a2 = p.op("s", lambda s: s.activation(ncc, n2c, AF.Sqrt), wait=[a1])
        a3 = p.op("v", lambda v: v.tensor_tensor(ncc, ncc, sqc, ALU.mult),
                  wait=[a2] + curv_ev)
        a4 = p.op("s", lambda s: s.activation(t1c, ncc, AF.Square), wait=[a3])
        a5 = p.op("v", lambda v: v.tensor_scalar_add(t1c, t1c, 1.0), wait=[a4])
        a6 = p.op("s", lambda s: s.activation(t1c, t1c, AF.Sqrt), wait=[a5])
        a7 = p.op("v", lambda v: v.tensor_tensor(t1c, t1c, ncc, ALU.add), wait=[a6])
        a8 = p.op("s", lambda s: s.activation(t1c, t1c, AF.Ln), wait=[a7])
        a9 = p.op("v", lambda v: v.reciprocal(coefc, ncc), wait=[a8])
        a10 = p.op("v", lambda v: v.tensor_tensor(coefc, coefc, t1c, ALU.mult),
                   wait=[a9])
        a11 = p.op("v", lambda v: v.tensor_scalar_mul(tok_bf, lnf, coefc), wait=[a10])
        return a11

    emit_ln_all(c, per_tile=logmap_tile)

    qtok = c.pool[:, 0:2048].rearrange("p (j d) -> p j d", j=NT)
    ktok = c.pool[:, 2048:4096].rearrange("p (j d) -> p j d", j=NT)
    Qa = c.pool[0:66, 4096:4608]
    Ka = c.pool[0:66, 4608:6656]
    vch = c.pool[:, 6656:8704].rearrange("p (j d) -> p j d", j=NT)
    ptb = c.pool[:, 8704:12800].rearrange("p (k q) -> p k q", k=8)
    escr = c.pool[:, 12800:14848]
    aoT = c.pool[:, 14848:15872].rearrange("p (cc d) -> p cc d", cc=8)
    qf = c.pool[:, 8704:10752]
    avacc = c.scrf[:, 1024:1280].rearrange("p (i d) -> p i d", i=4)
    ao = c.pool2[:, 0:16384].rearrange("p (j d) -> p j d", j=NT)
    wpb2 = c.pool2[:, 16384:17408].rearrange("p (cc o) -> p cc o", cc=8)
    wob = c.pool2[:, 16384:24576].rearrange("p (cc o) -> p cc o", cc=8)

    bev = []
    for nm, dst in (("wq_b", c.browq), ("wk_b", c.browk), ("wv_b", c.browv),
                    ("wo_b", c.browo)):
        bev.append(p.dma("g", lambda g, d=dst, s=c.wbf[px + nm]: g.dma_start(d, s),
                         wait=[c.fence_ev]))
    brow_ev = bev[-1]

    wqn = c.wbf[px + "wq_w"]; wkn = c.wbf[px + "wk_w"]
    wvn = c.wbf[px + "wv_w"]; won = c.wbf[px + "wo_w"]
    evw = {k: c.ev_prepass[px + k] for k in ("wq_w", "wk_w", "wv_w", "wo_w")}

    n2c = c.st[:, S_TMP + 4:S_TMP + 5]
    ncc = c.st[:, S_TMP + 5:S_TMP + 6]
    coefc = c.st[:, S_TMP + 6:S_TMP + 7]
    t1c = c.st[:, S_TMP + 7:S_TMP + 8]
    n2h = c.st[:, S_AT + 16:S_AT + 32]
    timbf = c.st[:, S_AT + 32:S_AT + 40].bitcast(BF16)
    pairb = c.st[:, S_AT + 40:S_AT + 42].bitcast(BF16)[:, 0:4]
    mcol = c.st[:, S_AT + 42:S_AT + 43]
    mbf = c.st[:, S_AT + 43:S_AT + 44].bitcast(BF16)[:, 0:1]
    biasc = c.st[:, S_AT + 44:S_AT + 45]
    rcol = c.st[:, S_AT + 45:S_AT + 46]
    rinv = c.st[:, S_AT + 46:S_AT + 47]
    rsl = c.st[:, S_AT + 8:S_AT + 12]
    sq_scr = c.scrf[:, 0:64]

    xev = {}
    wp_free = c.fence_ev
    qf_free = c.fence_ev

    def proj_feat_to_tok(cc, wscr, wld_ev, brow, tokdst):
        nonlocal wp_free, qf_free
        qf_free = ("t", p.cnt["t"])
        wp_ld = p.dma("y", lambda y, s_=wscr, c_=cc: y.dma_start(
            wpb2, s_[c_]), wait=[wp_free, wld_ev])
        qf_ev = None
        for tg in range(4):
            bk = c.banks[4 + tg % 2]
            for dc in range(NDC):
                p.op("t", lambda t, d=dc, tg_=tg, bk_=bk: t.matmul(
                    bk_.ap, wpb2[:, d, :],
                    xt_view(c, d, tg_ * 512, tg_ * 512 + 512),
                    start=(d == 0), stop=False),
                    wait=[bk.rd, wp_ld, c.xt_ev], tag=False)
            evm = p.op("t", lambda t, c_=cc, bk_=bk, br=brow: t.matmul(
                bk_.ap, br[0:1, c_ * 128:(c_ + 1) * 128], c.ones[0:1, 0:512],
                start=False, stop=True), wait=[brow_ev], tag=True)
            bk.wr = evm
            ee = p.op("s", lambda s, tg_=tg, bk_=bk: s.copy(
                qf[:, tg_ * 512:(tg_ + 1) * 512], bk_.ap), wait=[bk.wr, qf_free])
            bk.rd = ee
            qf_ev = ee
        wp_free = ("t", p.cnt["t"])
        tev = None
        b7 = c.banks[7]
        for jq in range(4):
            for q in range(4):
                j = jq * 4 + q
                et = p.op("t", lambda t, d=bank_bf(b7, q),
                          s=qf[:, j * 128:(j + 1) * 128]: t.transpose(d, s, c.identb),
                          wait=[b7.rd, qf_ev], tag=(q == 3))
            b7.wr = et
            for q in range(4):
                j = jq * 4 + q
                ec = p.op("s", lambda s, jj=j, sp=bank_bf(b7, q), td=tokdst: s.copy(
                    td[:, jj, :], sp), wait=[b7.wr], tag=(q == 3))
            b7.rd = ec
            tev = ec
        qf_free = ("t", p.cnt["t"])
        return tev

    nb2 = c.st[:, S_AT + 16:S_AT + 32]
    ncb = c.st[:, S_AT + 64:S_AT + 80]
    e1b = c.st[:, S_AT + 80:S_AT + 96]
    e2b = c.st[:, S_AT + 96:S_AT + 112]
    coefb = c.st[:, S_AT + 112:S_AT + 128]
    timf = c.st[:, S_AT + 128:S_AT + 144]
    timbfQ = c.st[:, S_AT + 144:S_AT + 152].bitcast(BF16)
    timbfK = c.st[:, S_AT + 32:S_AT + 40].bitcast(BF16)
    scr2 = c.scrf[:, 0:1024].rearrange("p (j d) -> p j d", j=NT)
    exm_prev = [None]

    war_tim = {"K": None, "Q": None}

    def expmap_batch(tok, hd, timdst, neg, ev_tok, war_key):
        sl3 = tok[:, :, hd * 64:(hd + 1) * 64]
        a1 = p.op("v", lambda v: v.tensor_tensor(scr2, sl3, sl3, ALU.mult),
                  wait=[ev_tok, exm_prev[0], war_tim[war_key]])
        a2 = p.op("v", lambda v: v.tensor_reduce(
            nb2.rearrange("p (j o) -> p j o", o=1), scr2, AX.X, ALU.add), wait=[a1])
        a3 = p.op("s", lambda s: s.activation(ncb, nb2, AF.Sqrt), wait=[a2])
        a4 = p.op("v", lambda v: v.tensor_scalar_mul(ncb, ncb, sqc),
                  wait=[a3] + curv_ev)
        a5 = p.op("s", lambda s: s.activation(e1b, ncb, AF.Exp), wait=[a4])
        a6 = p.op("s", lambda s: s.activation(e2b, ncb, AF.Exp, scale=-1.0), wait=[a4])
        a7 = p.op("v", lambda v: v.tensor_tensor(e1b, e1b, e2b, ALU.subtract),
                  wait=[a5, a6])
        a8 = p.op("v", lambda v: v.reciprocal(e2b, ncb), wait=[a6, a7])
        a9 = p.op("v", lambda v: v.tensor_tensor(coefb, e1b, e2b, ALU.mult),
                  wait=[a7, a8])
        a10 = p.op("v", lambda v: v.tensor_scalar_mul(coefb, coefb, 0.5), wait=[a9])
        ah = a10
        for j in range(NT):
            ah = p.op("v", lambda v, jj=j: v.tensor_scalar_mul(
                tok[:, jj, hd * 64:(hd + 1) * 64], tok[:, jj, hd * 64:(hd + 1) * 64],
                coefb[:, jj:jj + 1]), wait=[a10] if j == 0 else [ah])
        a12 = p.op("v", lambda v: v.tensor_tensor(scr2, sl3, sl3, ALU.mult), wait=[ah])
        a13 = p.op("v", lambda v: v.tensor_reduce(
            nb2.rearrange("p (j o) -> p j o", o=1), scr2, AX.X, ALU.add), wait=[a12])
        a14 = p.op("v", lambda v: v.tensor_single_scalar(timf, nb2, invc, ALU.add),
                   wait=[a13] + curv_ev)
        a15 = p.op("s", lambda s: s.activation(timf, timf, AF.Sqrt), wait=[a14])
        a16 = p.op("v", lambda v, sc=-1.0 if neg else 1.0: v.tensor_scalar_mul(
            timdst, timf, sc), wait=[a15])
        exm_prev[0] = a16
        return a16

    for cc in range(8):
        tq = proj_feat_to_tok(cc, c.s2l1q, c.s2ev[("l1q", cc)], c.browq, qtok)
        tk = proj_feat_to_tok(cc, c.s2l1k, c.s2ev[("l1k", cc)], c.browk, ktok)
        wv_ld = p.dma("y", lambda y, c_=cc: y.dma_start(
            wpb2, c.s2l1v[c_]), wait=[wp_free, c.s2ev[("l1v", cc)]])
        vev = None
        for j in range(NT):
            bk = c.banks[4 + j % 2]
            for dc in range(NDC):
                p.op("t", lambda t, d=dc, jj=j, bk_=bk: t.matmul(
                    bk_.ap[:, 0:128], xt_view(c, d, jj * 128, (jj + 1) * 128),
                    wpb2[:, d, :], start=(d == 0), stop=False),
                    wait=[bk.rd, wv_ld, c.xt_ev], tag=False)
            evm = p.op("t", lambda t, bk_=bk, c_=cc: t.matmul(
                bk_.ap[:, 0:128], c.ones[0:1, 0:128],
                c.browv[0:1, c_ * 128:(c_ + 1) * 128], start=False, stop=True),
                wait=[brow_ev], tag=True)
            bk.wr = evm
            vev = p.op("s", lambda s, jj=j, bk_=bk: s.copy(
                vch[:, jj, :], bk_.ap[:, 0:128]), wait=[bk.wr])
            bk.rd = vev
        wp_free = ("t", p.cnt["t"])
        v_done = vev
        for hd in range(2):
            h = cc * 2 + hd
            ek = expmap_batch(ktok, hd, timbfK, True, tk, "K")
            eka0 = p.op("g", lambda g: g.memset(Ka[64:66, :], -1.0), wait=[ek])
            kev = None
            for j in range(NT):
                sl = ktok[:, j, hd * 64:(hd + 1) * 64]
                et = p.op("t", lambda t, d=bank_bf(b6, 0)[0:64, :], sl_=sl:
                          t.transpose(d, sl_, c.identb), wait=[ek, b6.rd])
                b6.wr = et
                ec = p.op("s", lambda s, jj=j, d=bank_bf(b6, 0)[0:64, :]: s.copy(
                    Ka[0:64, jj * 128:(jj + 1) * 128], d), wait=[et])
                b6.rd = ec
                et2 = p.op("t", lambda t, jj=j, d=bank_bf(b6, 1)[0:1, :]: t.transpose(
                    d, timbfK[:, jj:jj + 1], c.identb), wait=[ek, b6.rd])
                b6.wr = et2
                ec2 = p.op("s", lambda s, jj=j, d=bank_bf(b6, 1)[0:1, :]: s.copy(
                    Ka[64:65, jj * 128:(jj + 1) * 128], d), wait=[et2, eka0])
                b6.rd = ec2
                kev = ec2
            ka_done = kev
            war_tim["K"] = ka_done
            eq = expmap_batch(qtok, hd, timbfQ, False, tq, "Q")
            for jg in range(4):
                qa_evs = []
                for i in range(4):
                    j = jg * 4 + i
                    sl = qtok[:, j, hd * 64:(hd + 1) * 64]
                    et = p.op("t", lambda t, d=bank_bf(b6, 0)[0:64, :], sl_=sl:
                              t.transpose(d, sl_, c.identb), wait=[eq, b6.rd])
                    b6.wr = et
                    ec = p.op("s", lambda s, i_=i, d=bank_bf(b6, 0)[0:64, :]: s.copy(
                        Qa[0:64, i_ * 128:(i_ + 1) * 128], d), wait=[et])
                    b6.rd = ec
                    et2 = p.op("t", lambda t, jj=j, d=bank_bf(b6, 1)[0:1, :]:
                               t.transpose(d, timbfQ[:, jj:jj + 1], c.identb),
                               wait=[eq, b6.rd])
                    b6.wr = et2
                    ec2 = p.op("s", lambda s, i_=i, d=bank_bf(b6, 1)[0:1, :]: s.copy(
                        Qa[64:65, i_ * 128:(i_ + 1) * 128], d), wait=[et2])
                    b6.rd = ec2
                    qa_evs.append(ec2)
                qa_done = qa_evs[-1]
                row_evs = []
                stat_free = None
                for i in range(4):
                    j = jg * 4 + i
                    off = 0 if i % 2 == 0 else 4
                    reg = c.pa if off == 0 else c.pb
                    wl2 = [qa_done, ka_done, stat_free] + \
                          [c.banks[off + q].rd for q in range(4)]
                    for kb in range(4):
                        evm = p.op("t", lambda t, kb_=kb, i_=i, r_=reg: t.matmul(
                            r_[:, kb_ * 512:(kb_ + 1) * 512],
                            Qa[0:65, i_ * 128:(i_ + 1) * 128],
                            Ka[0:65, kb_ * 512:(kb_ + 1) * 512],
                            start=True, stop=True), wait=wl2, tag=(kb == 3))
                    for kb in range(4):
                        c.banks[off + kb].wr = evm
                    e1 = p.op("v", lambda v, r_=reg: v.tensor_reduce(
                        mcol, r_[:, 0:2048], AX.X, ALU.min), wait=[evm])
                    e2 = p.op("v", lambda v: v.tensor_copy(mbf, mcol), wait=[e1])
                    e3 = p.op("s", lambda s: s.mul(biasc, mbf, 0.125), wait=[e2])
                    e4 = p.op("s", lambda s, r_=reg: s.activation(
                        escr, r_[:, 0:2048], AF.Exp, scale=-0.125, bias=biasc,
                        accum_out=rcol), wait=[e3])
                    for kb in range(4):
                        c.banks[off + kb].rd = e4
                    e5 = p.op("v", lambda v: v.reciprocal(rinv, rcol), wait=[e4])
                    p.op("v", lambda v, i_=i: v.tensor_copy(
                        rsl[:, i_:i_ + 1], rinv), wait=[e5])
                    e6 = p.op("v", lambda v, jj=j: v.tensor_copy(
                        pairb[:, 2:3], timbfQ[:, jj:jj + 1]), wait=[eq])
                    e7 = p.op("v", lambda v: v.tensor_copy(pairb[:, 3:4], mbf),
                              wait=[e2, e6])
                    et = p.op("t", lambda t, d=bank_bf(b6, 2)[0:2, :]: t.transpose(
                        d, pairb[:, 2:4], c.identb), wait=[e7, b6.rd])
                    b6.wr = et
                    erow = p.op("s", lambda s, i_=i, d=bank_bf(b6, 2)[0:2, :]: s.copy(
                        Qa[64:66, i_ * 128:(i_ + 1) * 128], d), wait=[et, e4])
                    b6.rd = erow
                    row_evs.append(erow)
                    stat_free = erow
                rows_done = row_evs[-1]
                war_tim["Q"] = rows_done
                rinv_done = ("v", p.cnt["v"])
                avb = [c.banks[q] for q in range(4)]
                for half in range(2):
                    kts = list(range(half * 8, half * 8 + 8))
                    pt_done = {}
                    for kt in kts:
                        b45 = c.banks[4 + kt % 2]
                        evm = p.op("t", lambda t, kt_=kt, bk=b45: t.matmul(
                            bk.ap, Ka[0:66, kt_ * 128:(kt_ + 1) * 128], Qa[0:66, :],
                            start=True, stop=True),
                            wait=[b45.rd, rows_done], tag=True)
                        b45.wr = evm
                        ee = p.op("s", lambda s, kt_=kt, bk=b45: s.activation(
                            ptb[:, kt_ % 8, :], bk.ap, AF.Exp, scale=-0.125),
                            wait=[evm, pt_done.get(kt % 2)])
                        b45.rd = ee
                        pt_done[kt % 2] = ee
                    pt_all = ("s", p.cnt["s"])
                    for i in range(4):
                        lo, hi = half * 8, half * 8 + 7
                        for kt in range(lo, hi + 1):
                            eav = p.op("t", lambda t, i_=i, kt_=kt, hd_=hd, lo_=lo,
                                       hi_=hi: t.matmul(
                                avb[i_].ap[:, 0:64],
                                ptb[:, kt_ % 8, i_ * 128:(i_ + 1) * 128],
                                vch[:, kt_, hd_ * 64:(hd_ + 1) * 64],
                                start=(kt_ == lo_), stop=(kt_ == hi_)),
                                wait=([avb[i].rd, pt_all, v_done] if kt == lo
                                      else []),
                                tag=(kt == hi))
                        avb[i].wr = eav
                        if half == 0:
                            evo = p.op("v", lambda v, i_=i: v.tensor_copy(
                                avacc[:, i_, :], avb[i_].ap[:, 0:64]),
                                wait=[avb[i].wr])
                        else:
                            e6 = p.op("v", lambda v, i_=i: v.tensor_tensor(
                                avacc[:, i_, :], avacc[:, i_, :],
                                avb[i_].ap[:, 0:64], ALU.add), wait=[avb[i].wr])
                            j = jg * 4 + i
                            evo = p.op("v", lambda v, i_=i, jj=j, h_=h:
                                       v.tensor_scalar_mul(
                                ao[:, jj, h_ * 64:(h_ + 1) * 64], avacc[:, i_, :],
                                rsl[:, i_:i_ + 1]), wait=[e6, rinv_done])
                        avb[i].rd = evo
    ao_done = ("v", p.cnt["v"])
    wo_ld = p.dma("y", lambda y: y.dma_start(
        wob.rearrange("p cc o -> p (cc o)"), c.s2l1o.rearrange("p a b -> p (a b)")),
        wait=[c.s2ev["l1o"], wp_free, ("t", p.cnt["t"])])
    b7 = c.banks[7]
    for j in range(NT):
        aoT_ev = None
        for halfc in range(2):
            for q in range(4):
                dc = halfc * 4 + q
                et = p.op("t", lambda t, d=bank_bf(b7, q),
                          s=ao[:, j, dc * 128:(dc + 1) * 128]:
                          t.transpose(d, s, c.identb),
                          wait=[b7.rd, ao_done], tag=(q == 3))
            b7.wr = et
            for q in range(4):
                dc = halfc * 4 + q
                ee = p.op("s", lambda s, d=aoT[:, dc, :], sp=bank_bf(b7, q):
                          s.copy(d, sp), wait=[b7.wr], tag=(q == 3))
            b7.rd = ee
        aoT_ev = ee
        for o in range(2):
            bk = c.banks[o]
            for dc in range(NDC):
                p.op("t", lambda t, d=dc, o_=o, bk_=bk: t.matmul(
                    bk_.ap, aoT[:, d, :], wob[:, d, o_ * 512:(o_ + 1) * 512],
                    start=(d == 0), stop=False),
                    wait=[bk.rd, wo_ld, aoT_ev], tag=False)
            evm = p.op("t", lambda t, o_=o, bk_=bk: t.matmul(
                bk_.ap, c.ones[0:1, 0:128], c.browo[0:1, o_ * 512:(o_ + 1) * 512],
                start=False, stop=True), wait=[brow_ev], tag=True)
            bk.wr = evm
            evx = p.op("v", lambda v, jj=j, o_=o, bk_=bk: v.tensor_tensor(
                c.x[:, jj, o_ * 512:(o_ + 1) * 512],
                c.x[:, jj, o_ * 512:(o_ + 1) * 512], bk_.ap, ALU.add),
                wait=[bk.wr, xev.get((j, o)), c.last_x_ev])
            bk.rd = evx
            xev[(j, o)] = evx
    c.last_x_ev = ("v", c.p.cnt["v"])


SUB_PREFIX = {"f0a": "L0_ffn_", "f0b": "L0_ffn_", "a0": "L0_attn_",
              "f1a": "L1_ffn_", "f1b": "L1_ffn_", "a1": "L1_attn_"}
SUB_WNAMES = {
    "f0a": ["router_weight", "gate_up_proj", "down_proj", "shared_gate_w",
            "shared_up_w", "shared_down_w", "shared_expert_gate_w"],
    "a0": ["W_q", "W_k", "W_v", "W_o"],
    "f1a": ["w_router", "w_up", "w_gate", "w_down", "ws_up", "ws_gate", "ws_down"],
    "a1": ["curv", "wq_w", "wq_b", "wk_w", "wk_b", "wv_w", "wv_b", "wo_w", "wo_b"],
}
SUB_WNAMES["f0b"] = SUB_WNAMES["f0a"]
SUB_WNAMES["f1b"] = SUB_WNAMES["f1a"]
EMIT = {"f0a": lambda c: emit_ffn0(c, "L0_ffn_"),
        "f0b": lambda c: emit_ffn0(c, "L0_ffn_"),
        "a0": lambda c: emit_attn0(c, "L0_attn_"),
        "f1a": lambda c: emit_ffn1(c, "L1_ffn_"),
        "f1b": lambda c: emit_ffn1(c, "L1_ffn_"),
        "a1": lambda c: emit_attn1(c, "L1_attn_")}

INPUT_SHAPES = {
    "L0_attn_W_q": [1024, 1024], "L0_attn_W_k": [256, 1024],
    "L0_attn_W_v": [256, 1024], "L0_attn_W_o": [1024, 1024],
    "L0_ffn_router_weight": [8, 1024], "L0_ffn_gate_up_proj": [8, 1024, 1024],
    "L0_ffn_down_proj": [8, 1024, 512], "L0_ffn_shared_gate_w": [2048, 1024],
    "L0_ffn_shared_up_w": [2048, 1024], "L0_ffn_shared_down_w": [1024, 2048],
    "L0_ffn_shared_expert_gate_w": [1, 1024],
    "L1_attn_curv": [], "L1_attn_wq_w": [1024, 1024], "L1_attn_wq_b": [1024],
    "L1_attn_wk_w": [1024, 1024], "L1_attn_wk_b": [1024],
    "L1_attn_wv_w": [1024, 1024], "L1_attn_wv_b": [1024],
    "L1_attn_wo_w": [1024, 1024], "L1_attn_wo_b": [1024],
    "L1_ffn_w_router": [1024, 8], "L1_ffn_w_up": [7, 1024, 4096],
    "L1_ffn_w_gate": [7, 1024, 4096], "L1_ffn_w_down": [7, 4096, 1024],
    "L1_ffn_ws_up": [1024, 4096], "L1_ffn_ws_gate": [1024, 4096],
    "L1_ffn_ws_down": [4096, 1024],
}


def build_group(subs, taps=()):
    nc = bass.Bass("TRN2", target_bir_lowering=False, debug=False)
    p = Prog(nc)
    c = alloc_ctx(nc, p)
    c._taps = list(taps)
    needed = []
    for s in subs:
        for w in SUB_WNAMES[s]:
            full = SUB_PREFIX[s] + w
            if full not in needed:
                needed.append(full)
    for full in needed:
        shp = INPUT_SHAPES[full]
        if full == "L1_attn_curv":
            c.dram[full] = nc.dram_tensor(full, [1], F32, kind="ExternalInput").ap()
        else:
            c.dram[full] = nc.dram_tensor(full, shp, F32, kind="ExternalInput").ap()
    c.dram["x_in"] = nc.dram_tensor("x_in", [T, D], F32, kind="ExternalInput").ap()
    c.dram["x_out"] = nc.dram_tensor("x_out", [T, D], F32, kind="ExternalOutput").ap()
    if any(s == "a0" for s in subs):
        c.dram["rope_cs"] = nc.dram_tensor("rope_cs", [T, 1024], F32,
                                           kind="ExternalInput").ap()
    emit_consts(c)
    emit_prepass_casts(c, subs)
    exl = p.dma("y", lambda y: y.dma_start(
        c.x, c.dram["x_in"].rearrange("(j p) d -> p j d", p=128)))
    c.last_x_ev = exl
    c.fence_ev = None
    pre_done = set()
    for s in subs:
        fence_w = [(k, p.cnt[k]) for k in ("t", "v", "s", "g")]
        c.fence_ev = p.op("s", lambda s: s.nop(), wait=fence_w)
        fam = SUB_PREFIX[s]
        if fam not in pre_done:
            pre_done.add(fam)
            emit_prepass_family(c, s)
        EMIT[s](c)
    sev = p.dma("y", lambda y: y.dma_start(
        c.dram["x_out"].rearrange("(j p) d -> p j d", p=128), c.x),
        wait=[c.last_x_ev])
    for name, getter in c._taps:
        ap = getter(c)
        shp = list(ap.shape)
        dt_ = ap.dtype
        od = nc.dram_tensor("tap_" + name, shp, dt_, kind="ExternalOutput").ap()
        allw = [(k, p.cnt[k]) for k in ("t", "v", "s", "g")]
        sev = p.dma("y", lambda y, o=od, a=ap: y.dma_start(o, a), wait=allw)
    p.op("y", lambda y: y.nop(), wait=[sev], tag=False)
    p.finalize()
    nc.finalize()
    return nc


GROUPS = [["f0a", "a0", "f0b", "f1a", "a1", "f1b"]]

_ROPE = None


def _rope_table():
    global _ROPE
    if _ROPE is None:
        theta = 1.0 / 10000.0 ** (np.arange(512, dtype=np.float64) / 512.0)
        ang = np.arange(T, dtype=np.float64)[:, None] * theta[None, :]
        _ROPE = np.ascontiguousarray(
            np.concatenate([np.cos(ang), np.sin(ang)], 1).astype(np.float32))
    return _ROPE


_CACHE = {}


def kernel(**inputs):
    X = np.ascontiguousarray(np.asarray(inputs["L0_attn_X"], np.float32))
    xs = [np.ascontiguousarray(X[i]) for i in range(NCORE)]
    for subs in GROUPS:
        key = tuple(subs)
        if key not in _CACHE:
            _CACHE[key] = build_group(subs)
        nc = _CACHE[key]
        needed = []
        for s in subs:
            for w in SUB_WNAMES[s]:
                full = SUB_PREFIX[s] + w
                if full not in needed:
                    needed.append(full)
        base = {}
        for full in needed:
            a = np.ascontiguousarray(np.asarray(inputs[full], np.float32))
            if full == "L1_attn_curv":
                a = a.reshape(1)
            base[full] = a
        if any(s == "a0" for s in subs):
            base["rope_cs"] = _rope_table()
        in_maps = [dict(base, x_in=xs[i]) for i in range(NCORE)]
        res = run_bass_kernel_spmd(nc, in_maps, core_ids=list(range(NCORE)))
        xs = [np.asarray(res.results[i]["x_out"], np.float32) for i in range(NCORE)]
    return np.stack(xs, 0)


# version 09
# speedup vs baseline: 1.3810x; 1.0199x over previous
import numpy as np
import ml_dtypes
import concourse.bass as bass
import concourse.mybir as mybir
from concourse.bass_utils import run_bass_kernel_spmd

F32 = mybir.dt.float32
BF16 = mybir.dt.bfloat16
AF = mybir.ActivationFunctionType
ALU = mybir.AluOpType
AX = mybir.AxisListType

D = 1024
T = 2048
NT = 16
NDC = 8
NCORE = 8

S_MEAN, S_NEGM, S_VAR, S_RSTD = 0, 16, 32, 48
S_CW = 64
S_SEG = 192
S_TMP = 208
S_AT = 256


class Prog:
    def __init__(self, nc):
        self.nc = nc
        self.qs = {k: [] for k in "tvsgy"}
        keys = ["t", "v", "s", "g", "y", "gd", "yd", "sd"]
        self.sems = {k: nc.alloc_semaphore("sem_" + k) for k in keys}
        self.cnt = {k: 0 for k in keys}
        self.waited = {e: {} for e in "tvsgy"}

    def _push(self, eng, fn, wait, inc):
        wl = []
        seen = self.waited[eng]
        for w in wait:
            if w is None:
                continue
            k, v = w
            if seen.get(k, 0) >= v:
                continue
            seen[k] = v
            wl.append((k, v))
        self.qs[eng].append((fn, wl, inc))

    def op(self, eng, fn, wait=(), tag=True):
        ev = None
        inc = None
        if tag:
            self.cnt[eng] += 1
            inc = (eng, 1)
            ev = (eng, self.cnt[eng])
        self._push(eng, fn, wait, inc)
        return ev

    def dma(self, eng, fn, wait=()):
        k = eng + "d"
        self.cnt[k] += 16
        self._push(eng, fn, wait, (k, 16))
        return (k, self.cnt[k])

    def finalize(self):
        nc = self.nc
        sems = self.sems
        for e in "tvsgy":
            for _, wl, _ in self.qs[e]:
                for k, v in wl:
                    assert v <= self.cnt[k], ("wait beyond plan", e, k, v, self.cnt[k])

        def runner(q):
            def f(eng):
                for fn, wl, inc in q:
                    for k, v in wl:
                        eng.wait_ge(sems[k], v)
                    inst = fn(eng)
                    if inc is not None:
                        inst.then_inc(sems[inc[0]], inc[1])
            return f

        with nc.Block() as blk:
            blk.tensor(runner(self.qs["t"]))
            blk.vector(runner(self.qs["v"]))
            blk.scalar(runner(self.qs["s"]))
            blk.gpsimd(runner(self.qs["g"]))
            blk.sync(runner(self.qs["y"]))


class Bank:
    __slots__ = ("ap", "wr", "rd")

    def __init__(self, ap):
        self.ap = ap
        self.wr = None
        self.rd = None


def bank_bf(bank, slot):
    return bank.ap[:, slot * 64:(slot + 1) * 64].bitcast(BF16)


class Ctx:
    pass


def alloc_ctx(nc, p):
    c = Ctx()
    c.nc = nc
    c.p = p
    sb = lambda n, s, d: nc.alloc_sbuf_tensor(n, s, d).ap()
    c.x = sb("x_res", [128, NT, D], F32)
    c.xt = sb("xt_fm", [128, 8 * T], BF16)
    c.pool = sb("pool", [128, 16384], BF16)
    c.pool2 = sb("pool2", [128, 24576], BF16)
    c.scrf = sb("scrf", [128, 3072], F32)
    c.st = sb("stats", [128, 512], F32)
    c.cstf = sb("cstf", [128, 384], F32)
    c.cstb = sb("cstb", [128, 768], BF16)
    c.browq = sb("browq", [1, 1024], BF16)
    c.browk = sb("browk", [1, 1024], BF16)
    c.browv = sb("browv", [1, 1024], BF16)
    c.browo = sb("browo", [1, 1024], BF16)
    c.wr8 = sb("wr8", [128, 80], BF16)
    c.rt8 = sb("rt8", [9, 1024], BF16)
    c.curv = sb("curv", [1, 4], F32)
    c.ccol = sb("ccol", [128, 4], F32)
    c.pa = nc.alloc_psum_tensor("pa", [128, 2048], F32).ap()
    c.pb = nc.alloc_psum_tensor("pb", [128, 2048], F32).ap()
    c.banks = [Bank(c.pa[:, i * 512:(i + 1) * 512]) for i in range(4)] + \
              [Bank(c.pb[:, i * 512:(i + 1) * 512]) for i in range(4)]
    c.stg = sb("stg", [128, 2048], BF16)
    c.stg_free = [None, None]
    c.stg_par = 0
    c.s2ev = {}
    c.dram = {}
    c.wbf = {}
    c.ev_prepass = {}
    c.last_x_ev = None
    c.xt_ev = None
    c.fence_ev = None
    return c


def emit_consts(c):
    p = c.p
    def mk_ident(ap):
        e1 = p.op("g", lambda g, a=ap: g.memset(a, 0.0))
        return p.op("g", lambda g, a=ap: g.affine_select(
            out=a, in_=a, compare_op=ALU.not_equal, fill=1.0, base=0,
            pattern=[[-1, 128]], channel_multiplier=1), wait=[e1])
    mk_ident(c.cstf[:, 0:128])
    e1 = p.op("g", lambda g: g.memset(c.cstf[:, 128:256], 0.0))
    p.op("g", lambda g: g.affine_select(
        out=c.cstf[:, 128:256], in_=c.cstf[:, 128:256], compare_op=ALU.is_ge,
        fill=-1e9, base=0, pattern=[[-1, 128]], channel_multiplier=1), wait=[e1])
    e2 = p.op("g", lambda g: g.memset(c.cstf[:, 256:384], 0.0))
    p.op("g", lambda g: g.affine_select(
        out=c.cstf[:, 256:384], in_=c.cstf[:, 256:384], compare_op=ALU.is_ge,
        fill=-1e9, base=0, pattern=[[1, 128]], channel_multiplier=-1), wait=[e2])
    mk_ident(c.cstb[:, 0:128])
    ev = p.op("g", lambda g: g.memset(c.cstb[:, 128:640], 1.0))
    c.ev_consts = ev
    c.identf = c.cstf[:, 0:128]
    c.identb = c.cstb[:, 0:128]
    c.maskQ = c.cstf[:, 128:256]
    c.maskT = c.cstf[:, 256:384]
    c.ones = c.cstb[:, 128:640]


def emit_cast(c, full_names):
    p = c.p
    for n in full_names:
        if n in c.wbf:
            continue
        src = c.dram[n]
        shp = list(src.shape)
        if len(shp) == 1:
            shp = [1, shp[0]]
            src = src.rearrange("(a d) -> a d", a=1)
        dst = c.nc.dram_tensor("bf_" + n, shp, BF16).ap()
        if len(shp) == 3:
            evs = []
            for e in range(shp[0]):
                evs.append(p.dma("g", lambda g, a=dst[e], b=src[e]: g.dma_start(a, b)))
                c.ev_prepass[(n, e)] = evs[-1]
            c.ev_prepass[n] = evs[-1]
        else:
            c.ev_prepass[n] = p.dma("g", lambda g, a=dst, b=src: g.dma_start(a, b))
        c.wbf[n] = dst


def emit_rearrange_cast(c, key, src, dst, ev=None):
    c.s2ev[key] = c.p.dma("g", lambda g, a=dst, b=src: g.dma_start(a, b),
                          wait=[ev])


def emit_pre_transpose(c, key, pieces):
    p = c.p
    ev = None
    for src, dst, wev in pieces:
        b = c.stg_par % 2
        R = src.shape[0]
        stga = c.stg[:, b * 1024: b * 1024 + R]
        e1 = p.dma("y", lambda s, a=stga, b_=src: s.dma_start_transpose(a, b_),
                   wait=[wev, c.stg_free[b]])
        e2 = p.dma("y", lambda s, a=dst, b_=stga: s.dma_start(a, b_), wait=[e1])
        c.stg_free[b] = e2
        c.stg_par += 1
        ev = e2
    c.s2ev[key] = ev


def emit_prepass_casts(c, subs):
    done = set(subs)
    if "f0a" in done or "f0b" in done:
        px = "L0_ffn_"
        emit_cast(c, [px + n for n in ("router_weight", "shared_expert_gate_w",
                                       "gate_up_proj", "down_proj",
                                       "shared_gate_w", "shared_up_w",
                                       "shared_down_w")])
    if "a0" in done:
        emit_cast(c, ["L0_attn_" + n for n in ("W_q", "W_k", "W_v", "W_o")])
    if "f1a" in done or "f1b" in done:
        px = "L1_ffn_"
        emit_cast(c, [px + "w_router"])
        nc = c.nc
        c.s2gug = nc.dram_tensor("s2gug", [8, 16, 128, 2, 8, 256], BF16).ap()
        c.s2wd = nc.dram_tensor("s2wd", [8, 4, 128, 8, 1024], BF16).ap()
        for ei in range(8):
            if ei < 7:
                usrc = c.dram[px + "w_up"][ei]
                gsrc = c.dram[px + "w_gate"][ei]
                dsrc = c.dram[px + "w_down"][ei]
            else:
                usrc = c.dram[px + "ws_up"]
                gsrc = c.dram[px + "ws_gate"]
                dsrc = c.dram[px + "ws_down"]
            for hp in range(16):
                emit_rearrange_cast(c, ("gug", ei), usrc[:, hp * 256:(hp + 1) * 256]
                                    .rearrange("(cc p) h -> p cc h", p=128),
                                    c.s2gug[ei, hp][:, 0])
                emit_rearrange_cast(c, ("gug", ei), gsrc[:, hp * 256:(hp + 1) * 256]
                                    .rearrange("(cc p) h -> p cc h", p=128),
                                    c.s2gug[ei, hp][:, 1])
            for q in range(4):
                emit_rearrange_cast(c, ("wd", ei), dsrc[q * 1024:(q + 1) * 1024, :]
                                    .rearrange("(hh p) o -> p hh o", p=128),
                                    c.s2wd[ei, q])
    if "a1" in done:
        emit_cast(c, ["L1_attn_" + n for n in ("wq_w", "wk_w", "wv_w", "wo_w",
                                               "wq_b", "wk_b", "wv_b", "wo_b")])


def emit_prepass_family(c, s):
    nc = c.nc
    dram = lambda n, sh: nc.dram_tensor(n, sh, BF16).ap()
    if s in ("f0a", "f0b"):
        px = "L0_ffn_"
        c.s2gu0 = dram("s2gu0", [8, 128, 8, 1024])
        c.s2dn0 = dram("s2dn0", [8, 128, 4, 1024])
        c.s2sg0 = dram("s2sg0", [2, 4, 128, 8, 512])
        c.s2sd0 = dram("s2sd0", [128, 16, 1024])
        gu_bf = c.wbf[px + "gate_up_proj"]
        dn_bf = c.wbf[px + "down_proj"]
        for e in range(8):
            emit_pre_transpose(c, ("gu0", e), [
                (gu_bf[e][:, dc * 128:(dc + 1) * 128], c.s2gu0[e][:, dc, :],
                 c.ev_prepass[(px + "gate_up_proj", e)]) for dc in range(8)])
            emit_pre_transpose(c, ("dn0", e), [
                (dn_bf[e][:, ic * 128:(ic + 1) * 128], c.s2dn0[e][:, ic, :],
                 c.ev_prepass[(px + "down_proj", e)]) for ic in range(4)])
        for part, nm in ((0, "shared_gate_w"), (1, "shared_up_w")):
            for grp in range(4):
                emit_pre_transpose(c, ("sg0", part, grp), [
                    (c.wbf[px + nm][grp * 512:(grp + 1) * 512,
                                    dc * 128:(dc + 1) * 128],
                     c.s2sg0[part, grp][:, dc, :], c.ev_prepass[px + nm])
                    for dc in range(8)])
        emit_pre_transpose(c, "sd0", [
            (c.wbf[px + "shared_down_w"][:, ic * 128:(ic + 1) * 128],
             c.s2sd0[:, ic, :], c.ev_prepass[px + "shared_down_w"])
            for ic in range(16)])
    elif s == "a0":
        px = "L0_attn_"
        c.s2l0q = dram("s2l0q", [128, 8, 1024])
        c.s2l0k = dram("s2l0k", [128, 8, 256])
        c.s2l0v = dram("s2l0v", [128, 8, 256])
        c.s2l0o = dram("s2l0o", [128, 8, 1024])
        for key, nm, scr in (("l0k", "W_k", c.s2l0k), ("l0v", "W_v", c.s2l0v),
                             ("l0q", "W_q", c.s2l0q), ("l0o", "W_o", c.s2l0o)):
            emit_pre_transpose(c, key, [
                (c.wbf[px + nm][:, dc * 128:(dc + 1) * 128], scr[:, dc, :],
                 c.ev_prepass[px + nm]) for dc in range(8)])
    elif s in ("f1a", "f1b"):
        pass
    elif s == "a1":
        px = "L1_attn_"
        c.s2l1q = dram("s2l1q", [8, 128, 8, 128])
        c.s2l1k = dram("s2l1k", [8, 128, 8, 128])
        c.s2l1v = dram("s2l1v", [8, 128, 8, 128])
        c.s2l1o = dram("s2l1o", [128, 8, 1024])
        for key, nm, scr in (("l1q", "wq_w", c.s2l1q), ("l1k", "wk_w", c.s2l1k),
                             ("l1v", "wv_w", c.s2l1v)):
            for cc in range(8):
                emit_pre_transpose(c, (key, cc), [
                    (c.wbf[px + nm][cc * 128:(cc + 1) * 128,
                                    dc * 128:(dc + 1) * 128],
                     scr[cc][:, dc, :], c.ev_prepass[px + nm]) for dc in range(8)])
        emit_pre_transpose(c, "l1o", [
            (c.wbf[px + "wo_w"][:, dc * 128:(dc + 1) * 128], c.s2l1o[:, dc, :],
             c.ev_prepass[px + "wo_w"]) for dc in range(8)])



def emit_tok_to_feat(c, j, src_bf, bank, wait=()):
    p = c.p
    ev = None
    for half in range(2):
        tw = [bank.rd] + list(wait)
        for i in range(4):
            dc = half * 4 + i
            tev = p.op("t", lambda t, d=bank_bf(bank, i),
                       s=src_bf[:, dc * 128:(dc + 1) * 128]:
                       t.transpose(d, s, c.identb), wait=tw, tag=(i == 3))
        bank.wr = tev
        for i in range(4):
            dc = half * 4 + i
            eev = p.op("s", lambda s, d=c.xt[:, dc * T + j * 128: dc * T + (j + 1) * 128],
                       sp=bank_bf(bank, i): s.copy(d, sp),
                       wait=[bank.wr], tag=(i == 3))
        bank.rd = eev
        ev = eev
    return ev


def emit_ln_all(c, per_tile=None):
    p = c.p
    st = c.st
    mean_all = st[:, S_MEAN:S_MEAN + 16]
    negm_all = st[:, S_NEGM:S_NEGM + 16]
    var_all = st[:, S_VAR:S_VAR + 16]
    rstd_all = st[:, S_RSTD:S_RSTD + 16]
    scr = c.scrf[:, 0:1024]
    red = None
    for j in range(NT):
        red = p.op("v", lambda v, jj=j: v.reduce_sum(
            mean_all[:, jj:jj + 1], c.x[:, jj, :], AX.X), wait=[c.last_x_ev])
    em = p.op("v", lambda v: v.tensor_scalar_mul(mean_all, mean_all, 1.0 / 1024.0),
              wait=[red])
    en = p.op("s", lambda s: s.mul(negm_all, mean_all, -1.0), wait=[em])
    sq = None
    for j in range(NT):
        sq = p.op("s", lambda s, jj=j: s.activation(
            scr, c.x[:, jj, :], AF.Square, bias=negm_all[:, jj:jj + 1], scale=1.0,
            accum_out=var_all[:, jj:jj + 1]), wait=[en])
    ev1 = p.op("v", lambda v: v.tensor_scalar(out=rstd_all, in0=var_all,
                                              scalar1=1.0 / 1024.0, scalar2=1e-5,
                                              op0=ALU.mult, op1=ALU.add), wait=[sq])
    ev2 = p.op("v", lambda v: v.reciprocal(rstd_all, rstd_all), wait=[ev1])
    ev3 = p.op("s", lambda s: s.activation(rstd_all, rstd_all, AF.Sqrt), wait=[ev2])
    tok_bf = c.pool[:, 0:1024]
    lnf = c.scrf[:, 1024:2048]
    lnev = None
    for j in range(NT):
        dst = lnf if per_tile is not None else tok_bf
        e = p.op("v", lambda v, jj=j, d=dst: v.tensor_scalar(
            out=d, in0=c.x[:, jj, :], scalar1=mean_all[:, jj:jj + 1],
            scalar2=rstd_all[:, jj:jj + 1], op0=ALU.subtract, op1=ALU.mult),
            wait=[ev3, lnev, c.fence_ev, em])
        if per_tile is not None:
            e = per_tile(j, lnf, tok_bf, e)
        lnev = emit_tok_to_feat(c, j, tok_bf, c.banks[7], wait=[e])
    c.xt_ev = lnev


def xt_view(c, dc, t0, t1):
    return c.xt[:, dc * T + t0: dc * T + t1]



def emit_router_l1(c, px):
    p = c.p
    lw = p.dma("s", lambda y: y.dma_start(
        c.wr8[:, 0:64].rearrange("p (cc e) -> p cc e", e=8),
        c.wbf[px + "w_router"].rearrange("(cc p) e -> p cc e", p=128)),
        wait=[c.ev_prepass[px + "w_router"], c.fence_ev])
    bank = c.banks[0]
    cw = c.st[:, S_CW:S_CW + 128].rearrange("p (j e) -> p j e", e=8)
    ev = None
    prev_read = None
    for j in range(NT):
        for dc in range(NDC):
            evm = p.op("t", lambda t, d=dc, jj=j: t.matmul(
                bank.ap[:, 0:8], xt_view(c, d, jj * 128, (jj + 1) * 128),
                c.wr8[:, d * 8:(d + 1) * 8], start=(d == 0), stop=(d == NDC - 1)),
                wait=[bank.rd, lw, c.xt_ev, prev_read], tag=(dc == NDC - 1))
        bank.wr = evm
        lg = bank.ap[:, 0:8]
        mcol = c.st[:, S_TMP:S_TMP + 1]
        sumc = c.st[:, S_TMP + 1:S_TMP + 2]
        m1 = c.st[:, S_TMP + 2:S_TMP + 3]
        escr = c.scrf[:, 0:8]
        e1 = p.op("v", lambda v: v.tensor_reduce(mcol, lg, AX.X, ALU.max),
                  wait=[bank.wr, prev_read])
        e2 = p.op("s", lambda s: s.mul(mcol, mcol, -1.0), wait=[e1])
        e3 = p.op("s", lambda s: s.activation(escr, lg, AF.Exp, bias=mcol, scale=1.0,
                                              accum_out=sumc), wait=[e2, prev_read])
        bank.rd = e3
        e4 = p.op("v", lambda v: v.tensor_reduce(m1, escr, AX.X, ALU.max), wait=[e3])
        e5 = p.op("v", lambda v: v.reciprocal(sumc, sumc), wait=[e3, e4])
        e6 = p.op("v", lambda v, jj=j: v.tensor_scalar(
            out=cw[:, jj, :], in0=escr, scalar1=m1, scalar2=sumc,
            op0=ALU.is_ge, op1=ALU.mult), wait=[e5, e4])
        prev_read = e6
        ev = e6
    return ev


def emit_router_l0(c, px):
    p = c.p
    lw1 = p.dma("s", lambda y: y.dma_start(c.rt8[0:8, :], c.wbf[px + "router_weight"]),
                wait=[c.ev_prepass[px + "router_weight"], c.fence_ev])
    lw2 = p.dma("s", lambda y: y.dma_start(c.rt8[8:9, :],
                                           c.wbf[px + "shared_expert_gate_w"]),
                wait=[c.ev_prepass[px + "shared_expert_gate_w"], c.fence_ev])
    bank = c.banks[0]
    wr9 = c.wr8.rearrange("p (cc e) -> p cc e", e=10)
    for dcg in range(2):
        for i in range(4):
            dc = dcg * 4 + i
            tev = p.op("t", lambda t, d=bank_bf(bank, i)[:, 0:9],
                       s=c.rt8[0:9, dc * 128:(dc + 1) * 128]:
                       t.transpose(d, s, c.identb[0:9, 0:9]),
                       wait=[bank.rd, lw1, lw2], tag=(i == 3))
        bank.wr = tev
        for i in range(4):
            dc = dcg * 4 + i
            eev = p.op("s", lambda s, d=wr9[:, dc, 0:9],
                       sp=bank_bf(bank, i)[:, 0:9]: s.copy(d, sp),
                       wait=[bank.wr], tag=(i == 3))
        bank.rd = eev
    lwr = bank.rd
    cw = c.st[:, S_CW:S_CW + 128].rearrange("p (j e) -> p j e", e=8)
    seg = c.st[:, S_SEG:S_SEG + 16]
    ev = None
    prev_read = None
    for j in range(NT):
        for dc in range(NDC):
            evm = p.op("t", lambda t, d=dc, jj=j: t.matmul(
                bank.ap[:, 0:9], xt_view(c, d, jj * 128, (jj + 1) * 128),
                wr9[:, d, 0:9], start=(d == 0), stop=(d == NDC - 1)),
                wait=[bank.rd, lwr, c.xt_ev], tag=(dc == NDC - 1))
        bank.wr = evm
        e8 = p.op("s", lambda s, jj=j: s.activation(
            seg[:, jj:jj + 1], bank.ap[:, 8:9], AF.Sigmoid, scale=1.0),
            wait=[bank.wr])
        e8b = p.op("s", lambda s, jj=j: s.mul(seg[:, jj:jj + 1], seg[:, jj:jj + 1], 0.5),
                   wait=[e8])
        lg = bank.ap[:, 0:8]
        mcol = c.st[:, S_TMP:S_TMP + 1]
        den = c.st[:, S_TMP + 1:S_TMP + 2]
        escr = c.scrf[:, 0:8]
        s8 = c.scrf[:, 8:16]
        keep = c.scrf[:, 16:24]
        e1 = p.op("v", lambda v: v.tensor_reduce(mcol, lg, AX.X, ALU.max),
                  wait=[bank.wr, prev_read])
        e2 = p.op("s", lambda s: s.mul(mcol, mcol, -1.0), wait=[e1])
        e3 = p.op("s", lambda s: s.activation(escr, lg, AF.Exp, bias=mcol, scale=1.0),
                  wait=[e2, prev_read])
        bank.rd = e3
        e4 = p.op("v", lambda v: v.max(s8, escr), wait=[e3])
        e5 = p.op("v", lambda v: v.tensor_tensor(den, s8[:, 0:1], s8[:, 1:2], ALU.add),
                  wait=[e4])
        e6 = p.op("v", lambda v: v.reciprocal(den, den), wait=[e5])
        e7 = p.op("v", lambda v: v.tensor_single_scalar(keep, escr, s8[:, 1:2],
                                                        ALU.is_ge), wait=[e4, e3])
        e9 = p.op("v", lambda v: v.tensor_tensor(keep, keep, escr, ALU.mult), wait=[e7])
        e10 = p.op("v", lambda v, jj=j: v.tensor_scalar(
            out=cw[:, jj, :], in0=keep, scalar1=den, scalar2=0.5,
            op0=ALU.mult, op1=ALU.mult), wait=[e9, e6])
        prev_read = e10
        ev = e10
    return ev



def emit_ffn1(c, px):
    p = c.p
    emit_ln_all(c)
    rt_ev = emit_router_l1(c, px)

    wu = c.pool2[:, 0:8192].rearrange("p (b ug cc h) -> p b ug cc h", b=2, ug=2,
                                      cc=8)
    wd = c.pool2[:, 8192:24576].rearrange("p (b hh o) -> p b hh o", b=2, hh=8)
    wu_ev = [None, None]
    wu_free = [c.fence_ev, c.fence_ev]
    wd_ev = [None, None]
    wd_free = [c.fence_ev, c.fence_ev]
    h_t = c.pool.rearrange("p (hh tt) -> p hh tt", hh=32)
    xev = {}
    prev_sil_read = [None, None]

    experts = list(range(7)) + ["sh"]
    for tg in range(4):
        t0 = tg * 512
        for e in experts:
            ei = 7 if e == "sh" else e
            evug = c.s2ev[("gug", ei)]
            evwd = c.s2ev[("wd", ei)]
            h_done_q = {}
            for hp in range(16):
                b = hp % 2
                wu_ev[b] = p.dma("y", lambda y, e_=ei, hp_=hp, b_=b: y.dma_start(
                    wu[:, b_], c.s2gug[e_, hp_]),
                    wait=[wu_free[b], evug])
                for hi in range(2):
                    hh = hp * 2 + hi
                    bu = c.banks[hh % 2]
                    bg = c.banks[2 + hh % 2]
                    sil = c.scrf[:, 1024 + 512 * (hh % 2): 1536 + 512 * (hh % 2)]
                    for dc in range(NDC):
                        evu = p.op("t", lambda t, d=dc, b_=b, hi_=hi, bk=bu,
                                   t0_=t0: t.matmul(
                            bk.ap, wu[:, b_, 0, d, hi_ * 128:(hi_ + 1) * 128],
                            xt_view(c, d, t0_, t0_ + 512),
                            start=(d == 0), stop=(d == 7)),
                            wait=[bu.rd, wu_ev[b], c.xt_ev], tag=(dc == 7))
                    bu.wr = evu
                    for dc in range(NDC):
                        evg = p.op("t", lambda t, d=dc, b_=b, hi_=hi, bk=bg,
                                   t0_=t0: t.matmul(
                            bk.ap, wu[:, b_, 1, d, hi_ * 128:(hi_ + 1) * 128],
                            xt_view(c, d, t0_, t0_ + 512),
                            start=(d == 0), stop=(d == 7)),
                            wait=[bg.rd, wu_ev[b], c.xt_ev], tag=(dc == 7))
                    bg.wr = evg
                    es = p.op("s", lambda s, bk=bg, sl=sil: s.activation(
                        sl, bk.ap, AF.Silu), wait=[bg.wr, prev_sil_read[hh % 2]])
                    bg.rd = es
                    eh = p.op("v", lambda v, hh_=hh, bk=bu, sl=sil: v.tensor_tensor(
                        h_t[:, hh_, :], bk.ap, sl, ALU.mult), wait=[es, bu.wr])
                    bu.rd = eh
                    prev_sil_read[hh % 2] = eh
                    if hi == 1:
                        wu_free[b] = eh
                    if hh % 8 == 7:
                        h_done_q[hh // 8] = eh
            for q in range(4):
                b = q % 2
                wd_ev[b] = p.dma("y", lambda y, e_=ei, q_=q, b_=b: y.dma_start(
                    wd[:, b_], c.s2wd[e_, q_]),
                    wait=[wd_free[b], evwd])
                ev_last = None
                for i2 in range(2):
                    for o in range(2):
                        for half in range(2):
                            i = half * 2 + i2
                            bk = c.banks[4 + i2 * 2 + o]
                            for hh8 in range(8):
                                hh = q * 8 + hh8
                                ev_last = p.op("t", lambda t, hh_=hh, hh8_=hh8, i_=i,
                                               o_=o, b_=b, bk_=bk: t.matmul(
                                    bk_.ap, h_t[:, hh_, i_ * 128:(i_ + 1) * 128],
                                    wd[:, b_, hh8_, o_ * 512:(o_ + 1) * 512],
                                    start=(hh8_ == 0), stop=(hh8_ == 7)),
                                    wait=([bk.rd, wd_ev[b], h_done_q[q]]
                                          if hh8 == 0 else []),
                                    tag=(hh8 == 7))
                            bk.wr = ev_last
                            j = tg * 4 + i
                            sc = (0.5 if e == "sh" else
                                  c.st[:, S_CW + j * 8 + e:S_CW + j * 8 + e + 1])
                            evx = p.op("v", lambda v, jj=j, o_=o, bk_=bk, sc_=sc:
                                       v.scalar_tensor_tensor(
                                           out=c.x[:, jj, o_ * 512:(o_ + 1) * 512],
                                           in0=bk_.ap, scalar=sc_,
                                           in1=c.x[:, jj, o_ * 512:(o_ + 1) * 512],
                                           op0=ALU.mult, op1=ALU.add),
                                       wait=[bk.wr, rt_ev, xev.get((j, o))])
                            bk.rd = evx
                            xev[(j, o)] = evx
                wd_free[b] = ev_last
    c.last_x_ev = ("v", c.p.cnt["v"])



def emit_ffn0(c, px, parts=("experts", "shared"), tgs=(0, 1, 2, 3)):
    p = c.p
    emit_ln_all(c)
    rt_ev = emit_router_l0(c, px)
    cw = c.st[:, S_CW:S_CW + 128].rearrange("p (j e) -> p j e", e=8)
    seg = c.st[:, S_SEG:S_SEG + 16]

    gu = c.pool2[:, 0:16384].rearrange("p (b cc o) -> p b cc o", b=2, cc=8)
    dn = c.pool2[:, 16384:24576].rearrange("p (b cc o) -> p b cc o", b=2, cc=4)
    gu_ev = [None, None]; dn_ev = [None, None]
    gu_free = [c.fence_ev, c.fence_ev]; dn_free = [c.fence_ev, c.fence_ev]
    h_t = c.pool.rearrange("p (hh tt) -> p hh tt", hh=32)
    xev = {}
    prev_h_read = None
    sd_free_ev = c.fence_ev

    for tg in tgs:
        t0 = tg * 512
        for e in (range(8) if "experts" in parts else []):
            b = e % 2
            gu_ev[b] = p.dma("y", lambda y, e_=e, b_=b: y.dma_start(
                gu[:, b_], c.s2gu0[e_]),
                wait=[gu_free[b], c.s2ev[("gu0", e)]])
            dn_ev[b] = p.dma("y", lambda y, e_=e, b_=b: y.dma_start(
                dn[:, b_], c.s2dn0[e_]),
                wait=[dn_free[b], c.s2ev[("dn0", e)]])
            sil_ev = None
            for oc in range(8):
                bk = c.banks[oc % 2]
                for dc in range(NDC):
                    evm = p.op("t", lambda t, d=dc, o_=oc, b_=b, bk_=bk, t0_=t0:
                               t.matmul(
                        bk_.ap, gu[:, b_, d, o_ * 128:(o_ + 1) * 128],
                        xt_view(c, d, t0_, t0_ + 512),
                        start=(d == 0), stop=(d == 7)),
                        wait=[bk.rd, gu_ev[b], c.xt_ev], tag=(dc == 7))
                bk.wr = evm
                if oc < 4:
                    es = p.op("s", lambda s, o_=oc, bk_=bk: s.activation(
                        h_t[:, 16 + o_, :], bk_.ap, AF.Silu),
                        wait=[bk.wr, prev_h_read])
                    bk.rd = es
                    sil_ev = es
                else:
                    eh = p.op("v", lambda v, o_=oc, bk_=bk: v.tensor_tensor(
                        h_t[:, o_ - 4, :], bk_.ap, h_t[:, 16 + o_ - 4, :], ALU.mult),
                        wait=[bk.wr, sil_ev])
                    bk.rd = eh
            h_done = eh
            gu_free[b] = eh
            ev_last = None
            for i in range(4):
                jj = tg * 4 + i
                for o in range(2):
                    bk = c.banks[4 + (i % 2) * 2 + o]
                    for ic in range(4):
                        ev_last = p.op("t", lambda t, i_=i, o_=o, ic_=ic, b_=b, bk_=bk:
                                       t.matmul(
                            bk_.ap, h_t[:, ic_, i_ * 128:(i_ + 1) * 128],
                            dn[:, b_, ic_, o_ * 512:(o_ + 1) * 512],
                            start=(ic_ == 0), stop=(ic_ == 3)),
                            wait=[bk.rd, dn_ev[b], h_done], tag=(ic == 3))
                    bk.wr = ev_last
                    evx = p.op("v", lambda v, jj_=jj, o_=o, bk_=bk, e_=e:
                               v.scalar_tensor_tensor(
                                   out=c.x[:, jj_, o_ * 512:(o_ + 1) * 512],
                                   in0=bk_.ap, scalar=c.st[:, S_CW + jj_ * 8 + e_:S_CW + jj_ * 8 + e_ + 1],
                                   in1=c.x[:, jj_, o_ * 512:(o_ + 1) * 512],
                                   op0=ALU.mult, op1=ALU.add),
                               wait=[bk.wr, rt_ev, xev.get((jj, o))])
                    bk.rd = evx
                    xev[(jj, o)] = evx
            dn_free[b] = ev_last
            prev_h_read = ev_last
        if "shared" not in parts:
            continue
        shb = c.pool2[:, 0:8192].rearrange("p (b cc o) -> p b cc o", b=2, cc=8)
        sh_ev = [gu_ev[0], gu_ev[1]]
        sh_free = [gu_free[0], gu_free[1]]
        sil_done = {}
        for part, nm in ((0, "shared_gate_w"), (1, "shared_up_w")):
            for grp in range(4):
                b = grp % 2
                sh_ev[b] = p.dma("y", lambda y, b_=b, pt=part, g_=grp: y.dma_start(
                    shb[:, b_], c.s2sg0[pt, g_]),
                    wait=[sh_free[b], c.s2ev[("sg0", part, grp)]])
                for oc in range(4):
                    hc = grp * 4 + oc
                    bk = c.banks[oc % 2]
                    for dc in range(NDC):
                        evm = p.op("t", lambda t, d=dc, o_=oc, b_=b, bk_=bk, t0_=t0:
                                   t.matmul(
                            bk_.ap, shb[:, b_, d, o_ * 128:(o_ + 1) * 128],
                            xt_view(c, d, t0_, t0_ + 512),
                            start=(d == 0), stop=(d == 7)),
                            wait=[bk.rd, sh_ev[b], c.xt_ev], tag=(dc == 7))
                    bk.wr = evm
                    if part == 0:
                        es = p.op("s", lambda s, h_=hc, bk_=bk: s.activation(
                            h_t[:, 16 + h_, :], bk_.ap, AF.Silu),
                            wait=[bk.wr, prev_h_read])
                        bk.rd = es
                        sil_done[hc] = es
                    else:
                        eh = p.op("v", lambda v, h_=hc, bk_=bk: v.tensor_tensor(
                            h_t[:, h_, :], bk_.ap, h_t[:, 16 + h_, :], ALU.mult),
                            wait=[bk.wr, sil_done[hc], prev_h_read])
                        bk.rd = eh
                sh_free[b] = bk.rd
        h_done = bk.rd
        sdb = c.pool2[:, 8192:24576].rearrange("p (cc o) -> p cc o", cc=16)
        sd_ld = p.dma("y", lambda y: y.dma_start(sdb, c.s2sd0),
                      wait=[sd_free_ev, gu_free[0], gu_free[1], dn_free[0],
                            dn_free[1], c.s2ev["sd0"]])
        ev_last = None
        for i in range(4):
            jj = tg * 4 + i
            for o in range(2):
                bk = c.banks[4 + (i % 2) * 2 + o]
                for ic in range(16):
                    ev_last = p.op("t", lambda t, ic_=ic, i_=i, o_=o, bk_=bk:
                                   t.matmul(
                        bk_.ap, h_t[:, ic_, i_ * 128:(i_ + 1) * 128],
                        sdb[:, ic_, o_ * 512:(o_ + 1) * 512],
                        start=(ic_ == 0), stop=(ic_ == 15)),
                        wait=([bk.rd, sd_ld, h_done] if ic == 0 else []),
                        tag=(ic == 15))
                bk.wr = ev_last
                evx = p.op("v", lambda v, jj_=jj, o_=o, bk_=bk: v.scalar_tensor_tensor(
                    out=c.x[:, jj_, o_ * 512:(o_ + 1) * 512],
                    in0=bk_.ap, scalar=seg[:, jj_:jj_ + 1],
                    in1=c.x[:, jj_, o_ * 512:(o_ + 1) * 512],
                    op0=ALU.mult, op1=ALU.add),
                    wait=[bk.wr, rt_ev, xev.get((jj, o))])
                bk.rd = evx
                xev[(jj, o)] = evx
        sd_free_ev = ev_last
        prev_h_read = ev_last
        gu_free = [ev_last, ev_last]
        dn_free = [ev_last, ev_last]
    c.last_x_ev = ("v", c.p.cnt["v"])




def attn_core(c, layer, heads_iter, ka_of, vtok_of, rinv_sl, aowrite):
    raise NotImplementedError


def emit_attn0(c, px):
    p = c.p
    ropebuf = c.scrf[:, 2048:3072]
    tmp1 = c.scrf[:, 0:512]
    tmp2 = c.scrf[:, 512:1024]
    rope_evs = {}

    def rope_tile(j, lnf, tok_bf, ev):
        ld = p.dma("y", lambda y, jj=j: y.dma_start(
            ropebuf, c.dram["rope_cs"][jj * 128:(jj + 1) * 128, :]),
            wait=[rope_evs.get("last")])
        xe = lnf[:, 0:1024:2]
        xo = lnf[:, 1:1024:2]
        cosv = ropebuf[:, 0:512]
        sinv = ropebuf[:, 512:1024]
        e1 = p.op("v", lambda v: v.tensor_tensor(tmp1, xe, cosv, ALU.mult), wait=[ev, ld])
        e2 = p.op("v", lambda v: v.tensor_tensor(tmp2, xo, sinv, ALU.mult), wait=[ev, ld])
        e3 = p.op("v", lambda v: v.tensor_tensor(tok_bf[:, 0:512], tmp1, tmp2,
                                                 ALU.subtract), wait=[e1, e2])
        e4 = p.op("v", lambda v: v.tensor_tensor(tmp1, xe, sinv, ALU.mult), wait=[e3])
        e5 = p.op("v", lambda v: v.tensor_tensor(tmp2, xo, cosv, ALU.mult), wait=[e3])
        e6 = p.op("v", lambda v: v.tensor_tensor(tok_bf[:, 512:1024], tmp1, tmp2,
                                                 ALU.add), wait=[e4, e5])
        rope_evs["last"] = e6
        return e6

    emit_ln_all(c, per_tile=rope_tile)

    ka = [c.pool2[0:65, g * 2048:(g + 1) * 2048] for g in range(4)]
    v_tok = c.pool2[:, 8192:12288].rearrange("p (j o) -> p j o", o=256)
    wkT = c.pool2[:, 12288:14336].rearrange("p (cc o) -> p cc o", cc=8)
    wvT = c.pool2[:, 14336:16384].rearrange("p (cc o) -> p cc o", cc=8)
    wqT = c.pool2[:, 16384:24576].rearrange("p (cc o) -> p cc o", cc=8)
    wob = wqT
    q_t = c.pool.rearrange("p (cc tt) -> p cc tt", cc=8)

    wq = c.wbf[px + "W_q"]; wk = c.wbf[px + "W_k"]
    wv = c.wbf[px + "W_v"]; wo = c.wbf[px + "W_o"]
    evw = {k: c.ev_prepass[px + k] for k in ("W_q", "W_k", "W_v", "W_o")}

    e1 = p.dma("y", lambda y: y.dma_start(
        wkT.rearrange("p cc o -> p (cc o)"), c.s2l0k.rearrange("p a b -> p (a b)")),
        wait=[c.s2ev["l0k"], c.fence_ev])
    e2 = p.dma("y", lambda y: y.dma_start(
        wvT.rearrange("p cc o -> p (cc o)"), c.s2l0v.rearrange("p a b -> p (a b)")),
        wait=[c.s2ev["l0v"], c.fence_ev])
    kv_ld = e2
    wq_ld = p.dma("y", lambda y: y.dma_start(
        wqT.rearrange("p cc o -> p (cc o)"), c.s2l0q.rearrange("p a b -> p (a b)")),
        wait=[c.s2ev["l0q"], c.fence_ev])

    for tg in range(4):
        t0 = tg * 512
        for kc in range(2):
            bk = c.banks[kc]
            for dc in range(NDC):
                evm = p.op("t", lambda t, d=dc, k_=kc, t0_=t0, bk_=bk: t.matmul(
                    bk_.ap, wkT[:, d, k_ * 128:(k_ + 1) * 128],
                    xt_view(c, d, t0_, t0_ + 512), start=(d == 0), stop=(d == 7)),
                    wait=[bk.rd, kv_ld, c.xt_ev], tag=(dc == 7))
            bk.wr = evm
            e1 = p.op("s", lambda s, k_=kc, t0_=t0, bk_=bk: s.copy(
                ka[2 * k_][0:64, t0_:t0_ + 512], bk_.ap[0:64, :]), wait=[bk.wr])
            e2 = p.op("s", lambda s, k_=kc, t0_=t0, bk_=bk: s.copy(
                ka[2 * k_ + 1][0:64, t0_:t0_ + 512], bk_.ap[64:128, :]), wait=[bk.wr])
            bk.rd = e2
    k_done = e2
    mevs = [p.op("g", lambda gg, g_=g: gg.memset(ka[g_][64:65, :], -1.0),
                 wait=[c.fence_ev]) for g in range(4)]
    ka_neg1 = mevs[-1]

    vev = None
    for j in range(NT):
        bk = c.banks[2 + j % 2]
        for dc in range(NDC):
            evm = p.op("t", lambda t, d=dc, jj=j, bk_=bk: t.matmul(
                bk_.ap[:, 0:256], xt_view(c, d, jj * 128, (jj + 1) * 128),
                wvT[:, d, :], start=(d == 0), stop=(d == 7)),
                wait=[bk.rd, kv_ld, c.xt_ev], tag=(dc == 7))
        bk.wr = evm
        vev = p.op("s", lambda s, jj=j, bk_=bk: s.copy(v_tok[:, jj, :],
                                                       bk_.ap[:, 0:256]),
                   wait=[bk.wr])
        bk.rd = vev
    v_done = vev

    qev = None
    for oc in range(NDC):
        for tg in range(4):
            t0 = tg * 512
            bk = c.banks[4 + tg % 2]
            for dc in range(NDC):
                evm = p.op("t", lambda t, d=dc, o_=oc, t0_=t0, bk_=bk: t.matmul(
                    bk_.ap, wqT[:, d, o_ * 128:(o_ + 1) * 128],
                    xt_view(c, d, t0_, t0_ + 512), start=(d == 0), stop=(d == 7)),
                    wait=[bk.rd, wq_ld, c.xt_ev], tag=(dc == 7))
            bk.wr = evm
            ev = p.op("s", lambda s, o_=oc, t0_=t0, bk_=bk: s.copy(
                q_t[:, o_, t0_:t0_ + 512], bk_.ap), wait=[bk.wr])
            bk.rd = ev
            qev = ev

    Qa = c.xt[0:65, 0:512]
    escr = c.xt[:, 512:2560]
    ptb = c.xt[:, 2560:6656].rearrange("p (k q) -> p k q", k=8)
    aost = c.xt[:, 6656:10752].rearrange("p (i d) -> p i d", i=4)
    aoT = c.xt[:, 10752:11776].rearrange("p (cc d) -> p cc d", cc=8)
    avacc = c.scrf[:, 1024:1280].rearrange("p (i d) -> p i d", i=4)

    mcol = c.st[:, S_AT:S_AT + 1]
    mbf = c.st[:, S_AT + 1:S_AT + 2].bitcast(BF16)[:, 0:1]
    biasc = c.st[:, S_AT + 2:S_AT + 3]
    rcol = c.st[:, S_AT + 3:S_AT + 4]
    rinv = c.st[:, S_AT + 4:S_AT + 5]
    rsl = c.st[:, S_AT + 8:S_AT + 12]

    xev = {}
    qa_free = c.fence_ev
    stat_free = None
    for jg in range(4):
        aosl_ev = {}
        for h in range(16):
            g = h // 4
            qa_evs = []
            for i in range(4):
                j = jg * 4 + i
                qa_evs.append(p.op("s", lambda ss, i_=i, jj=j, h_=h: ss.copy(
                    Qa[0:64, i_ * 128:(i_ + 1) * 128],
                    q_t[64 * (h_ % 2):64 * (h_ % 2) + 64, h_ // 2,
                        jj * 128:(jj + 1) * 128]),
                    wait=[qev, qa_free]))
            qa_done = qa_evs[-1]
            row_evs = []
            for i in range(4):
                j = jg * 4 + i
                L = (j + 1) * 128
                nb = (L + 511) // 512
                off = 0 if i % 2 == 0 else 4
                reg = c.pa if off == 0 else c.pb
                wl = [qa_done, k_done, ka_neg1] + \
                     [c.banks[off + q].rd for q in range(nb)]
                for kb in range(nb):
                    n = min(512, L - kb * 512)
                    evm = p.op("t", lambda t, kb_=kb, n_=n, i_=i, g_=g, r_=reg:
                               t.matmul(
                        r_[:, kb_ * 512:kb_ * 512 + n_],
                        Qa[0:64, i_ * 128:(i_ + 1) * 128],
                        ka[g_][0:64, kb_ * 512:kb_ * 512 + n_],
                        start=True, stop=True),
                        wait=wl, tag=(kb == nb - 1))
                for kb in range(nb):
                    c.banks[off + kb].wr = evm
                em = p.op("v", lambda v, jj=j, r_=reg: v.tensor_tensor(
                    r_[:, jj * 128:(jj + 1) * 128],
                    r_[:, jj * 128:(jj + 1) * 128], c.maskQ, ALU.add),
                    wait=[evm, c.ev_consts])
                e1 = p.op("v", lambda v, L_=L, r_=reg: v.tensor_reduce(
                    mcol, r_[:, 0:L_], AX.X, ALU.max), wait=[em, stat_free])
                e2 = p.op("v", lambda v: v.tensor_copy(mbf, mcol), wait=[e1])
                e3 = p.op("s", lambda s: s.mul(biasc, mbf, -0.125), wait=[e2])
                e4 = p.op("s", lambda s, L_=L, r_=reg: s.activation(
                    escr[:, 0:L_], r_[:, 0:L_], AF.Exp, scale=0.125, bias=biasc,
                    accum_out=rcol), wait=[e3])
                for kb in range(nb):
                    c.banks[off + kb].rd = e4
                e5 = p.op("v", lambda v: v.reciprocal(rinv, rcol), wait=[e4])
                e5b = p.op("v", lambda v, i_=i: v.tensor_copy(
                    rsl[:, i_:i_ + 1], rinv), wait=[e5])
                b6 = c.banks[6]
                et = p.op("t", lambda t, d=bank_bf(b6, 0)[0:1, :]: t.transpose(
                    d, mbf, c.identb), wait=[e2, b6.rd, e4])
                b6.wr = et
                erow = p.op("s", lambda s, i_=i, d=bank_bf(b6, 0)[0:1, :]: s.copy(
                    Qa[64:65, i_ * 128:(i_ + 1) * 128], d), wait=[et, qa_done])
                b6.rd = erow
                row_evs.append(erow)
                stat_free = erow
            rows_done = row_evs[-1]
            rinv_done = ("v", p.cnt["v"])
            nkt = jg * 4 + 4
            avb = [c.banks[q] for q in range(4)]
            for half in range((nkt + 7) // 8):
                kts = list(range(half * 8, min(nkt, half * 8 + 8)))
                pt_done = {}
                for kt in kts:
                    b45 = c.banks[4 + kt % 2]
                    evm = p.op("t", lambda t, kt_=kt, g_=g, bk=b45: t.matmul(
                        bk.ap, ka[g_][0:65, kt_ * 128:(kt_ + 1) * 128],
                        Qa[0:65, :], start=True, stop=True),
                        wait=[b45.rd, rows_done, ka_neg1], tag=True)
                    b45.wr = evm
                    if kt >= jg * 4:
                        i = kt - jg * 4
                        evm = p.op("v", lambda v, i_=i, bk=b45: v.tensor_tensor(
                            bk.ap[:, i_ * 128:(i_ + 1) * 128],
                            bk.ap[:, i_ * 128:(i_ + 1) * 128], c.maskT, ALU.add),
                            wait=[evm])
                    ee = p.op("s", lambda s, kt_=kt, bk=b45: s.activation(
                        ptb[:, kt_ % 8, :], bk.ap[:, 0:512], AF.Exp, scale=0.125),
                        wait=[evm, pt_done.get(kt % 2)])
                    b45.rd = ee
                    pt_done[kt % 2] = ee
                pt_all = ("s", p.cnt["s"])
                for i in range(4):
                    j = jg * 4 + i
                    lo = half * 8
                    hi = min(j, half * 8 + 7)
                    if hi < lo:
                        continue
                    for kt in range(lo, hi + 1):
                        eav = p.op("t", lambda t, i_=i, kt_=kt, g_=g, lo_=lo, hi_=hi:
                                   t.matmul(
                            avb[i_].ap[:, 0:64], ptb[:, kt_ % 8, i_ * 128:(i_ + 1) * 128],
                            v_tok[:, kt_, g_ * 64:(g_ + 1) * 64],
                            start=(kt_ == lo_), stop=(kt_ == hi_)),
                            wait=([avb[i].rd, pt_all, v_done] if kt == lo else []),
                            tag=(kt == hi))
                    avb[i].wr = eav
                    if half == 0:
                        if j < 8:
                            evo = p.op("v", lambda v, i_=i, h_=h: v.tensor_scalar_mul(
                                aost[:, i_, h_ * 64:(h_ + 1) * 64],
                                avb[i_].ap[:, 0:64], rsl[:, i_:i_ + 1]),
                                wait=[avb[i].wr, rinv_done, aosl_ev.get((i, h))])
                        else:
                            evo = p.op("v", lambda v, i_=i: v.tensor_copy(
                                avacc[:, i_, :], avb[i_].ap[:, 0:64]),
                                wait=[avb[i].wr])
                    else:
                        e6 = p.op("v", lambda v, i_=i: v.tensor_tensor(
                            avacc[:, i_, :], avacc[:, i_, :], avb[i_].ap[:, 0:64],
                            ALU.add), wait=[avb[i].wr])
                        evo = p.op("v", lambda v, i_=i, h_=h: v.tensor_scalar_mul(
                            aost[:, i_, h_ * 64:(h_ + 1) * 64], avacc[:, i_, :],
                            rsl[:, i_:i_ + 1]),
                            wait=[e6, rinv_done, aosl_ev.get((i, h))])
                    avb[i].rd = evo
                    aosl_ev[(i, h)] = evo
            qa_free = ("t", p.cnt["t"])
        ao_done = ("v", p.cnt["v"])
        for i in range(4):
            j = jg * 4 + i
            b6 = c.banks[6]
            aoT_ev = None
            for halfc in range(2):
                for q in range(4):
                    dc = halfc * 4 + q
                    et = p.op("t", lambda t, d=bank_bf(b6, q),
                              s=aost[:, i, dc * 128:(dc + 1) * 128]:
                              t.transpose(d, s, c.identb),
                              wait=[b6.rd, ao_done], tag=(q == 3))
                b6.wr = et
                for q in range(4):
                    dc = halfc * 4 + q
                    ee = p.op("s", lambda s, d=aoT[:, dc, :], sp=bank_bf(b6, q):
                              s.copy(d, sp), wait=[b6.wr], tag=(q == 3))
                b6.rd = ee
            aoT_ev = ee
            if jg == 0 and i == 0:
                c.wo_ld = p.dma("y", lambda y: y.dma_start(
                    wob.rearrange("p cc o -> p (cc o)"),
                    c.s2l0o.rearrange("p a b -> p (a b)")),
                    wait=[c.s2ev["l0o"], ("t", p.cnt["t"])])
            for o in range(2):
                bk = c.banks[o]
                for dc in range(NDC):
                    evm = p.op("t", lambda t, d=dc, o_=o, bk_=bk: t.matmul(
                        bk_.ap, aoT[:, d, :], wob[:, d, o_ * 512:(o_ + 1) * 512],
                        start=(d == 0), stop=(d == 7)),
                        wait=[bk.rd, c.wo_ld, aoT_ev], tag=(dc == 7))
                bk.wr = evm
                evx = p.op("v", lambda v, jj=j, o_=o, bk_=bk: v.tensor_tensor(
                    c.x[:, jj, o_ * 512:(o_ + 1) * 512],
                    c.x[:, jj, o_ * 512:(o_ + 1) * 512], bk_.ap, ALU.add),
                    wait=[bk.wr, xev.get((j, o)), c.last_x_ev])
                bk.rd = evx
                xev[(j, o)] = evx
    c.last_x_ev = ("v", c.p.cnt["v"])



def emit_attn1(c, px):
    p = c.p
    lc = p.dma("s", lambda y: y.dma_start(
        c.curv[0:1, 0:1], c.dram[px + "curv"].rearrange("(a x) -> a x", a=1)))
    b6 = c.banks[6]
    cbv = c.curv.bitcast(BF16)[0:1, 2:3]
    ecst = p.op("v", lambda v: v.tensor_copy(cbv, c.curv[0:1, 0:1]), wait=[lc])
    e1 = p.op("t", lambda t: t.matmul(b6.ap[:, 0:1], c.ones[0:1, 0:128],
                                      cbv, start=True, stop=True),
              wait=[ecst, b6.rd, c.ev_consts])
    b6.wr = e1
    e2 = p.op("s", lambda s: s.copy(c.ccol[:, 0:1], b6.ap[:, 0:1]), wait=[e1])
    b6.rd = e2
    e3 = p.op("s", lambda s: s.activation(c.ccol[:, 1:2], c.ccol[:, 0:1], AF.Sqrt),
              wait=[e2])
    e4 = p.op("v", lambda v: v.reciprocal(c.ccol[:, 2:3], c.ccol[:, 0:1]), wait=[e2])
    sqc = c.ccol[:, 1:2]
    invc = c.ccol[:, 2:3]
    curv_ev = [e3, e4]

    n2c = c.st[:, S_TMP + 4:S_TMP + 5]
    ncc = c.st[:, S_TMP + 5:S_TMP + 6]
    coefc = c.st[:, S_TMP + 6:S_TMP + 7]
    t1c = c.st[:, S_TMP + 7:S_TMP + 8]

    def logmap_tile(j, lnf, tok_bf, ev):
        scr = c.scrf[:, 0:1024]
        a1 = p.op("s", lambda s: s.activation(scr, lnf, AF.Square, accum_out=n2c),
                  wait=[ev])
        a2 = p.op("s", lambda s: s.activation(ncc, n2c, AF.Sqrt), wait=[a1])
        a3 = p.op("v", lambda v: v.tensor_tensor(ncc, ncc, sqc, ALU.mult),
                  wait=[a2] + curv_ev)
        a4 = p.op("s", lambda s: s.activation(t1c, ncc, AF.Square), wait=[a3])
        a5 = p.op("v", lambda v: v.tensor_scalar_add(t1c, t1c, 1.0), wait=[a4])
        a6 = p.op("s", lambda s: s.activation(t1c, t1c, AF.Sqrt), wait=[a5])
        a7 = p.op("v", lambda v: v.tensor_tensor(t1c, t1c, ncc, ALU.add), wait=[a6])
        a8 = p.op("s", lambda s: s.activation(t1c, t1c, AF.Ln), wait=[a7])
        a9 = p.op("v", lambda v: v.reciprocal(coefc, ncc), wait=[a8])
        a10 = p.op("v", lambda v: v.tensor_tensor(coefc, coefc, t1c, ALU.mult),
                   wait=[a9])
        a11 = p.op("v", lambda v: v.tensor_scalar_mul(tok_bf, lnf, coefc), wait=[a10])
        return a11

    emit_ln_all(c, per_tile=logmap_tile)

    qtok = c.pool[:, 0:2048].rearrange("p (j d) -> p j d", j=NT)
    ktok = c.pool[:, 2048:4096].rearrange("p (j d) -> p j d", j=NT)
    Qa = c.pool[0:66, 4096:4608]
    Ka = c.pool[0:66, 4608:6656]
    vch = c.pool[:, 6656:8704].rearrange("p (j d) -> p j d", j=NT)
    ptb = c.pool[:, 8704:12800].rearrange("p (k q) -> p k q", k=8)
    escr = c.pool[:, 12800:14848]
    aoT = c.pool[:, 14848:15872].rearrange("p (cc d) -> p cc d", cc=8)
    qf = c.pool[:, 8704:10752]
    avacc = c.scrf[:, 1024:1280].rearrange("p (i d) -> p i d", i=4)
    ao = c.pool2[:, 0:16384].rearrange("p (j d) -> p j d", j=NT)
    wpb2 = c.pool2[:, 16384:17408].rearrange("p (cc o) -> p cc o", cc=8)
    wob = c.pool2[:, 16384:24576].rearrange("p (cc o) -> p cc o", cc=8)

    bev = []
    for nm, dst in (("wq_b", c.browq), ("wk_b", c.browk), ("wv_b", c.browv),
                    ("wo_b", c.browo)):
        bev.append(p.dma("g", lambda g, d=dst, s=c.wbf[px + nm]: g.dma_start(d, s),
                         wait=[c.fence_ev]))
    brow_ev = bev[-1]

    wqn = c.wbf[px + "wq_w"]; wkn = c.wbf[px + "wk_w"]
    wvn = c.wbf[px + "wv_w"]; won = c.wbf[px + "wo_w"]
    evw = {k: c.ev_prepass[px + k] for k in ("wq_w", "wk_w", "wv_w", "wo_w")}

    n2c = c.st[:, S_TMP + 4:S_TMP + 5]
    ncc = c.st[:, S_TMP + 5:S_TMP + 6]
    coefc = c.st[:, S_TMP + 6:S_TMP + 7]
    t1c = c.st[:, S_TMP + 7:S_TMP + 8]
    n2h = c.st[:, S_AT + 16:S_AT + 32]
    timbf = c.st[:, S_AT + 32:S_AT + 40].bitcast(BF16)
    pairb = c.st[:, S_AT + 40:S_AT + 42].bitcast(BF16)[:, 0:4]
    mcol = c.st[:, S_AT + 42:S_AT + 43]
    mbf = c.st[:, S_AT + 43:S_AT + 44].bitcast(BF16)[:, 0:1]
    biasc = c.st[:, S_AT + 44:S_AT + 45]
    rcol = c.st[:, S_AT + 45:S_AT + 46]
    rinv = c.st[:, S_AT + 46:S_AT + 47]
    rsl = c.st[:, S_AT + 8:S_AT + 12]
    sq_scr = c.scrf[:, 0:64]

    xev = {}
    wp_free = c.fence_ev
    qf_free = c.fence_ev

    def proj_feat_to_tok(cc, wscr, wld_ev, brow, tokdst):
        nonlocal wp_free, qf_free
        qf_free = ("t", p.cnt["t"])
        wp_ld = p.dma("y", lambda y, s_=wscr, c_=cc: y.dma_start(
            wpb2, s_[c_]), wait=[wp_free, wld_ev])
        qf_ev = None
        for tg in range(4):
            bk = c.banks[4 + tg % 2]
            for dc in range(NDC):
                p.op("t", lambda t, d=dc, tg_=tg, bk_=bk: t.matmul(
                    bk_.ap, wpb2[:, d, :],
                    xt_view(c, d, tg_ * 512, tg_ * 512 + 512),
                    start=(d == 0), stop=False),
                    wait=[bk.rd, wp_ld, c.xt_ev], tag=False)
            evm = p.op("t", lambda t, c_=cc, bk_=bk, br=brow: t.matmul(
                bk_.ap, br[0:1, c_ * 128:(c_ + 1) * 128], c.ones[0:1, 0:512],
                start=False, stop=True), wait=[brow_ev], tag=True)
            bk.wr = evm
            ee = p.op("s", lambda s, tg_=tg, bk_=bk: s.copy(
                qf[:, tg_ * 512:(tg_ + 1) * 512], bk_.ap), wait=[bk.wr, qf_free])
            bk.rd = ee
            qf_ev = ee
        wp_free = ("t", p.cnt["t"])
        tev = None
        b7 = c.banks[7]
        for jq in range(4):
            for q in range(4):
                j = jq * 4 + q
                et = p.op("t", lambda t, d=bank_bf(b7, q),
                          s=qf[:, j * 128:(j + 1) * 128]: t.transpose(d, s, c.identb),
                          wait=[b7.rd, qf_ev], tag=(q == 3))
            b7.wr = et
            for q in range(4):
                j = jq * 4 + q
                ec = p.op("s", lambda s, jj=j, sp=bank_bf(b7, q), td=tokdst: s.copy(
                    td[:, jj, :], sp), wait=[b7.wr], tag=(q == 3))
            b7.rd = ec
            tev = ec
        qf_free = ("t", p.cnt["t"])
        return tev

    nb2 = c.st[:, S_AT + 16:S_AT + 32]
    ncb = c.st[:, S_AT + 64:S_AT + 80]
    e1b = c.st[:, S_AT + 80:S_AT + 96]
    e2b = c.st[:, S_AT + 96:S_AT + 112]
    coefb = c.st[:, S_AT + 112:S_AT + 128]
    timf = c.st[:, S_AT + 128:S_AT + 144]
    timbfQ = c.st[:, S_AT + 144:S_AT + 152].bitcast(BF16)
    timbfK = c.st[:, S_AT + 32:S_AT + 40].bitcast(BF16)
    scr2 = c.scrf[:, 0:1024].rearrange("p (j d) -> p j d", j=NT)
    exm_prev = [None]

    war_tim = {"K": None, "Q": None}

    def expmap_batch(tok, hd, timdst, neg, ev_tok, war_key):
        sl3 = tok[:, :, hd * 64:(hd + 1) * 64]
        a1 = p.op("v", lambda v: v.tensor_tensor(scr2, sl3, sl3, ALU.mult),
                  wait=[ev_tok, exm_prev[0], war_tim[war_key]])
        a2 = p.op("v", lambda v: v.tensor_reduce(
            nb2.rearrange("p (j o) -> p j o", o=1), scr2, AX.X, ALU.add), wait=[a1])
        a3 = p.op("s", lambda s: s.activation(ncb, nb2, AF.Sqrt), wait=[a2])
        a4 = p.op("v", lambda v: v.tensor_scalar_mul(ncb, ncb, sqc),
                  wait=[a3] + curv_ev)
        a5 = p.op("s", lambda s: s.activation(e1b, ncb, AF.Exp), wait=[a4])
        a6 = p.op("s", lambda s: s.activation(e2b, ncb, AF.Exp, scale=-1.0), wait=[a4])
        a7 = p.op("v", lambda v: v.tensor_tensor(e1b, e1b, e2b, ALU.subtract),
                  wait=[a5, a6])
        a8 = p.op("v", lambda v: v.reciprocal(e2b, ncb), wait=[a6, a7])
        a9 = p.op("v", lambda v: v.tensor_tensor(coefb, e1b, e2b, ALU.mult),
                  wait=[a7, a8])
        a10 = p.op("v", lambda v: v.tensor_scalar_mul(coefb, coefb, 0.5), wait=[a9])
        ah = a10
        for j in range(NT):
            ah = p.op("v", lambda v, jj=j: v.tensor_scalar_mul(
                tok[:, jj, hd * 64:(hd + 1) * 64], tok[:, jj, hd * 64:(hd + 1) * 64],
                coefb[:, jj:jj + 1]), wait=[a10] if j == 0 else [ah])
        a12 = p.op("v", lambda v: v.tensor_tensor(scr2, sl3, sl3, ALU.mult), wait=[ah])
        a13 = p.op("v", lambda v: v.tensor_reduce(
            nb2.rearrange("p (j o) -> p j o", o=1), scr2, AX.X, ALU.add), wait=[a12])
        a14 = p.op("v", lambda v: v.tensor_single_scalar(timf, nb2, invc, ALU.add),
                   wait=[a13] + curv_ev)
        a15 = p.op("s", lambda s: s.activation(timf, timf, AF.Sqrt), wait=[a14])
        a16 = p.op("v", lambda v, sc=-1.0 if neg else 1.0: v.tensor_scalar_mul(
            timdst, timf, sc), wait=[a15])
        exm_prev[0] = a16
        return a16

    for cc in range(8):
        tq = proj_feat_to_tok(cc, c.s2l1q, c.s2ev[("l1q", cc)], c.browq, qtok)
        tk = proj_feat_to_tok(cc, c.s2l1k, c.s2ev[("l1k", cc)], c.browk, ktok)
        wv_ld = p.dma("y", lambda y, c_=cc: y.dma_start(
            wpb2, c.s2l1v[c_]), wait=[wp_free, c.s2ev[("l1v", cc)]])
        vev = None
        for j in range(NT):
            bk = c.banks[4 + j % 2]
            for dc in range(NDC):
                p.op("t", lambda t, d=dc, jj=j, bk_=bk: t.matmul(
                    bk_.ap[:, 0:128], xt_view(c, d, jj * 128, (jj + 1) * 128),
                    wpb2[:, d, :], start=(d == 0), stop=False),
                    wait=[bk.rd, wv_ld, c.xt_ev], tag=False)
            evm = p.op("t", lambda t, bk_=bk, c_=cc: t.matmul(
                bk_.ap[:, 0:128], c.ones[0:1, 0:128],
                c.browv[0:1, c_ * 128:(c_ + 1) * 128], start=False, stop=True),
                wait=[brow_ev], tag=True)
            bk.wr = evm
            vev = p.op("s", lambda s, jj=j, bk_=bk: s.copy(
                vch[:, jj, :], bk_.ap[:, 0:128]), wait=[bk.wr])
            bk.rd = vev
        wp_free = ("t", p.cnt["t"])
        v_done = vev
        for hd in range(2):
            h = cc * 2 + hd
            ek = expmap_batch(ktok, hd, timbfK, True, tk, "K")
            eka0 = p.op("g", lambda g: g.memset(Ka[64:66, :], -1.0), wait=[ek])
            kev = None
            for j in range(NT):
                sl = ktok[:, j, hd * 64:(hd + 1) * 64]
                et = p.op("t", lambda t, d=bank_bf(b6, 0)[0:64, :], sl_=sl:
                          t.transpose(d, sl_, c.identb), wait=[ek, b6.rd])
                b6.wr = et
                ec = p.op("s", lambda s, jj=j, d=bank_bf(b6, 0)[0:64, :]: s.copy(
                    Ka[0:64, jj * 128:(jj + 1) * 128], d), wait=[et])
                b6.rd = ec
                et2 = p.op("t", lambda t, jj=j, d=bank_bf(b6, 1)[0:1, :]: t.transpose(
                    d, timbfK[:, jj:jj + 1], c.identb), wait=[ek, b6.rd])
                b6.wr = et2
                ec2 = p.op("s", lambda s, jj=j, d=bank_bf(b6, 1)[0:1, :]: s.copy(
                    Ka[64:65, jj * 128:(jj + 1) * 128], d), wait=[et2, eka0])
                b6.rd = ec2
                kev = ec2
            ka_done = kev
            war_tim["K"] = ka_done
            eq = expmap_batch(qtok, hd, timbfQ, False, tq, "Q")
            for jg in range(4):
                qa_evs = []
                for i in range(4):
                    j = jg * 4 + i
                    sl = qtok[:, j, hd * 64:(hd + 1) * 64]
                    et = p.op("t", lambda t, d=bank_bf(b6, 0)[0:64, :], sl_=sl:
                              t.transpose(d, sl_, c.identb), wait=[eq, b6.rd])
                    b6.wr = et
                    ec = p.op("s", lambda s, i_=i, d=bank_bf(b6, 0)[0:64, :]: s.copy(
                        Qa[0:64, i_ * 128:(i_ + 1) * 128], d), wait=[et])
                    b6.rd = ec
                    et2 = p.op("t", lambda t, jj=j, d=bank_bf(b6, 1)[0:1, :]:
                               t.transpose(d, timbfQ[:, jj:jj + 1], c.identb),
                               wait=[eq, b6.rd])
                    b6.wr = et2
                    ec2 = p.op("s", lambda s, i_=i, d=bank_bf(b6, 1)[0:1, :]: s.copy(
                        Qa[64:65, i_ * 128:(i_ + 1) * 128], d), wait=[et2])
                    b6.rd = ec2
                    qa_evs.append(ec2)
                qa_done = qa_evs[-1]
                row_evs = []
                stat_free = None
                for i in range(4):
                    j = jg * 4 + i
                    off = 0 if i % 2 == 0 else 4
                    reg = c.pa if off == 0 else c.pb
                    wl2 = [qa_done, ka_done, stat_free] + \
                          [c.banks[off + q].rd for q in range(4)]
                    for kb in range(4):
                        evm = p.op("t", lambda t, kb_=kb, i_=i, r_=reg: t.matmul(
                            r_[:, kb_ * 512:(kb_ + 1) * 512],
                            Qa[0:65, i_ * 128:(i_ + 1) * 128],
                            Ka[0:65, kb_ * 512:(kb_ + 1) * 512],
                            start=True, stop=True), wait=wl2, tag=(kb == 3))
                    for kb in range(4):
                        c.banks[off + kb].wr = evm
                    e1 = p.op("v", lambda v, r_=reg: v.tensor_reduce(
                        mcol, r_[:, 0:2048], AX.X, ALU.min), wait=[evm])
                    e2 = p.op("v", lambda v: v.tensor_copy(mbf, mcol), wait=[e1])
                    e3 = p.op("s", lambda s: s.mul(biasc, mbf, 0.125), wait=[e2])
                    e4 = p.op("s", lambda s, r_=reg: s.activation(
                        escr, r_[:, 0:2048], AF.Exp, scale=-0.125, bias=biasc,
                        accum_out=rcol), wait=[e3])
                    for kb in range(4):
                        c.banks[off + kb].rd = e4
                    e5 = p.op("v", lambda v: v.reciprocal(rinv, rcol), wait=[e4])
                    p.op("v", lambda v, i_=i: v.tensor_copy(
                        rsl[:, i_:i_ + 1], rinv), wait=[e5])
                    e6 = p.op("v", lambda v, jj=j: v.tensor_copy(
                        pairb[:, 2:3], timbfQ[:, jj:jj + 1]), wait=[eq])
                    e7 = p.op("v", lambda v: v.tensor_copy(pairb[:, 3:4], mbf),
                              wait=[e2, e6])
                    et = p.op("t", lambda t, d=bank_bf(b6, 2)[0:2, :]: t.transpose(
                        d, pairb[:, 2:4], c.identb), wait=[e7, b6.rd])
                    b6.wr = et
                    erow = p.op("s", lambda s, i_=i, d=bank_bf(b6, 2)[0:2, :]: s.copy(
                        Qa[64:66, i_ * 128:(i_ + 1) * 128], d), wait=[et, e4])
                    b6.rd = erow
                    row_evs.append(erow)
                    stat_free = erow
                rows_done = row_evs[-1]
                war_tim["Q"] = rows_done
                rinv_done = ("v", p.cnt["v"])
                avb = [c.banks[q] for q in range(4)]
                for half in range(2):
                    kts = list(range(half * 8, half * 8 + 8))
                    pt_done = {}
                    for kt in kts:
                        b45 = c.banks[4 + kt % 2]
                        evm = p.op("t", lambda t, kt_=kt, bk=b45: t.matmul(
                            bk.ap, Ka[0:66, kt_ * 128:(kt_ + 1) * 128], Qa[0:66, :],
                            start=True, stop=True),
                            wait=[b45.rd, rows_done], tag=True)
                        b45.wr = evm
                        ee = p.op("s", lambda s, kt_=kt, bk=b45: s.activation(
                            ptb[:, kt_ % 8, :], bk.ap, AF.Exp, scale=-0.125),
                            wait=[evm, pt_done.get(kt % 2)])
                        b45.rd = ee
                        pt_done[kt % 2] = ee
                    pt_all = ("s", p.cnt["s"])
                    for i in range(4):
                        lo, hi = half * 8, half * 8 + 7
                        for kt in range(lo, hi + 1):
                            eav = p.op("t", lambda t, i_=i, kt_=kt, hd_=hd, lo_=lo,
                                       hi_=hi: t.matmul(
                                avb[i_].ap[:, 0:64],
                                ptb[:, kt_ % 8, i_ * 128:(i_ + 1) * 128],
                                vch[:, kt_, hd_ * 64:(hd_ + 1) * 64],
                                start=(kt_ == lo_), stop=(kt_ == hi_)),
                                wait=([avb[i].rd, pt_all, v_done] if kt == lo
                                      else []),
                                tag=(kt == hi))
                        avb[i].wr = eav
                        if half == 0:
                            evo = p.op("v", lambda v, i_=i: v.tensor_copy(
                                avacc[:, i_, :], avb[i_].ap[:, 0:64]),
                                wait=[avb[i].wr])
                        else:
                            e6 = p.op("v", lambda v, i_=i: v.tensor_tensor(
                                avacc[:, i_, :], avacc[:, i_, :],
                                avb[i_].ap[:, 0:64], ALU.add), wait=[avb[i].wr])
                            j = jg * 4 + i
                            evo = p.op("v", lambda v, i_=i, jj=j, h_=h:
                                       v.tensor_scalar_mul(
                                ao[:, jj, h_ * 64:(h_ + 1) * 64], avacc[:, i_, :],
                                rsl[:, i_:i_ + 1]), wait=[e6, rinv_done])
                        avb[i].rd = evo
    ao_done = ("v", p.cnt["v"])
    wo_ld = p.dma("y", lambda y: y.dma_start(
        wob.rearrange("p cc o -> p (cc o)"), c.s2l1o.rearrange("p a b -> p (a b)")),
        wait=[c.s2ev["l1o"], wp_free, ("t", p.cnt["t"])])
    b7 = c.banks[7]
    for j in range(NT):
        aoT_ev = None
        for halfc in range(2):
            for q in range(4):
                dc = halfc * 4 + q
                et = p.op("t", lambda t, d=bank_bf(b7, q),
                          s=ao[:, j, dc * 128:(dc + 1) * 128]:
                          t.transpose(d, s, c.identb),
                          wait=[b7.rd, ao_done], tag=(q == 3))
            b7.wr = et
            for q in range(4):
                dc = halfc * 4 + q
                ee = p.op("s", lambda s, d=aoT[:, dc, :], sp=bank_bf(b7, q):
                          s.copy(d, sp), wait=[b7.wr], tag=(q == 3))
            b7.rd = ee
        aoT_ev = ee
        for o in range(2):
            bk = c.banks[o]
            for dc in range(NDC):
                p.op("t", lambda t, d=dc, o_=o, bk_=bk: t.matmul(
                    bk_.ap, aoT[:, d, :], wob[:, d, o_ * 512:(o_ + 1) * 512],
                    start=(d == 0), stop=False),
                    wait=[bk.rd, wo_ld, aoT_ev], tag=False)
            evm = p.op("t", lambda t, o_=o, bk_=bk: t.matmul(
                bk_.ap, c.ones[0:1, 0:128], c.browo[0:1, o_ * 512:(o_ + 1) * 512],
                start=False, stop=True), wait=[brow_ev], tag=True)
            bk.wr = evm
            evx = p.op("v", lambda v, jj=j, o_=o, bk_=bk: v.tensor_tensor(
                c.x[:, jj, o_ * 512:(o_ + 1) * 512],
                c.x[:, jj, o_ * 512:(o_ + 1) * 512], bk_.ap, ALU.add),
                wait=[bk.wr, xev.get((j, o)), c.last_x_ev])
            bk.rd = evx
            xev[(j, o)] = evx
    c.last_x_ev = ("v", c.p.cnt["v"])


SUB_PREFIX = {"f0a": "L0_ffn_", "f0b": "L0_ffn_", "a0": "L0_attn_",
              "f1a": "L1_ffn_", "f1b": "L1_ffn_", "a1": "L1_attn_"}
SUB_WNAMES = {
    "f0a": ["router_weight", "gate_up_proj", "down_proj", "shared_gate_w",
            "shared_up_w", "shared_down_w", "shared_expert_gate_w"],
    "a0": ["W_q", "W_k", "W_v", "W_o"],
    "f1a": ["w_router", "w_up", "w_gate", "w_down", "ws_up", "ws_gate", "ws_down"],
    "a1": ["curv", "wq_w", "wq_b", "wk_w", "wk_b", "wv_w", "wv_b", "wo_w", "wo_b"],
}
SUB_WNAMES["f0b"] = SUB_WNAMES["f0a"]
SUB_WNAMES["f1b"] = SUB_WNAMES["f1a"]
EMIT = {"f0a": lambda c: emit_ffn0(c, "L0_ffn_"),
        "f0b": lambda c: emit_ffn0(c, "L0_ffn_"),
        "a0": lambda c: emit_attn0(c, "L0_attn_"),
        "f1a": lambda c: emit_ffn1(c, "L1_ffn_"),
        "f1b": lambda c: emit_ffn1(c, "L1_ffn_"),
        "a1": lambda c: emit_attn1(c, "L1_attn_")}

INPUT_SHAPES = {
    "L0_attn_W_q": [1024, 1024], "L0_attn_W_k": [256, 1024],
    "L0_attn_W_v": [256, 1024], "L0_attn_W_o": [1024, 1024],
    "L0_ffn_router_weight": [8, 1024], "L0_ffn_gate_up_proj": [8, 1024, 1024],
    "L0_ffn_down_proj": [8, 1024, 512], "L0_ffn_shared_gate_w": [2048, 1024],
    "L0_ffn_shared_up_w": [2048, 1024], "L0_ffn_shared_down_w": [1024, 2048],
    "L0_ffn_shared_expert_gate_w": [1, 1024],
    "L1_attn_curv": [], "L1_attn_wq_w": [1024, 1024], "L1_attn_wq_b": [1024],
    "L1_attn_wk_w": [1024, 1024], "L1_attn_wk_b": [1024],
    "L1_attn_wv_w": [1024, 1024], "L1_attn_wv_b": [1024],
    "L1_attn_wo_w": [1024, 1024], "L1_attn_wo_b": [1024],
    "L1_ffn_w_router": [1024, 8], "L1_ffn_w_up": [7, 1024, 4096],
    "L1_ffn_w_gate": [7, 1024, 4096], "L1_ffn_w_down": [7, 4096, 1024],
    "L1_ffn_ws_up": [1024, 4096], "L1_ffn_ws_gate": [1024, 4096],
    "L1_ffn_ws_down": [4096, 1024],
}


def build_group(subs, taps=()):
    nc = bass.Bass("TRN2", target_bir_lowering=False, debug=False)
    p = Prog(nc)
    c = alloc_ctx(nc, p)
    c._taps = list(taps)
    needed = []
    for s in subs:
        for w in SUB_WNAMES[s]:
            full = SUB_PREFIX[s] + w
            if full not in needed:
                needed.append(full)
    for full in needed:
        shp = INPUT_SHAPES[full]
        if full == "L1_attn_curv":
            c.dram[full] = nc.dram_tensor(full, [1], F32, kind="ExternalInput").ap()
        else:
            c.dram[full] = nc.dram_tensor(full, shp, F32, kind="ExternalInput").ap()
    c.dram["x_in"] = nc.dram_tensor("x_in", [T, D], F32, kind="ExternalInput").ap()
    c.dram["x_out"] = nc.dram_tensor("x_out", [T, D], F32, kind="ExternalOutput").ap()
    if any(s == "a0" for s in subs):
        c.dram["rope_cs"] = nc.dram_tensor("rope_cs", [T, 1024], F32,
                                           kind="ExternalInput").ap()
    emit_consts(c)
    emit_prepass_casts(c, subs)
    exl = p.dma("y", lambda y: y.dma_start(
        c.x, c.dram["x_in"].rearrange("(j p) d -> p j d", p=128)))
    c.last_x_ev = exl
    c.fence_ev = None
    pre_done = set()
    for s in subs:
        fence_w = [(k, p.cnt[k]) for k in ("t", "v", "s", "g")]
        c.fence_ev = p.op("s", lambda s: s.nop(), wait=fence_w)
        fam = SUB_PREFIX[s]
        if fam not in pre_done:
            pre_done.add(fam)
            emit_prepass_family(c, s)
        EMIT[s](c)
    sev = p.dma("y", lambda y: y.dma_start(
        c.dram["x_out"].rearrange("(j p) d -> p j d", p=128), c.x),
        wait=[c.last_x_ev])
    for name, getter in c._taps:
        ap = getter(c)
        shp = list(ap.shape)
        dt_ = ap.dtype
        od = nc.dram_tensor("tap_" + name, shp, dt_, kind="ExternalOutput").ap()
        allw = [(k, p.cnt[k]) for k in ("t", "v", "s", "g")]
        sev = p.dma("y", lambda y, o=od, a=ap: y.dma_start(o, a), wait=allw)
    p.op("y", lambda y: y.nop(), wait=[sev], tag=False)
    p.finalize()
    nc.finalize()
    return nc


GROUPS = [["f0a", "a0", "f0b", "f1a", "a1", "f1b"]]

_ROPE = None


def _rope_table():
    global _ROPE
    if _ROPE is None:
        theta = 1.0 / 10000.0 ** (np.arange(512, dtype=np.float64) / 512.0)
        ang = np.arange(T, dtype=np.float64)[:, None] * theta[None, :]
        _ROPE = np.ascontiguousarray(
            np.concatenate([np.cos(ang), np.sin(ang)], 1).astype(np.float32))
    return _ROPE


_CACHE = {}


def kernel(**inputs):
    X = np.ascontiguousarray(np.asarray(inputs["L0_attn_X"], np.float32))
    xs = [np.ascontiguousarray(X[i]) for i in range(NCORE)]
    for subs in GROUPS:
        key = tuple(subs)
        if key not in _CACHE:
            _CACHE[key] = build_group(subs)
        nc = _CACHE[key]
        needed = []
        for s in subs:
            for w in SUB_WNAMES[s]:
                full = SUB_PREFIX[s] + w
                if full not in needed:
                    needed.append(full)
        base = {}
        for full in needed:
            a = np.ascontiguousarray(np.asarray(inputs[full], np.float32))
            if full == "L1_attn_curv":
                a = a.reshape(1)
            base[full] = a
        if any(s == "a0" for s in subs):
            base["rope_cs"] = _rope_table()
        in_maps = [dict(base, x_in=xs[i]) for i in range(NCORE)]
        res = run_bass_kernel_spmd(nc, in_maps, core_ids=list(range(NCORE)))
        xs = [np.asarray(res.results[i]["x_out"], np.float32) for i in range(NCORE)]
    return np.stack(xs, 0)
